# version 08
# baseline (speedup 1.0000x reference)
import contextlib
import numpy as np
import ml_dtypes
import concourse.bass as bass
import concourse.mybir as mybir
from concourse.bass_core_rust import IndirectOffsetOnAxis
from concourse.bass_utils import run_bass_kernel_spmd

F32 = mybir.dt.float32
BF16 = mybir.dt.bfloat16
I32 = mybir.dt.int32
AF = mybir.ActivationFunctionType
ALU = mybir.AluOpType
AX = mybir.AxisListType
BF = ml_dtypes.bfloat16

B, T, D = 4, 4096, 1024
NBLK, BLK = 8, 512
OWN = {0: [0, 2, 5, 7], 1: [1, 3, 4, 6]}
NT = 2048
NTT = NT // 512
DC = D // 128

LN_EPS = 1e-5
RMS_EPS = 1e-6


def own_rows(r):
    idx = []
    for j in OWN[r]:
        idx.extend(range(j * BLK, (j + 1) * BLK))
    return np.array(idx)



ENGINES = ("tensor", "vector", "scalar", "gpsimd", "sync")


class SemL:
    def __init__(self, h):
        self.h = h
        self.n = 0
        self.base = 0


class PhaseSem:

    def __init__(self, pool):
        self._p = pool
        self.base = pool.n

    @property
    def h(self):
        return self._p.h

    @property
    def n(self):
        return self._p.n - self.base

    @n.setter
    def n(self, v):
        self._p.n = self.base + v


class Ctx:
    def __init__(self, nc, stack):
        self.nc = nc
        self.stack = stack
        self.rec = {e: [] for e in ENGINES}
        self.sems = {}
        self.pslots = []
        self._phase = {}
        self.prefix = ""
        self._barn = 0

    def new_phase(self, prefix):
        self.prefix = prefix
        self._phase = {}

    def sem(self, name):
        key = self.prefix + name
        if key not in self._phase:
            slot = len(self._phase)
            if slot >= len(self.pslots):
                h = self.stack.enter_context(
                    self.nc.semaphore(f"pool{slot}"))
                self.pslots.append(SemL(h))
            self._phase[key] = PhaseSem(self.pslots[slot])
        return self._phase[key]

    def gsem(self, name):
        if name not in self.sems:
            h = self.stack.enter_context(self.nc.semaphore(name))
            self.sems[name] = SemL(h)
        return self.sems[name]

    def barrier(self):
        self._barn += 1
        bars = {e: self.gsem("bar_" + e) for e in ENGINES}
        for e in ENGINES:
            sem = bars[e]
            sem.n += 1
            self.rec[e].append(lambda eng, s=sem.h: eng.nop().then_inc(s, 1))
        for e in ENGINES:
            for e2 in ENGINES:
                if e2 == e:
                    continue
                self.wait(e, bars[e2])

    def op(self, eng, fn, inc=None):
        if inc is None:
            self.rec[eng].append(fn)
        else:
            sem, k = inc
            sem.n += k
            self.rec[eng].append(
                lambda e, fn=fn, s=sem.h, k=k: fn(e).then_inc(s, k))

    def wait(self, eng, sem, val=None):
        v = sem.n if val is None else val
        if v > 0:
            va = v + sem.base
            self.rec[eng].append(lambda e, s=sem.h, v=va: e.wait_ge(s, v))

    def finalize(self):
        with self.nc.Block() as block:
            for eng in ENGINES:
                ops = self.rec[eng]
                if not ops:
                    continue

                def runner(e, ops=ops):
                    for f in ops:
                        f(e)
                getattr(block, eng)(runner)


class Stage:

    def __init__(self, ctx, buf, nslots, width, name="stg"):
        self.ctx = ctx
        self.buf = buf
        self.nslots = nslots
        self.width = width
        self.sems = [ctx.sem(f"{name}{i}") for i in range(nslots)]
        self.idx = 0

    def next(self, writer_eng):
        s = self.idx % self.nslots
        self.idx += 1
        if self.sems[s].n > 0:
            self.ctx.wait(writer_eng, self.sems[s])
        return self.buf[:, s * self.width:(s + 1) * self.width], s

    def store(self, queue_eng, fn, s):
        self.ctx.op(queue_eng, fn, inc=(self.sems[s], 16))

    def final_waits(self, eng):
        for sem in self.sems:
            self.ctx.wait(eng, sem)


class DBuf:

    def __init__(self, ctx, nslots, name):
        self.ctx = ctx
        self.sems = [ctx.sem(f"{name}{i}") for i in range(nslots)]
        self.nslots = nslots
        self.idx = 0

    def load(self, queue_eng, fn):
        s = self.idx % self.nslots
        self.idx += 1
        self.ctx.op(queue_eng, fn, inc=(self.sems[s], 16))
        return s

    def wait(self, eng, s=None):
        if s is None:
            s = (self.idx - 1) % self.nslots
        self.ctx.wait(eng, self.sems[s])


class NCProxy:

    def __init__(self, nc, prefix):
        self._nc = nc
        self._prefix = prefix

    def sbuf_tensor(self, name, *a, **kw):
        return self._nc.sbuf_tensor(self._prefix + name, *a, **kw)

    def psum_tensor(self, name, *a, **kw):
        return self._nc.psum_tensor(self._prefix + name, *a, **kw)

    def __getattr__(self, k):
        return getattr(self._nc, k)


def _wrap(decl, emit):
    nc = bass.Bass("TRN2", target_bir_lowering=False)
    io = {}

    def din(name, shape, dtype=F32):
        io[name] = nc.dram_tensor(name, shape, dtype,
                                  kind="ExternalInput").ap()

    def dout(name, shape, dtype=F32):
        io[name] = nc.dram_tensor(name, shape, dtype,
                                  kind="ExternalOutput").ap()

    decl(din, din, dout)
    with contextlib.ExitStack() as stack:
        ctx = Ctx(nc, stack)
        emit(nc, ctx, io, stack.enter_context)
        ctx.finalize()
    return nc



def fmT(a):
    return np.ascontiguousarray(a.T)


def stage_bias(b):
    k = b.shape[0] // 128
    return np.ascontiguousarray(b.reshape(k, 128).T)


def fff_host_stage(p, node_w, node_b, w1, b1, w2, b2):
    return {
        p + "_nw": fmT(node_w),
        p + "_nb": np.ascontiguousarray(node_b.reshape(15, 1)),
        p + "_w1p": np.ascontiguousarray(np.stack(
            [np.concatenate([w1[2 * i], w1[2 * i + 1]], axis=1)
             for i in range(8)])),
        p + "_b1p": np.ascontiguousarray(np.stack(
            [np.concatenate([b1[2 * i], b1[2 * i + 1]]) for i in range(8)],
            axis=1)),
        p + "_w2p": np.ascontiguousarray(np.stack(
            [np.concatenate([w2[2 * i], w2[2 * i + 1]], axis=0)
             for i in range(8)])),
        p + "_b2": np.ascontiguousarray(b2),
    }


def fff_amat():
    A = np.zeros((15, 16), np.float32)
    thr = np.zeros((16, 1), np.float32)
    for leaf in range(16):
        d0, d1 = (leaf >> 3) & 1, (leaf >> 2) & 1
        d2, d3 = (leaf >> 1) & 1, leaf & 1
        nodes = [0, 1 + d0, 3 + 2 * d0 + d1, 7 + 4 * d0 + 2 * d1 + d2]
        nleft = 0
        for n, d in zip(nodes, (d0, d1, d2, d3)):
            A[n, leaf] = 1.0 if d else -1.0
            nleft += (0 if d else 1)
        thr[leaf, 0] = 3.5 - nleft
    return A, thr


def e16_mat():
    E = np.zeros((16, 8 * 128), np.float32)
    for lp in range(8):
        E[2 * lp, lp * 128: lp * 128 + 64] = 1.0
        E[2 * lp + 1, lp * 128 + 64: lp * 128 + 128] = 1.0
    return E


def const_arrays():
    A, thr = fff_amat()
    return {
        "c_ones_bf": np.ones((128, 1), np.float32),
        "c_ones_row": np.ones((1, 128), np.float32),
        "c_amat": A,
        "c_thr": thr,
        "c_e16": e16_mat(),
    }



def emit_ln_fm(ctx, consts, xbuf, scr, rows, pb, nt, eps,
               pre=(), tag="", rms_only=False):
    sS = ctx.sem("lnS" + tag)
    sT = ctx.sem("lnT" + tag)
    sV = ctx.sem("lnV" + tag)
    done = ctx.sem("lnD" + tag)
    ntt = nt // 512
    ones_bf, ones_row = consts["ones_bf"], consts["ones_row"]

    for eng, sem, val in pre:
        ctx.wait(eng, sem, val)
    ctx.op("scalar", lambda e: e.activation(
        out=scr[:, 0:DC * nt], in_=xbuf[:, 0:DC * nt],
        func=AF.Identity), inc=(sS, 1))
    if not rms_only:
        ctx.wait("tensor", sS)
        for tt in range(ntt):
            o = tt * 512
            for c in range(DC):
                ctx.op("tensor", lambda e, c=c, o=o, tt=tt: e.matmul(
                    pb[tt][0:1, 0:512], ones_bf[:, 0:1],
                    scr[:, c * nt + o: c * nt + o + 512],
                    start=(c == 0), stop=(c == DC - 1)),
                    inc=(sT, 1) if c == DC - 1 else None)
        ctx.wait("vector", sT)
        for tt in range(ntt):
            o = tt * 512
            ctx.op("vector", lambda e, o=o, tt=tt: e.tensor_scalar_mul(
                out=rows[0:1, o:o + 512], in0=pb[tt][0:1, 0:512],
                scalar1=1.0 / D), inc=(sV, 1))
    ctx.wait("scalar", sT)
    ctx.op("scalar", lambda e: e.activation(
        out=scr[:, 0:DC * nt], in_=scr[:, 0:DC * nt], func=AF.Square),
        inc=(sS, 1))
    ctx.wait("tensor", sS)
    if not rms_only:
        ctx.wait("tensor", sV)
    for tt in range(ntt):
        o = tt * 512
        for c in range(DC):
            ctx.op("tensor", lambda e, c=c, o=o, tt=tt: e.matmul(
                pb[tt][0:1, 0:512], ones_bf[:, 0:1],
                scr[:, c * nt + o: c * nt + o + 512],
                start=(c == 0), stop=(c == DC - 1)),
                inc=(sT, 1) if c == DC - 1 else None)
    ctx.wait("vector", sT)
    for tt in range(ntt):
        o = tt * 512
        ctx.op("vector", lambda e, o=o, tt=tt: e.tensor_scalar_mul(
            out=rows[0:1, nt + o:nt + o + 512], in0=pb[tt][0:1, 0:512],
            scalar1=1.0 / D))
        if not rms_only:
            ctx.op("vector", lambda e, o=o: e.tensor_mul(
                out=rows[0:1, 2 * nt:2 * nt + 512],
                in0=rows[0:1, o:o + 512], in1=rows[0:1, o:o + 512]))
            ctx.op("vector", lambda e, o=o: e.tensor_sub(
                out=rows[0:1, nt + o:nt + o + 512],
                in0=rows[0:1, nt + o:nt + o + 512],
                in1=rows[0:1, 2 * nt:2 * nt + 512]))
        ctx.op("vector", lambda e, o=o: e.tensor_scalar_add(
            out=rows[0:1, nt + o:nt + o + 512],
            in0=rows[0:1, nt + o:nt + o + 512], scalar1=float(eps)))
        ctx.op("vector", lambda e, o=o: e.reciprocal(
            out=rows[0:1, nt + o:nt + o + 512],
            in_=rows[0:1, nt + o:nt + o + 512]), inc=(sV, 1))
    ctx.wait("scalar", sV)
    ctx.op("scalar", lambda e: e.activation(
        out=rows[0:1, nt:nt + nt], in_=rows[0:1, nt:nt + nt],
        func=AF.Sqrt), inc=(sS, 1))
    ctx.wait("tensor", sS)
    for tt in range(ntt):
        o = tt * 512
        if tt >= 1:
            ctx.wait("tensor", done, tt)
        if not rms_only:
            ctx.op("tensor", lambda e, o=o: e.matmul(
                pb[0][:, 0:512], ones_row[0:1, :], rows[0:1, o:o + 512],
                start=True, stop=True))
        ctx.op("tensor", lambda e, o=o: e.matmul(
            pb[1][:, 0:512], ones_row[0:1, :],
            rows[0:1, nt + o:nt + o + 512], start=True, stop=True),
            inc=(sT, 1))
        ctx.wait("vector", sT)
        xs = xbuf[:, 0:DC * nt].rearrange("p (c t) -> p c t",
                                          c=DC)[:, :, o:o + 512]
        mub = pb[0][:, 0:512].rearrange("p (a t) -> p a t",
                                        a=1).broadcast_to([128, DC, 512])
        rsb = pb[1][:, 0:512].rearrange("p (a t) -> p a t",
                                        a=1).broadcast_to([128, DC, 512])
        if not rms_only:
            ctx.op("vector", lambda e, xs=xs, mub=mub: e.tensor_tensor(
                out=xs, in0=xs, in1=mub, op=ALU.subtract))
        ctx.op("vector", lambda e, xs=xs, rsb=rsb: e.tensor_tensor(
            out=xs, in0=xs, in1=rsb, op=ALU.mult), inc=(done, 1))
    return done



def decl_A(din, dmid, dout):
    din("x_fm", [D, NT])
    din("w_in", [D, 2 * D])
    din("b_in", [128, 16])
    for f in ("q", "k", "v"):
        din(f + "_nw", [D, 15])
        din(f + "_nb", [15, 1])
        din(f + "_w1p", [8, D, 128])
        din(f + "_b1p", [128, 8])
        din(f + "_w2p", [8, 128, D])
        din(f + "_b2", [16, D])
    din("c_ones_bf", [128, 1])
    din("c_ones_row", [1, 128])
    din("c_amat", [15, 16])
    din("c_thr", [16, 1])
    din("c_e16", [16, 8 * 128])
    dout("xn0", [D, NT], F32)
    dout("u", [D, NT], BF16)
    dout("v", [D, NT], BF16)
    dout("fq", [D, NT], BF16)
    dout("fk", [D, NT], BF16)
    dout("fv", [NT, D], BF16)


def emit_A(nc, ctx, io, en):
    xbuf = en(nc.sbuf_tensor("xbuf", [128, DC * NT], F32))
    bufa = en(nc.sbuf_tensor("bufa", [128, DC * NT], BF16))
    bufb = en(nc.sbuf_tensor("bufb", [128, DC * NT], BF16))
    wsb = en(nc.sbuf_tensor("wsb", [128, DC * NT], BF16))
    stg = en(nc.sbuf_tensor("stg", [128, 8 * 512], BF16))
    relu_t = en(nc.sbuf_tensor("relu_t", [128, 4 * 512], BF16))
    rows = en(nc.sbuf_tensor("rows", [1, 2 * NT + 512], F32))
    ones_bf = en(nc.sbuf_tensor("ones_bf", [128, 1], BF16))
    ones_row = en(nc.sbuf_tensor("ones_row", [1, 128], F32))
    b_in = en(nc.sbuf_tensor("b_in_sb", [128, 16], F32))
    nw = en(nc.sbuf_tensor("nw", [128, DC * 15], F32))
    nbneg = en(nc.sbuf_tensor("nbneg", [15, 1], F32))
    b1p = en(nc.sbuf_tensor("b1p", [128, 8], F32))
    b2sb = en(nc.sbuf_tensor("b2sb", [16, D], BF16))
    amat = en(nc.sbuf_tensor("amat", [15, 16], BF16))
    thr = en(nc.sbuf_tensor("thr", [16, 1], F32))
    e16 = en(nc.sbuf_tensor("e16", [16, 8 * 128], BF16))
    sgn = en(nc.sbuf_tensor("sgn", [15, NT], BF16))
    sel = en(nc.sbuf_tensor("sel", [16, NT], BF16))
    pb = [en(nc.psum_tensor(f"pb{i}", [128, 512], F32)) for i in range(8)]

    sLdX = ctx.sem("ldx")
    sLdC = ctx.sem("ldc")
    sLgC = ctx.sem("lgc")
    sLgW = ctx.sem("lgw")
    sStX = ctx.sem("stxn")
    consts = dict(ones_bf=ones_bf, ones_row=ones_row)
    stage = Stage(ctx, stg, 8, 512, "sg")

    ctx.op("sync", lambda e: e.dma_start(
        out=xbuf[:, 0:DC * NT].rearrange("p (c t) -> p c t", c=DC),
        in_=io["x_fm"].rearrange("(c p) t -> p c t", p=128)),
        inc=(sLdX, 16))
    ctx.op("sync", lambda e: e.dma_start(out=ones_row[:],
                                         in_=io["c_ones_row"]),
           inc=(sLdC, 16))
    ctx.op("sync", lambda e: e.dma_start(out=thr[:], in_=io["c_thr"]),
           inc=(sLdC, 16))
    ctx.op("sync", lambda e: e.dma_start(out=b_in[:], in_=io["b_in"]),
           inc=(sLdC, 16))
    for nm, dst in (("c_ones_bf", ones_bf), ("c_amat", amat),
                    ("c_e16", e16)):
        ctx.op("gpsimd", lambda e, nm=nm, dst=dst: e.dma_start(
            out=dst[:], in_=io[nm]), inc=(sLgC, 16))
    ctx.op("gpsimd", lambda e: e.dma_start(
        out=wsb[:, 0:DC * NT].rearrange("p (c m) -> p c m", c=DC),
        in_=io["w_in"].rearrange("(c p) m -> p c m", p=128)),
        inc=(sLgW, 16))

    ln_done = emit_ln_fm(
        ctx, consts, xbuf, bufa, rows, pb, NT, LN_EPS,
        pre=[("scalar", sLdX, 16), ("tensor", sLdC, 48),
             ("tensor", sLgC, 48)], tag="0")
    sXnb = ctx.sem("xnb")
    ctx.wait("scalar", ln_done)
    ctx.op("scalar", lambda e: e.activation(
        out=bufa[:, 0:DC * NT], in_=xbuf[:, 0:DC * NT],
        func=AF.Identity), inc=(sXnb, 1))
    ctx.wait("sync", ln_done)
    ctx.op("sync", lambda e: e.dma_start(
        out=io["xn0"].rearrange("(c p) t -> p c t", p=128),
        in_=xbuf[:, 0:DC * NT].rearrange("p (c t) -> p c t", c=DC)),
        inc=(sStX, 16))

    sTi = ctx.sem("Tinp")
    sEi = ctx.sem("Einp")
    ctx.wait("tensor", sXnb)
    ctx.wait("tensor", sLgW)
    ctx.wait("scalar", sLdC)
    for mc in range(16):
        base = (mc % 2) * NTT
        if mc >= 2:
            ctx.wait("tensor", sEi, (mc - 1) * NTT)
        for dc in range(DC):
            for tt in range(NTT):
                ctx.op("tensor", lambda e, mc=mc, dc=dc, tt=tt, base=base:
                       e.matmul(
                           pb[base + tt][:, 0:512],
                           wsb[:, (dc * 16 + mc) * 128:
                               (dc * 16 + mc) * 128 + 128],
                           bufa[:, dc * NT + tt * 512:
                                dc * NT + tt * 512 + 512],
                           start=(dc == 0), stop=(dc == DC - 1)),
                       inc=(sTi, 1) if (dc == DC - 1 and tt == NTT - 1)
                       else None)
        ctx.wait("scalar", sTi, mc + 1)
        hbm = io["u"] if mc < 8 else io["v"]
        mcl = mc % 8
        func = AF.Identity if mc < 8 else AF.Silu
        for tt in range(NTT):
            tile, slot = stage.next("scalar")
            ctx.op("scalar", lambda e, tile=tile, mc=mc, tt=tt,
                   base=base, func=func: e.activation(
                       out=tile, in_=pb[base + tt][:, 0:512], func=func,
                       bias=b_in[:, mc:mc + 1], scale=1.0),
                   inc=(sEi, 1))
            ctx.wait("sync", sEi)
            stage.store("sync", lambda e, tile=tile, hbm=hbm, mcl=mcl,
                        tt=tt: e.dma_start(
                            out=hbm.rearrange("(c p) t -> p c t", p=128)
                            [:, mcl:mcl + 1, tt * 512:tt * 512 + 512],
                            in_=tile), slot)

    prev = dict(T=sTi, V=None, S=None, E=sEi)
    out_map = {"q": ("fq", "fm"), "k": ("fk", "fm"), "v": ("fv", "tm")}
    for fi, f in enumerate(("q", "k", "v")):
        tg = "f" + f
        sG2 = ctx.sem("ldg" + tg)
        sT = ctx.sem("T" + tg)
        sE = ctx.sem("E" + tg)
        sVf = ctx.sem("V" + tg)
        sSf = ctx.sem("S" + tg)
        sMk = ctx.sem("M" + tg)
        ctx.wait("gpsimd", prev["T"])
        if prev["V"] is not None:
            ctx.wait("gpsimd", prev["V"])
        if prev["S"] is not None:
            ctx.wait("gpsimd", prev["S"])
        ctx.op("gpsimd", lambda e, f=f: e.dma_start(
            out=nw[:, 0:DC * 15].rearrange("p (c n) -> p c n", c=DC),
            in_=io[f + "_nw"].rearrange("(c p) n -> p c n", p=128)),
            inc=(sG2, 16))
        ctx.op("gpsimd", lambda e, f=f: e.dma_start(
            out=wsb[:, 0:8 * 1024].rearrange("p (l c m) -> p l c m",
                                             l=8, c=DC),
            in_=io[f + "_w1p"].rearrange("l (c p) m -> p l c m",
                                         p=128)), inc=(sG2, 16))
        ctx.op("gpsimd", lambda e, f=f: e.dma_start(
            out=wsb[:, 8 * 1024:16 * 1024].rearrange(
                "p (l m) -> p l m", l=8),
            in_=io[f + "_w2p"].rearrange("l p m -> p l m")),
            inc=(sG2, 16))
        ctx.op("gpsimd", lambda e, f=f: e.dma_start(
            out=b1p[:], in_=io[f + "_b1p"]), inc=(sG2, 16))
        ctx.op("gpsimd", lambda e, f=f: e.dma_start(
            out=b2sb[:], in_=io[f + "_b2"]), inc=(sG2, 16))
        ctx.op("gpsimd", lambda e, f=f: e.dma_start(
            out=nbneg[:], in_=io[f + "_nb"]), inc=(sG2, 16))
        ctx.wait("vector", sG2)
        ctx.wait("vector", prev["T"])
        ctx.op("vector", lambda e: e.tensor_scalar_mul(
            out=nbneg[:], in0=nbneg[:], scalar1=-1.0), inc=(sVf, 1))
        ctx.wait("vector", sVf)

        ctx.wait("tensor", sG2)
        ctx.wait("tensor", prev["E"])
        if fi == 0:
            ctx.wait("vector", sLdC)
        for tt in range(NTT):
            o = tt * 512
            if tt >= 1:
                ctx.wait("tensor", sVf, 1 + 2 * tt)
            for c in range(DC):
                ctx.op("tensor", lambda e, c=c, o=o: e.matmul(
                    pb[0][0:15, 0:512], nw[:, c * 15:c * 15 + 15],
                    xbuf[:, c * NT + o:c * NT + o + 512],
                    start=(c == 0), stop=(c == DC - 1)),
                    inc=(sT, 1) if c == DC - 1 else None)
            ctx.wait("vector", sT)
            ctx.op("vector", lambda e, o=o: e.tensor_single_scalar(
                out=sgn[:, o:o + 512], in_=pb[0][0:15, 0:512],
                scalar=nbneg[:], op=ALU.is_ge), inc=(sVf, 1))
            ctx.wait("tensor", sVf)
            ctx.op("tensor", lambda e, o=o: e.matmul(
                pb[1][0:16, 0:512], amat[:], sgn[:, o:o + 512],
                start=True, stop=True), inc=(sT, 1))
            ctx.wait("vector", sT)
            ctx.op("vector", lambda e, o=o: e.tensor_single_scalar(
                out=sel[:, o:o + 512], in_=pb[1][0:16, 0:512],
                scalar=thr[:], op=ALU.is_ge), inc=(sVf, 1))

        ctx.wait("tensor", sVf)
        for lp in range(8):
            if lp >= 1:
                ctx.wait("tensor", sSf, 4 * lp)
            for dc in range(DC):
                for tt in range(NTT):
                    ctx.op("tensor",
                           lambda e, lp=lp, dc=dc, tt=tt: e.matmul(
                               pb[2 + tt][:, 0:512],
                               wsb[:, lp * 1024 + dc * 128:
                                   lp * 1024 + dc * 128 + 128],
                               bufa[:, dc * NT + tt * 512:
                                    dc * NT + tt * 512 + 512],
                               start=(dc == 0), stop=(dc == DC - 1)),
                           inc=(sT, 1)
                           if (dc == DC - 1 and tt == NTT - 1) else None)
            for tt in range(NTT):
                idx = lp * NTT + tt
                o = tt * 512
                ctx.wait("scalar", sT)
                if idx >= 4:
                    ctx.wait("scalar", sMk, idx - 4 + 1)
                ctx.op("scalar", lambda e, lp=lp, tt=tt, idx=idx:
                       e.activation(
                           out=relu_t[:, (idx % 4) * 512:
                                      (idx % 4) * 512 + 512],
                           in_=pb[2 + tt][:, 0:512], func=AF.Relu,
                           bias=b1p[:, lp:lp + 1], scale=1.0),
                       inc=(sSf, 1))
                if idx >= 2:
                    ctx.wait("tensor", sMk, idx - 2 + 1)
                ctx.op("tensor", lambda e, lp=lp, o=o, idx=idx: e.matmul(
                    pb[6 + idx % 2][:, 0:512],
                    e16[:, lp * 128:lp * 128 + 128],
                    sel[:, o:o + 512],
                    start=True, stop=True), inc=(sT, 1))
                ctx.wait("vector", sT)
                ctx.wait("vector", sSf)
                ctx.op("vector", lambda e, lp=lp, tt=tt, idx=idx:
                       e.tensor_mul(
                           out=bufb[:, idx * 512:idx * 512 + 512],
                           in0=relu_t[:, (idx % 4) * 512:
                                      (idx % 4) * 512 + 512],
                           in1=pb[6 + idx % 2][:, 0:512]),
                       inc=(sMk, 1))

        hbm_name, kind = out_map[f]
        hbm = io[hbm_name]
        ctx.wait("tensor", sMk)
        if kind == "fm":
            for dc in range(DC):
                base = (dc % 2) * NTT
                if dc >= 2:
                    ctx.wait("tensor", sE, (dc - 1) * NTT)
                for lp in range(8):
                    for tt in range(NTT):
                        ctx.op("tensor",
                               lambda e, dc=dc, lp=lp, tt=tt, base=base:
                               e.matmul(
                                   pb[base + tt][:, 0:512],
                                   wsb[:, 8192 + lp * 1024 + dc * 128:
                                       8192 + lp * 1024 + dc * 128 + 128],
                                   bufb[:, (lp * NTT + tt) * 512:
                                        (lp * NTT + tt) * 512 + 512],
                                   start=(lp == 0), stop=False))
                for tt in range(NTT):
                    ctx.op("tensor", lambda e, dc=dc, tt=tt, base=base:
                           e.matmul(
                               pb[base + tt][:, 0:512],
                               b2sb[:, dc * 128:dc * 128 + 128],
                               sel[:, tt * 512:tt * 512 + 512],
                               start=False, stop=True),
                           inc=(sT, 1) if tt == NTT - 1 else None)
                ctx.wait("scalar", sT)
                for tt in range(NTT):
                    tile, slot = stage.next("scalar")
                    ctx.op("scalar", lambda e, tile=tile, tt=tt,
                           base=base: e.activation(
                               out=tile, in_=pb[base + tt][:, 0:512],
                               func=AF.Identity), inc=(sE, 1))
                    ctx.wait("sync", sE)
                    stage.store("sync", lambda e, tile=tile, hbm=hbm,
                                dc=dc, tt=tt: e.dma_start(
                                    out=hbm.rearrange(
                                        "(c p) t -> p c t", p=128)
                                    [:, dc:dc + 1,
                                     tt * 512:tt * 512 + 512],
                                    in_=tile), slot)
        else:
            for ts in range(16):
                base = (ts % 2) * 2
                if ts >= 2:
                    ctx.wait("tensor", sE, (ts - 1) * 2)
                for dh in range(2):
                    for lp in range(8):
                        ctx.op("tensor",
                               lambda e, ts=ts, dh=dh, lp=lp, base=base:
                               e.matmul(
                                   pb[base + dh][:, 0:512],
                                   bufb[:, lp * NT + ts * 128:
                                        lp * NT + ts * 128 + 128],
                                   wsb[:, 8192 + lp * 1024 + dh * 512:
                                       8192 + lp * 1024 + dh * 512
                                       + 512],
                                   start=(lp == 0), stop=False))
                    ctx.op("tensor", lambda e, ts=ts, dh=dh, base=base:
                           e.matmul(
                               pb[base + dh][:, 0:512],
                               sel[:, ts * 128:ts * 128 + 128],
                               b2sb[:, dh * 512:dh * 512 + 512],
                               start=False, stop=True),
                           inc=(sT, 1) if dh == 1 else None)
                ctx.wait("scalar", sT)
                for dh in range(2):
                    tile, slot = stage.next("scalar")
                    ctx.op("scalar", lambda e, tile=tile, dh=dh,
                           base=base: e.activation(
                               out=tile, in_=pb[base + dh][:, 0:512],
                               func=AF.Identity), inc=(sE, 1))
                    ctx.wait("sync", sE)
                    stage.store("sync", lambda e, tile=tile, ts=ts,
                                dh=dh: e.dma_start(
                                    out=io["fv"][ts * 128:ts * 128
                                                 + 128,
                                                 dh * 512:dh * 512
                                                 + 512],
                                    in_=tile), slot)
        prev = dict(T=sT, V=sMk, S=sSf, E=sE)

    ctx.wait("sync", sStX)
    stage.final_waits("sync")


def build_launch_A():
    return _wrap(decl_A, emit_A)


def stage_inputs_A(core, W):
    b, r = core // 2, core % 2
    rows = own_rows(r)
    d = {"x_fm": fmT(W["L0_attn_x"][b][rows]),
         "w_in": np.ascontiguousarray(W["L0_attn_in_proj_w"]),
         "b_in": stage_bias(W["L0_attn_in_proj_b"])}
    for f in ("q", "k", "v"):
        d.update(fff_host_stage(
            f,
            W[f"L0_ffn_{f}_node_w"], W[f"L0_ffn_{f}_node_b"],
            W[f"L0_ffn_{f}_w1"], W[f"L0_ffn_{f}_b1"],
            W[f"L0_ffn_{f}_w2"], W[f"L0_ffn_{f}_b2"]))
    d.update(const_arrays())
    return d





def decl_B(din, dmid, dout):
    dmid("v_ch", [512, T], BF16)
    for nm in ("cw0", "cw1", "cw2", "cb", "la", "sb_", "sc_", "sd_"):
        din(nm, [128, 4])
    dout("sy", [512, T], BF16)


def emit_B(nc, ctx, io, en):
    vbuf = en(nc.sbuf_tensor("vbuf", [128, 2 * T], BF16))
    vloc = en(nc.sbuf_tensor("vloc", [128, T], F32))
    hbuf = en(nc.sbuf_tensor("hbuf", [128, T], F32))
    sybuf = en(nc.sbuf_tensor("sybuf", [128, 2 * T], BF16))
    par = {}
    for nm in ("cw0", "cw1", "cw2", "cb", "la", "sb_", "sc_", "sd_"):
        par[nm] = en(nc.sbuf_tensor(nm + "_sb", [128, 4], F32))
    sLdP = ctx.sem("ldp")
    sV = ctx.sem("v")
    sS = ctx.sem("s")
    vload = DBuf(ctx, 2, "vl")
    sStS = [ctx.sem("sts0"), ctx.sem("sts1")]
    for nm in ("cw0", "cw1", "cw2", "cb", "la", "sb_", "sc_", "sd_"):
        ctx.op("sync", lambda e, nm=nm: e.dma_start(
            out=par[nm][:], in_=io[nm]), inc=(sLdP, 16))
    ctx.wait("scalar", sLdP)
    ctx.op("scalar", lambda e: e.activation(
        out=par["la"][:], in_=par["la"][:], func=AF.Sigmoid),
        inc=(sS, 1))
    for cc in range(4):
        buf = vbuf[:, (cc % 2) * T:(cc % 2) * T + T]
        syb = sybuf[:, (cc % 2) * T:(cc % 2) * T + T]
        if cc >= 2:
            ctx.wait("sync", sV, 7 * cc - 7)
        vload.load("sync", lambda e, cc=cc, buf=buf: e.dma_start(
            out=buf, in_=io["v_ch"][cc * 128:cc * 128 + 128, :]))
        vload.wait("vector")
        ctx.wait("vector", sS, 1)
        ctx.op("vector", lambda e, cc=cc, buf=buf: e.tensor_scalar_mul(
            out=vloc[:], in0=buf, scalar1=par["cw1"][:, cc:cc + 1]))
        ctx.op("vector", lambda e, cc=cc, buf=buf:
               e.scalar_tensor_tensor(
                   out=vloc[:, 1:T], in0=buf[:, 0:T - 1],
                   scalar=par["cw0"][:, cc:cc + 1], in1=vloc[:, 1:T],
                   op0=ALU.mult, op1=ALU.add))
        ctx.op("vector", lambda e, cc=cc, buf=buf:
               e.scalar_tensor_tensor(
                   out=vloc[:, 0:T - 1], in0=buf[:, 1:T],
                   scalar=par["cw2"][:, cc:cc + 1], in1=vloc[:, 0:T - 1],
                   op0=ALU.mult, op1=ALU.add))
        ctx.op("vector", lambda e, cc=cc: e.tensor_scalar_add(
            out=vloc[:], in0=vloc[:], scalar1=par["cb"][:, cc:cc + 1]))
        ctx.op("vector", lambda e, cc=cc: e.tensor_scalar_mul(
            out=hbuf[:], in0=vloc[:], scalar1=par["sb_"][:, cc:cc + 1]))
        a_b = par["la"][:, cc:cc + 1].broadcast_to([128, T])
        ctx.op("vector", lambda e, cc=cc, a_b=a_b: e.tensor_tensor_scan(
            out=hbuf[:], data0=a_b, data1=hbuf[:],
            initial=0.0, op0=ALU.mult, op1=ALU.add))
        ctx.op("vector", lambda e, cc=cc: e.tensor_scalar_mul(
            out=vloc[:], in0=vloc[:], scalar1=par["sd_"][:, cc:cc + 1]))
        ctx.op("vector", lambda e, cc=cc: e.scalar_tensor_tensor(
            out=vloc[:], in0=hbuf[:], scalar=par["sc_"][:, cc:cc + 1],
            in1=vloc[:], op0=ALU.mult, op1=ALU.add), inc=(sV, 7))
        ctx.wait("scalar", sV)
        if cc >= 2:
            ctx.wait("scalar", sStS[cc % 2], 16 * (cc // 2))
        ctx.op("scalar", lambda e, syb=syb: e.activation(
            out=syb, in_=vloc[:], func=AF.Silu), inc=(sS, 1))
        ctx.wait("sync", sS)
        ctx.op("sync", lambda e, cc=cc, syb=syb: e.dma_start(
            out=io["sy"][cc * 128:cc * 128 + 128, :], in_=syb),
            inc=(sStS[cc % 2], 16))
    ctx.wait("sync", sStS[0])
    ctx.wait("sync", sStS[1])


def build_launch_B():
    return _wrap(decl_B, emit_B)


def stage_inputs_B(core, W, v_full_fm):
    b, r = core // 2, core % 2
    ch0 = r * 512
    d = {"v_ch": np.ascontiguousarray(v_full_fm[ch0:ch0 + 512, :])}
    sl = slice(ch0, ch0 + 512)
    d["cw0"] = stage_bias(W["L0_attn_conv_w"][sl, 0, 0])
    d["cw1"] = stage_bias(W["L0_attn_conv_w"][sl, 0, 1])
    d["cw2"] = stage_bias(W["L0_attn_conv_w"][sl, 0, 2])
    d["cb"] = stage_bias(W["L0_attn_conv_b"][sl])
    d["la"] = stage_bias(W["L0_attn_logit_a"][sl])
    d["sb_"] = stage_bias(W["L0_attn_ssm_b"][sl])
    d["sc_"] = stage_bias(W["L0_attn_ssm_c"][sl])
    d["sd_"] = stage_bias(W["L0_attn_ssm_d"][sl])
    return d





def decl_C(din, dmid, dout):
    dmid("fq", [D, NT], BF16)
    dmid("kf", [8 * D, 512], BF16)
    dmid("vf", [T, D], BF16)
    din("c_ones_bf", [128, 1])
    din("c_ones_row", [1, 128])
    din("c_cfix", [128, 1])
    dout("g", [D, NT], BF16)


def emit_C(nc, ctx, io, en):
    KC = T // 128
    qbuf = en(nc.sbuf_tensor("qbuf", [128, DC * NT], BF16))
    ksb = en(nc.sbuf_tensor("ksb", [128, DC * T], BF16))
    psb = en(nc.sbuf_tensor("psb", [128, KC * 512], BF16))
    vbuf = en(nc.sbuf_tensor("vbuf", [128, 2 * 512], BF16))
    sqscr = en(nc.sbuf_tensor("sqscr", [128, 2 * 512], BF16))
    dinv = en(nc.sbuf_tensor("dinv", [128, 512], F32))
    onorm = en(nc.sbuf_tensor("onorm", [128, 512], F32))
    stg = en(nc.sbuf_tensor("stg", [128, 4 * 512], BF16))
    rows = en(nc.sbuf_tensor("rows", [1, 64], F32))
    bias_col = en(nc.sbuf_tensor("bias_col", [128, 1], F32))
    ones_bf = en(nc.sbuf_tensor("ones_bf", [128, 1], BF16))
    ones_row = en(nc.sbuf_tensor("ones_row", [1, 128], F32))
    pb = [en(nc.psum_tensor(f"pb{i}", [128, 512], F32)) for i in range(8)]

    sLd = ctx.sem("ld")
    sLg = ctx.sem("ldg")
    vload = DBuf(ctx, 2, "vl")
    sT = ctx.sem("t")
    sV = ctx.sem("v")
    sS = ctx.sem("s")
    stage = Stage(ctx, stg, 4, 512, "sg")

    ctx.op("sync", lambda e: e.dma_start(
        out=qbuf[:].rearrange("p (c t) -> p c t", c=DC),
        in_=io["fq"].rearrange("(c p) t -> p c t", p=128)),
        inc=(sLd, 16))
    for k_ in range(8):
        ctx.op("sync", lambda e, k_=k_: e.dma_start(
            out=ksb[:].rearrange("p (c k t) -> p c k t", c=DC, k=8)
            [:, :, k_, :],
            in_=io["kf"].rearrange("(k c p) t -> k p c t", c=DC, p=128)
            [k_]), inc=(sLd, 16))
    ctx.op("gpsimd", lambda e: e.dma_start(out=ones_bf[:],
                                           in_=io["c_ones_bf"]),
           inc=(sLg, 16))
    ctx.op("sync", lambda e: e.dma_start(out=ones_row[:],
                                         in_=io["c_ones_row"]),
           inc=(sLd, 16))

    ctx.wait("scalar", sLd)
    ctx.wait("tensor", sLd)
    ctx.wait("tensor", sLg)
    ctx.op("sync", lambda e: e.dma_start(out=bias_col[:],
                                         in_=io["c_cfix"]),
           inc=(sLd, 16))
    sPV = ctx.sem("pv")
    sDen = ctx.sem("den")
    for qb in range(NTT):
        qo = qb * 512
        ctx.wait("scalar", sLd)
        ctx.wait("scalar", sV)
        if qb >= 1:
            ctx.wait("scalar", sPV)
            ctx.wait("tensor", sV)
        tS = {}
        eS = {}

        def emit_S(kc, qo=qo):
            sb_ = kc % 2
            if kc >= 2:
                ctx.wait("tensor", sS, eS[kc - 2])
            for dc in range(DC):
                ctx.op("tensor", lambda e, kc=kc, dc=dc, qo=qo, sb_=sb_:
                       e.matmul(
                           pb[sb_][:, 0:512],
                           ksb[:, dc * T + kc * 128:dc * T + kc * 128
                               + 128],
                           qbuf[:, dc * NT + qo:dc * NT + qo + 512],
                           start=(dc == 0), stop=(dc == DC - 1)),
                       inc=(sT, 1) if dc == DC - 1 else None)
            tS[kc] = sT.n

        def emit_exp(kc):
            ctx.wait("scalar", sT, tS[kc])
            sb_ = kc % 2
            ctx.op("scalar", lambda e, kc=kc, sb_=sb_: e.activation(
                out=psb[:, kc * 512:kc * 512 + 512],
                in_=pb[sb_][:, 0:512], func=AF.Exp,
                bias=bias_col[:, 0:1], scale=1.0 / 32.0), inc=(sS, 1))
            eS[kc] = sS.n

        emit_S(0)
        emit_exp(0)
        for kc in range(KC):
            if kc + 1 < KC:
                emit_S(kc + 1)
                emit_exp(kc + 1)
            ctx.wait("tensor", sS, eS[kc])
            ctx.op("tensor", lambda e, kc=kc: e.matmul(
                pb[2][0:1, 0:512], ones_bf[:, 0:1],
                psb[:, kc * 512:kc * 512 + 512],
                start=(kc == 0), stop=(kc == KC - 1)),
                inc=(sDen, 1) if kc == KC - 1 else None)
        ctx.wait("vector", sDen)
        ctx.op("vector", lambda e: e.reciprocal(
            out=dinv[0:1, 0:512], in_=pb[2][0:1, 0:512]), inc=(sV, 1))
        ctx.wait("tensor", sV)
        ctx.op("tensor", lambda e: e.matmul(
            pb[7][:, 0:512], ones_row[0:1, :], dinv[0:1, 0:512],
            start=True, stop=True), inc=(sT, 1))
        ctx.wait("vector", sT)
        ctx.op("vector", lambda e: e.tensor_copy(
            out=dinv[:, 0:512], in_=pb[7][:, 0:512]), inc=(sV, 1))
        for half in range(2):
            ctx.wait("tensor", sV)
            lslot = {}

            def emit_vload(kc, half=half, qb=qb):
                nld = qb * 2 * KC + half * KC + kc
                if nld >= 2:
                    ctx.wait("scalar", sPV, (nld - 1) * 4)
                lslot[kc] = vload.load(
                    "scalar", lambda e, kc=kc, half=half: e.dma_start(
                        out=vbuf[:, (kc % 2) * 512:
                                 (kc % 2) * 512 + 512],
                        in_=io["vf"][kc * 128:kc * 128 + 128,
                                     half * 512:half * 512 + 512]))

            emit_vload(0)
            for kc in range(KC):
                if kc + 1 < KC:
                    emit_vload(kc + 1)
                vload.wait("tensor", lslot[kc])
                for du in range(4):
                    ctx.op("tensor", lambda e, kc=kc, du=du:
                           e.matmul(
                               pb[3 + du][:, 0:512],
                               vbuf[:, (kc % 2) * 512 + du * 128:
                                    (kc % 2) * 512 + du * 128 + 128],
                               psb[:, kc * 512:kc * 512 + 512],
                               start=(kc == 0), stop=(kc == KC - 1)),
                           inc=(sPV, 1))
            ctx.wait("vector", sPV)
            for du in range(4):
                dc = half * 4 + du
                ctx.wait("vector", sS)
                ctx.op("vector", lambda e, du=du: e.tensor_mul(
                    out=onorm[:, 0:512], in0=pb[3 + du][:, 0:512],
                    in1=dinv[:, 0:512]), inc=(sV, 1))
                ctx.wait("scalar", sV)
                tile, slot = stage.next("scalar")
                ctx.op("scalar", lambda e, tile=tile: e.activation(
                    out=tile, in_=onorm[:, 0:512], func=AF.Silu),
                    inc=(sS, 1))
                ctx.wait("sync", sS)
                stage.store("sync", lambda e, tile=tile, dc=dc, qo=qo:
                            e.dma_start(
                                out=io["g"].rearrange(
                                    "(c p) t -> p c t", p=128)
                                [:, dc:dc + 1, qo:qo + 512],
                                in_=tile), slot)
    stage.final_waits("sync")


def build_launch_C():
    return _wrap(decl_C, emit_C)


def stage_inputs_C(core, W, fq_mine, kf_full, vf_full):
    d = {"fq": fq_mine, "kf": kf_full, "vf": vf_full}
    d["c_ones_bf"] = np.ones((128, 1), np.float32)
    d["c_ones_row"] = np.ones((1, 128), np.float32)
    d["c_cfix"] = np.full((128, 1), -8.0, np.float32)
    return d





def decl_D1(din, dmid, dout):
    dmid("xn0", [D, NT], F32)
    dmid("u", [D, NT], BF16)
    dmid("syr", [D, NT], BF16)
    din("wout", [D, D])
    for nm in ("b_out", "lnw", "lnb"):
        din(nm, [128, 8])
    din("c_ones_bf", [128, 1])
    din("c_ones_row", [1, 128])
    dout("s0", [D, NT], F32)


def emit_D1(nc, ctx, io, en):
    hbuf = en(nc.sbuf_tensor("hbuf", [128, DC * NT], F32))
    ubuf = en(nc.sbuf_tensor("ubuf", [128, DC * NT], BF16))
    sybuf = en(nc.sbuf_tensor("sybuf", [128, DC * NT], BF16))
    wsb = en(nc.sbuf_tensor("wsb", [128, DC * 1024], BF16))
    stgf = en(nc.sbuf_tensor("stgf", [128, 2 * 512], F32))
    rows = en(nc.sbuf_tensor("rows", [1, 2 * NT + 512], F32))
    ones_bf = en(nc.sbuf_tensor("ones_bf", [128, 1], BF16))
    ones_row = en(nc.sbuf_tensor("ones_row", [1, 128], F32))
    par = {nm: en(nc.sbuf_tensor(nm + "_sb", [128, 8], F32))
           for nm in ("b_out", "lnw", "lnb")}
    pb = [en(nc.psum_tensor(f"pb{i}", [128, 512], F32)) for i in range(8)]

    sLd = ctx.sem("ld")
    sLg = ctx.sem("ldg")
    sZ = ctx.sem("z")
    sT = ctx.sem("t")
    sE = ctx.sem("evac")
    sVh = ctx.sem("vh")
    sSt = ctx.sem("st")
    consts = dict(ones_bf=ones_bf, ones_row=ones_row)

    ctx.op("sync", lambda e: e.dma_start(
        out=hbuf[:].rearrange("p (c t) -> p c t", c=DC),
        in_=io["xn0"].rearrange("(c p) t -> p c t", p=128)),
        inc=(sLd, 16))
    ctx.op("sync", lambda e: e.dma_start(
        out=ubuf[:].rearrange("p (c t) -> p c t", c=DC),
        in_=io["u"].rearrange("(c p) t -> p c t", p=128)),
        inc=(sLd, 16))
    ctx.op("sync", lambda e: e.dma_start(
        out=sybuf[:].rearrange("p (c t) -> p c t", c=DC),
        in_=io["syr"].rearrange("(c p) t -> p c t", p=128)),
        inc=(sLd, 16))
    ctx.op("sync", lambda e: e.dma_start(out=ones_row[:],
                                         in_=io["c_ones_row"]),
           inc=(sLd, 16))
    for nm in ("b_out", "lnw", "lnb"):
        ctx.op("sync", lambda e, nm=nm: e.dma_start(
            out=par[nm][:], in_=io[nm]), inc=(sLd, 16))
    ctx.op("gpsimd", lambda e: e.dma_start(out=ones_bf[:],
                                           in_=io["c_ones_bf"]),
           inc=(sLg, 16))
    ctx.op("gpsimd", lambda e: e.dma_start(
        out=wsb[:].rearrange("p (c m) -> p c m", c=DC),
        in_=io["wout"].rearrange("(c p) m -> p c m", p=128)),
        inc=(sLg, 16))

    ctx.wait("vector", sLd)
    ctx.op("vector", lambda e: e.tensor_mul(
        out=ubuf[:], in0=ubuf[:], in1=sybuf[:]), inc=(sZ, 1))

    ctx.wait("tensor", sZ)
    ctx.wait("tensor", sLg)
    for mc in range(DC):
        base = (mc % 2) * NTT
        if mc >= 2:
            ctx.wait("tensor", sE, (mc - 1) * NTT)
        for dc in range(DC):
            for tt in range(NTT):
                ctx.op("tensor", lambda e, mc=mc, dc=dc, tt=tt,
                       base=base: e.matmul(
                           pb[base + tt][:, 0:512],
                           wsb[:, (dc * DC + mc) * 128:
                               (dc * DC + mc) * 128 + 128],
                           ubuf[:, dc * NT + tt * 512:
                                dc * NT + tt * 512 + 512],
                           start=(dc == 0), stop=(dc == DC - 1)),
                       inc=(sT, 1) if (dc == DC - 1 and tt == NTT - 1)
                       else None)
        ctx.wait("scalar", sT, mc + 1)
        ctx.wait("scalar", sLd)
        for tt in range(NTT):
            slot = (mc * NTT + tt) % 2
            nv = mc * NTT + tt
            if nv >= 2:
                ctx.wait("scalar", sVh, nv - 1)
            ctx.op("scalar", lambda e, slot=slot, mc=mc, tt=tt,
                   base=base: e.activation(
                       out=stgf[:, slot * 512:slot * 512 + 512],
                       in_=pb[base + tt][:, 0:512], func=AF.Identity,
                       bias=par["b_out"][:, mc:mc + 1], scale=1.0),
                   inc=(sE, 1))
            ctx.wait("vector", sE)
            ctx.op("vector", lambda e, slot=slot, mc=mc, tt=tt:
                   e.tensor_add(
                       out=hbuf[:, mc * NT + tt * 512:
                                mc * NT + tt * 512 + 512],
                       in0=stgf[:, slot * 512:slot * 512 + 512],
                       in1=hbuf[:, mc * NT + tt * 512:
                                mc * NT + tt * 512 + 512]),
                   inc=(sVh, 1))

    ln_done = emit_ln_fm(
        ctx, consts, hbuf, sybuf, rows, pb, NT, LN_EPS,
        pre=[("scalar", sVh, 32), ("scalar", sZ, 1)], tag="s0")
    sFin = ctx.sem("fin")
    ctx.wait("vector", ln_done)
    for c in range(DC):
        ctx.op("vector", lambda e, c=c: e.tensor_scalar(
            out=hbuf[:, c * NT:(c + 1) * NT],
            in0=hbuf[:, c * NT:(c + 1) * NT],
            scalar1=par["lnw"][:, c:c + 1],
            scalar2=par["lnb"][:, c:c + 1],
            op0=ALU.mult, op1=ALU.add), inc=(sFin, 1))
    ctx.wait("sync", sFin)
    ctx.op("sync", lambda e: e.dma_start(
        out=io["s0"].rearrange("(c p) t -> p c t", p=128),
        in_=hbuf[:].rearrange("p (c t) -> p c t", c=DC)),
        inc=(sSt, 16))
    ctx.wait("sync", sSt)


def build_launch_D1():
    return _wrap(decl_D1, emit_D1)



def decl_D2(din, dmid, dout):
    din("x_fm", [D, NT])
    dmid("s0", [D, NT], F32)
    dmid("g", [D, NT], BF16)
    din("c_ones_bf", [128, 1])
    din("c_ones_row", [1, 128])
    dout("x1", [D, NT], F32)
    dout("xn1", [D, NT], F32)
    dout("xn1_bf", [D, NT], BF16)


def emit_D2(nc, ctx, io, en):
    xbuf = en(nc.sbuf_tensor("xbuf", [128, DC * NT], F32))
    sbuf0 = en(nc.sbuf_tensor("sbuf0", [128, DC * NT], F32))
    gbuf = en(nc.sbuf_tensor("gbuf", [128, DC * NT], BF16))
    rows = en(nc.sbuf_tensor("rows", [1, 2 * NT + 512], F32))
    ones_bf = en(nc.sbuf_tensor("ones_bf", [128, 1], BF16))
    ones_row = en(nc.sbuf_tensor("ones_row", [1, 128], F32))
    pb = [en(nc.psum_tensor(f"pb{i}", [128, 512], F32)) for i in range(4)]

    sLd = ctx.sem("ld")
    sLg = ctx.sem("ldg")
    sV = ctx.sem("v")
    sStX = ctx.sem("stx1")
    sStN = ctx.sem("stxn")
    sStB = ctx.sem("stxb")
    consts = dict(ones_bf=ones_bf, ones_row=ones_row)

    for nm, dst in (("x_fm", xbuf), ("s0", sbuf0)):
        ctx.op("sync", lambda e, nm=nm, dst=dst: e.dma_start(
            out=dst[:].rearrange("p (c t) -> p c t", c=DC),
            in_=io[nm].rearrange("(c p) t -> p c t", p=128)),
            inc=(sLd, 16))
    ctx.op("sync", lambda e: e.dma_start(
        out=gbuf[:].rearrange("p (c t) -> p c t", c=DC),
        in_=io["g"].rearrange("(c p) t -> p c t", p=128)),
        inc=(sLd, 16))
    ctx.op("sync", lambda e: e.dma_start(out=ones_row[:],
                                         in_=io["c_ones_row"]),
           inc=(sLd, 16))
    ctx.op("gpsimd", lambda e: e.dma_start(out=ones_bf[:],
                                           in_=io["c_ones_bf"]),
           inc=(sLg, 16))

    ctx.wait("vector", sLd)
    ctx.op("vector", lambda e: e.tensor_mul(
        out=sbuf0[:], in0=sbuf0[:], in1=gbuf[:]))
    ctx.op("vector", lambda e: e.tensor_add(
        out=xbuf[:], in0=xbuf[:], in1=sbuf0[:]), inc=(sV, 1))
    ctx.wait("sync", sV)
    ctx.op("sync", lambda e: e.dma_start(
        out=io["x1"].rearrange("(c p) t -> p c t", p=128),
        in_=xbuf[:].rearrange("p (c t) -> p c t", c=DC)),
        inc=(sStX, 16))
    ln_done = emit_ln_fm(
        ctx, consts, xbuf, gbuf, rows, pb, NT, LN_EPS,
        pre=[("scalar", sV, 1), ("tensor", sLd, 64),
             ("tensor", sLg, 16), ("vector", sStX, 16)], tag="1")
    sC = ctx.sem("cast")
    ctx.wait("sync", ln_done)
    ctx.op("sync", lambda e: e.dma_start(
        out=io["xn1"].rearrange("(c p) t -> p c t", p=128),
        in_=xbuf[:].rearrange("p (c t) -> p c t", c=DC)),
        inc=(sStN, 16))
    ctx.wait("scalar", ln_done)
    ctx.op("scalar", lambda e: e.activation(
        out=gbuf[:], in_=xbuf[:], func=AF.Identity), inc=(sC, 1))
    ctx.wait("sync", sC)
    ctx.op("sync", lambda e: e.dma_start(
        out=io["xn1_bf"].rearrange("(c p) t -> p c t", p=128),
        in_=gbuf[:].rearrange("p (c t) -> p c t", c=DC)),
        inc=(sStB, 16))
    ctx.wait("sync", sStX)
    ctx.wait("sync", sStN)
    ctx.wait("sync", sStB)


def build_launch_D2():
    return _wrap(decl_D2, emit_D2)



def decl_E(din, dmid, dout):
    dmid("xn1_bf", [D, NT], BF16)
    for nm in ("wq", "wk", "wv"):
        din(nm, [D, 2 * D])
    dout("Q", [2 * D, NT], BF16)
    dout("Kr", [2 * D, NT], BF16)
    dout("Vr", [NT, 2 * D], BF16)


def emit_E(nc, ctx, io, en):
    xnbuf = en(nc.sbuf_tensor("xnbuf", [128, DC * NT], BF16))
    wsb = en(nc.sbuf_tensor("wsb", [128, DC * 2048], BF16))
    stg = en(nc.sbuf_tensor("stg", [128, 8 * 512], BF16))
    pb = [en(nc.psum_tensor(f"pb{i}", [128, 512], F32)) for i in range(8)]

    sLd = ctx.sem("ld")
    sLw = [ctx.sem(f"ldw{i}") for i in range(3)]
    sT = ctx.sem("t")
    sE = ctx.sem("evac")
    stage = Stage(ctx, stg, 8, 512, "sg")

    ctx.op("sync", lambda e: e.dma_start(
        out=xnbuf[:].rearrange("p (c t) -> p c t", c=DC),
        in_=io["xn1_bf"].rearrange("(c p) t -> p c t", p=128)),
        inc=(sLd, 16))

    for wi, (wn, on) in enumerate((("wq", "Q"), ("wk", "Kr"),
                                   ("wv", "Vr"))):
        ctx.wait("gpsimd", sT)
        ctx.op("gpsimd", lambda e, wn=wn: e.dma_start(
            out=wsb[:].rearrange("p (c m) -> p c m", c=DC),
            in_=io[wn].rearrange("(c p) m -> p c m", p=128)),
            inc=(sLw[wi], 16))
        ctx.wait("tensor", sLw[wi])
        if wi == 0:
            ctx.wait("tensor", sLd, 16)
        if on != "Vr":
            hbm = io[on]
            for mc in range(16):
                base = (mc % 2) * NTT
                if mc >= 2:
                    ctx.wait("tensor", sE, sE.n - NTT)
                for dc in range(DC):
                    for tt in range(NTT):
                        ctx.op("tensor", lambda e, mc=mc, dc=dc, tt=tt,
                               base=base: e.matmul(
                                   pb[base + tt][:, 0:512],
                                   wsb[:, (dc * 16 + mc) * 128:
                                       (dc * 16 + mc) * 128 + 128],
                                   xnbuf[:, dc * NT + tt * 512:
                                         dc * NT + tt * 512 + 512],
                                   start=(dc == 0), stop=(dc == DC - 1)),
                               inc=(sT, 1)
                               if (dc == DC - 1 and tt == NTT - 1)
                               else None)
                ctx.wait("scalar", sT)
                for tt in range(NTT):
                    tile, slot = stage.next("scalar")
                    ctx.op("scalar", lambda e, tile=tile, tt=tt,
                           base=base: e.activation(
                               out=tile, in_=pb[base + tt][:, 0:512],
                               func=AF.Identity), inc=(sE, 1))
                    ctx.wait("sync", sE)
                    stage.store("sync", lambda e, tile=tile, hbm=hbm,
                                mc=mc, tt=tt: e.dma_start(
                                    out=hbm.rearrange(
                                        "(c p) t -> p c t", p=128)
                                    [:, mc:mc + 1,
                                     tt * 512:tt * 512 + 512],
                                    in_=tile), slot)
        else:
            for ts in range(16):
                base = (ts % 2) * NTT
                if ts >= 2:
                    ctx.wait("tensor", sE, sE.n - NTT)
                for ch in range(4):
                    for dc in range(DC):
                        ctx.op("tensor", lambda e, ts=ts, ch=ch, dc=dc,
                               base=base: e.matmul(
                                   pb[base + ch][:, 0:512],
                                   xnbuf[:, dc * NT + ts * 128:
                                         dc * NT + ts * 128 + 128],
                                   wsb[:, dc * 2048 + ch * 512:
                                       dc * 2048 + ch * 512 + 512],
                                   start=(dc == 0), stop=(dc == DC - 1)),
                               inc=(sT, 1)
                               if (dc == DC - 1 and ch == 3) else None)
                ctx.wait("scalar", sT)
                for ch in range(4):
                    tile, slot = stage.next("scalar")
                    ctx.op("scalar", lambda e, tile=tile, ch=ch,
                           base=base: e.activation(
                               out=tile, in_=pb[base + ch][:, 0:512],
                               func=AF.Identity), inc=(sE, 1))
                    ctx.wait("sync", sE)
                    stage.store("sync", lambda e, tile=tile, ts=ts,
                                ch=ch: e.dma_start(
                                    out=io["Vr"][ts * 128:ts * 128
                                                 + 128,
                                                 ch * 512:ch * 512
                                                 + 512],
                                    in_=tile), slot)
    stage.final_waits("sync")


def build_launch_E():
    return _wrap(decl_E, emit_E)





KCMAX = [8, 16, 24, 32]
MASKLO = [4 * min(OWN[0][i], OWN[1][i]) for i in range(4)]


def mask_table(role):
    out = np.zeros((128, 32 * 512), np.float32)
    pk = np.arange(128)[:, None]
    fq = np.arange(512)[None, :]
    for slot in range(4):
        jb = OWN[role][slot]
        lo = MASKLO[slot]
        for i, kc in enumerate(range(lo, KCMAX[slot])):
            if kc < 4 * jb:
                tile = np.ones((128, 512), np.float32)
            elif kc <= 4 * jb + 3:
                m = kc - 4 * jb
                tile = (pk <= fq - 128 * m).astype(np.float32)
            else:
                tile = np.zeros((128, 512), np.float32)
            out[:, (slot * 8 + i) * 512:(slot * 8 + i + 1) * 512] = tile
    return out


def e64c_mat():
    E = np.zeros((128, 2), np.float32)
    E[:64, 0] = 1.0
    E[64:, 1] = 1.0
    return E


def decl_F(din, dmid, dout, dbg=False):
    dmid("Q", [2 * D, NT], BF16)
    dmid("Kf", [16 * D, 512], BF16)
    dmid("Vf", [T, 2 * D], BF16)
    din("lam4", [16, 4 * 64])
    din("rsc", [128, 1])
    din("wo", [2 * D, D])
    din("c_maskt", [128, 32 * 512])
    din("c_id16", [16, 16])
    din("c_e64c", [128, 2])
    din("c_cfix", [128, 2])
    din("c_ones_bf", [128, 1])
    din("c_ones_row", [1, 128])
    dout("s1", [D, NT], BF16)
    if dbg:
        dout("dbg_oall", [128, 16 * NT], BF16)
        dout("dbg_p", [128, 1024], BF16)
        dout("dbg_b2", [128, 2], F32)
        dout("dbg_t", [128, 1024], F32)
        dout("dbg_r2", [2, 64], F32)
        dout("dbg_den", [2, 512], F32)
        dout("dbg_brow", [1, 2], F32)
        dout("dbg_bias2", [128, 2], F32)


def emit_F(nc, ctx, io, en, dbg=False, nheads=16, mode='full'):
    oall = en(nc.sbuf_tensor("oall", [128, 16 * NT], BF16))
    wsb = en(nc.sbuf_tensor("wsb", [128, 16 * 1024], BF16))
    masks = en(nc.sbuf_tensor("masks", [128, 32 * 512], BF16))
    qh = en(nc.sbuf_tensor("qh", [128, 2 * NT], BF16))
    kh = en(nc.sbuf_tensor("kh", [128, 2 * T], BF16))
    vh = en(nc.sbuf_tensor("vh", [128, 2 * T], BF16))
    pbuf = en(nc.sbuf_tensor("pbuf", [128, 4 * 512], BF16))
    sqs = en(nc.sbuf_tensor("sqs", [128, 2 * 512], BF16))
    combsq = en(nc.sbuf_tensor("combsq", [128, 512], BF16))
    t1b = en(nc.sbuf_tensor("t1b", [128, 512], F32))
    t2b = en(nc.sbuf_tensor("t2b", [128, 512], F32))
    d1s = en(nc.sbuf_tensor("d1s", [128, 512], F32))
    d2s = en(nc.sbuf_tensor("d2s", [128, 512], F32))
    combf = en(nc.sbuf_tensor("combf", [128, 512], F32))
    lam4 = en(nc.sbuf_tensor("lam4s", [16, 4 * 64], F32))
    lamt = en(nc.sbuf_tensor("lamt", [16, 4], F32))
    lrow = en(nc.sbuf_tensor("lrow", [1, 16], F32))
    lamnb = en(nc.sbuf_tensor("lamnb", [128, 16], F32))
    bias2 = en(nc.sbuf_tensor("bias2", [128, 2], F32))
    brow = en(nc.sbuf_tensor("brow", [1, 2], F32))
    rs02 = en(nc.sbuf_tensor("rs02", [128, 1], F32))
    id16 = en(nc.sbuf_tensor("id16", [16, 16], F32))
    e64c = en(nc.sbuf_tensor("e64c", [128, 2], BF16))
    ones_bf = en(nc.sbuf_tensor("ones_bf", [128, 1], BF16))
    ones_row = en(nc.sbuf_tensor("ones_row", [1, 128], F32))
    rows2 = en(nc.sbuf_tensor("rows2", [2, 64], F32))
    rrow = en(nc.sbuf_tensor("rrow", [1, 512], F32))
    stg = en(nc.sbuf_tensor("stg", [128, 4 * 512], BF16))
    pb = [en(nc.psum_tensor(f"pb{i}", [128, 512], F32)) for i in range(8)]

    sLd = ctx.sem("ld")
    sLg = ctx.sem("ldg")
    sLq = ctx.sem("ldq")
    sLk = ctx.sem("ldk")
    sLv = ctx.sem("ldv")
    sT = ctx.sem("t")
    sV = ctx.sem("v")
    sS = ctx.sem("s")
    sPk = ctx.sem("pk")
    sOF = ctx.sem("of")
    sE = ctx.sem("evac")
    stage = Stage(ctx, stg, 4, 512, "sg")

    ctx.op("sync", lambda e: e.dma_start(out=lam4[:], in_=io["lam4"]),
           inc=(sLd, 16))
    ctx.op("sync", lambda e: e.dma_start(out=rs02[:], in_=io["rsc"]),
           inc=(sLd, 16))
    ctx.op("sync", lambda e: e.dma_start(out=id16[:], in_=io["c_id16"]),
           inc=(sLd, 16))
    ctx.op("sync", lambda e: e.dma_start(out=ones_row[:],
                                         in_=io["c_ones_row"]),
           inc=(sLd, 16))
    ctx.op("sync", lambda e: e.dma_start(out=bias2[:],
                                         in_=io["c_cfix"]),
           inc=(sLd, 16))
    ctx.op("gpsimd", lambda e: e.dma_start(out=masks[:],
                                           in_=io["c_maskt"]),
           inc=(sLg, 16))
    ctx.op("gpsimd", lambda e: e.dma_start(out=e64c[:],
                                           in_=io["c_e64c"]),
           inc=(sLg, 16))
    ctx.op("gpsimd", lambda e: e.dma_start(out=ones_bf[:],
                                           in_=io["c_ones_bf"]),
           inc=(sLg, 16))
    ctx.op("gpsimd", lambda e: e.dma_start(
        out=wsb[:].rearrange("p (c m) -> p c m", c=16),
        in_=io["wo"].rearrange("(c p) m -> p c m", p=128)),
        inc=(sLg, 16))

    if dbg:
        ctx.op("vector", lambda e: e.memset(rows2[0:2, 0:64], 0.0))
    ctx.wait("vector", sLd)
    ctx.op("vector", lambda e: e.tensor_mul(
        out=lam4[:, 0:64], in0=lam4[:, 0:64], in1=lam4[:, 64:128]),
        inc=(sV, 1))
    ctx.op("vector", lambda e: e.tensor_mul(
        out=lam4[:, 128:192], in0=lam4[:, 128:192],
        in1=lam4[:, 192:256]), inc=(sV, 1))
    ctx.wait("vector", sV)
    ctx.op("vector", lambda e: e.reduce_sum(
        out=lamt[0:16, 0:1], in_=lam4[0:16, 0:64], axis=AX.X),
        inc=(sV, 1))
    ctx.op("vector", lambda e: e.reduce_sum(
        out=lamt[0:16, 1:2], in_=lam4[0:16, 128:192], axis=AX.X),
        inc=(sV, 1))
    ctx.wait("scalar", sV)
    ctx.op("scalar", lambda e: e.activation(
        out=lamt[0:16, 0:2], in_=lamt[0:16, 0:2], func=AF.Exp),
        inc=(sS, 1))
    ctx.wait("vector", sS)
    ctx.op("vector", lambda e: e.tensor_sub(
        out=lamt[0:16, 2:3], in0=lamt[0:16, 1:2], in1=lamt[0:16, 0:1]),
        inc=(sV, 1))
    ctx.wait("vector", sV)
    ctx.op("vector", lambda e: e.tensor_scalar_add(
        out=lamt[0:16, 2:3], in0=lamt[0:16, 2:3], scalar1=-0.8),
        inc=(sV, 1))
    ctx.wait("tensor", sV)
    ctx.wait("tensor", sLd)
    ctx.op("tensor", lambda e: e.transpose(
        pb[0][0:1, 0:16], lamt[0:16, 2:3], id16[:]), inc=(sT, 1))
    ctx.wait("vector", sT)
    ctx.op("vector", lambda e: e.tensor_copy(out=lrow[:],
                                             in_=pb[0][0:1, 0:16]),
           inc=(sV, 1))
    ctx.wait("tensor", sV)
    ctx.op("tensor", lambda e: e.matmul(
        pb[1][:, 0:16], ones_row[0:1, :], lrow[:], start=True,
        stop=True), inc=(sT, 1))
    ctx.wait("vector", sT)
    ctx.op("vector", lambda e: e.tensor_copy(out=lamnb[:],
                                             in_=pb[1][:, 0:16]),
           inc=(sV, 1))
    ctx.wait("vector", sV)
    ctx.op("vector", lambda e: e.tensor_scalar_mul(
        out=rs02[:], in0=rs02[:], scalar1=0.2), inc=(sV, 1))

    ctx.wait("tensor", sLg)
    ctx.wait("vector", sLg)
    t_end = {}
    pk_end = {}
    lqv = {}

    def emit_loads(h):
        hs = h % 2
        if h >= 2:
            ctx.wait("scalar", sT, t_end[h - 2])
            ctx.wait("scalar", sPk, pk_end[h - 2])
        ctx.op("scalar", lambda e, h=h, hs=hs: e.dma_start(
            out=qh[:, hs * NT:hs * NT + NT],
            in_=io["Q"][h * 128:h * 128 + 128, :]),
            inc=(sLq, 16))
        ctx.op("scalar", lambda e, h=h, hs=hs: e.dma_start(
            out=kh[:, hs * T:hs * T + T].rearrange(
                "p (k t) -> p k t", k=8),
            in_=io["Kf"].rearrange("(k c) t -> c k t", c=2048)
            [h * 128:h * 128 + 128]), inc=(sLk, 16))
        ctx.op("scalar", lambda e, h=h, hs=hs: e.dma_start(
            out=vh[:, hs * T:hs * T + T].rearrange(
                "p (kc c) -> p kc c", c=128),
            in_=io["Vf"][:, h * 128:h * 128 + 128].rearrange(
                "(kc p) c -> p kc c", p=128)), inc=(sLv, 16))
        lqv[h] = (sLq.n, sLk.n, sLv.n)

    ctx.wait("scalar", sLd)
    emit_loads(0)
    for h in range(nheads):
        hs = h % 2
        qho = hs * NT
        kho = hs * T
        ctx.wait("tensor", sLq, lqv[h][0])
        ctx.wait("tensor", sLk, lqv[h][1])
        ctx.wait("tensor", sLv, lqv[h][2])
        if h + 1 < nheads:
            emit_loads(h + 1)
        def mk_slot(slot, hs=hs, kho=kho, qho=qho, h=h):
            qo = slot * 512
            kcmax = KCMAX[slot]
            lo = MASKLO[slot]
            st = {"tS": {}, "eS": {}, "mV": {}, "base_pk": 0}

            def begin():
                st["base_pk"] = sPk.n
                ctx.wait("scalar", sV)
                ctx.wait("tensor", sS)

            def emit_S(kc):
                sb1, sb2 = kc % 2, 2 + kc % 2
                if kc >= 2:
                    ctx.wait("tensor", sS, st["eS"][kc - 2])
                for ci, (sb_, co) in enumerate(((sb1, 0), (sb2, 64))):
                    ctx.op("tensor", lambda e, kc=kc, sb_=sb_, co=co:
                           e.matmul(
                               pb[sb_][:, 0:512],
                               kh[co:co + 64, kho + kc * 128:
                                  kho + kc * 128 + 128],
                               qh[co:co + 64, qho + qo:qho + qo + 512],
                               start=True, stop=True), inc=(sT, 1))
                st["tS"][kc] = sT.n

            def emit_exp(kc):
                if kc >= 2:
                    ctx.wait("scalar", sPk, st["base_pk"] + kc - 1)
                elif st["base_pk"] > 0:
                    ctx.wait("scalar", sPk, st["base_pk"])
                ctx.wait("scalar", sT, st["tS"][kc])
                sb1, sb2 = kc % 2, 2 + kc % 2
                for ci, (sb_, ps) in enumerate(((sb1, 0), (sb2, 2))):
                    pslot = ps + kc % 2
                    ctx.op("scalar", lambda e, sb_=sb_, pslot=pslot,
                           ci=ci: e.activation(
                               out=pbuf[:, pslot * 512:pslot * 512
                                        + 512],
                               in_=pb[sb_][:, 0:512], func=AF.Exp,
                               bias=bias2[:, ci:ci + 1],
                               scale=1.0 / 8.0), inc=(sS, 1))
                st["eS"][kc] = sS.n
                if kc >= lo:
                    mi = slot * 8 + (kc - lo)
                    ctx.wait("vector", sS, st["eS"][kc])
                    for pslot in (kc % 2, 2 + kc % 2):
                        ctx.op("vector", lambda e, pslot=pslot, mi=mi:
                               e.tensor_mul(
                                   out=pbuf[:, pslot * 512:pslot * 512
                                            + 512],
                                   in0=pbuf[:, pslot * 512:pslot * 512
                                            + 512],
                                   in1=masks[:, mi * 512:mi * 512
                                             + 512]), inc=(sV, 1))
                    st["mV"][kc] = sV.n
                else:
                    st["mV"][kc] = None

            def den_pv(kc):
                if kc == 0:
                    ctx.wait("tensor", sOF)
                if st["mV"][kc] is not None:
                    ctx.wait("tensor", sV, st["mV"][kc])
                else:
                    ctx.wait("tensor", sS, st["eS"][kc])
                for ci, (pslot, dbank, obank) in enumerate(
                        ((kc % 2, 6, 4), (2 + kc % 2, 7, 5))):
                    ctx.op("tensor", lambda e, pslot=pslot, dbank=dbank,
                           kc=kc: e.matmul(
                               pb[dbank][0:1, 0:512], ones_bf[:, 0:1],
                               pbuf[:, pslot * 512:pslot * 512 + 512],
                               start=(kc == 0), stop=(kc == kcmax - 1)))
                    ctx.op("tensor", lambda e, pslot=pslot, obank=obank,
                           kc=kc: e.matmul(
                               pb[obank][:, 0:512],
                               vh[:, kho + kc * 128:kho + kc * 128
                                  + 128],
                               pbuf[:, pslot * 512:pslot * 512 + 512],
                               start=(kc == 0), stop=(kc == kcmax - 1)),
                           inc=(sPk, 1) if ci == 1 else None)

            def epilogue():
                ctx.wait("vector", sPk)
                ctx.op("vector", lambda e: e.reciprocal(
                    out=d1s[0:1, 0:512], in_=pb[6][0:1, 0:512]))
                ctx.op("vector", lambda e: e.reciprocal(
                    out=d2s[0:1, 0:512], in_=pb[7][0:1, 0:512]),
                    inc=(sV, 1))
                ctx.wait("tensor", sV)
                ctx.op("tensor", lambda e: e.matmul(
                    pb[6][:, 0:512], ones_row[0:1, :],
                    d1s[0:1, 0:512], start=True, stop=True))
                ctx.op("tensor", lambda e: e.matmul(
                    pb[7][:, 0:512], ones_row[0:1, :],
                    d2s[0:1, 0:512], start=True, stop=True),
                    inc=(sT, 1))
                ctx.wait("vector", sT)
                ctx.op("vector", lambda e: e.tensor_copy(
                    out=d1s[:], in_=pb[6][:, 0:512]))
                ctx.op("vector", lambda e: e.tensor_copy(
                    out=d2s[:], in_=pb[7][:, 0:512]))
                ctx.op("vector", lambda e: e.tensor_mul(
                    out=t1b[:], in0=pb[4][:, 0:512], in1=d1s[:]))
                ctx.op("vector", lambda e: e.tensor_mul(
                    out=t2b[:], in0=pb[5][:, 0:512], in1=d2s[:]))
                if mode == "full":
                    ctx.op("vector", lambda e, h=h:
                           e.scalar_tensor_tensor(
                               out=combf[:], in0=t2b[:],
                               scalar=lamnb[:, h:h + 1], in1=t1b[:],
                               op0=ALU.mult, op1=ALU.add), inc=(sV, 1))
                elif mode == "a1":
                    ctx.op("vector", lambda e: e.tensor_copy(
                        out=combf[:], in_=t1b[:]), inc=(sV, 1))
                else:
                    ctx.op("vector", lambda e: e.tensor_copy(
                        out=combf[:], in_=pb[4][:, 0:512]), inc=(sV, 1))
                ctx.wait("scalar", sV)
                ctx.op("scalar", lambda e: e.activation(
                    out=combsq[:], in_=combf[:], func=AF.Square),
                    inc=(sS, 1))
                ctx.wait("tensor", sS)
                ctx.op("tensor", lambda e: e.matmul(
                    pb[4][0:1, 0:512], ones_bf[:, 0:1], combsq[:],
                    start=True, stop=True), inc=(sT, 1))
                ctx.wait("vector", sT)
                ctx.op("vector", lambda e: e.tensor_scalar(
                    out=rrow[:], in0=pb[4][0:1, 0:512],
                    scalar1=1.0 / 128.0, scalar2=float(LN_EPS),
                    op0=ALU.mult, op1=ALU.add))
                ctx.op("vector", lambda e: e.reciprocal(
                    out=rrow[:], in_=rrow[:]), inc=(sV, 1))
                ctx.wait("scalar", sV)
                ctx.op("scalar", lambda e: e.activation(
                    out=rrow[:], in_=rrow[:], func=AF.Sqrt),
                    inc=(sS, 1))
                ctx.wait("tensor", sS)
                ctx.op("tensor", lambda e: e.matmul(
                    pb[5][:, 0:512], ones_row[0:1, :], rrow[:],
                    start=True, stop=True), inc=(sT, 1))
                ctx.wait("vector", sT)
                ctx.op("vector", lambda e, h=h, qo=qo:
                       e.scalar_tensor_tensor(
                           out=oall[:, h * NT + qo:h * NT + qo + 512],
                           in0=combf[:], scalar=rs02[:, 0:1],
                           in1=pb[5][:, 0:512], op0=ALU.mult,
                           op1=ALU.mult), inc=(sOF, 1))

            st["fns"] = (begin, emit_S, emit_exp, den_pv, epilogue)
            st["kcmax"] = kcmax
            return st

        slots_o = [mk_slot(s_) for s_ in range(NTT)]
        sl0 = slots_o[0]["fns"]
        sl0[0]()
        sl0[1](0)
        sl0[2](0)
        emitted = {0: 0}
        for slot in range(NTT):
            begin, emit_S, emit_exp, den_pv, epilogue = \
                slots_o[slot]["fns"]
            kcmax = slots_o[slot]["kcmax"]
            for kc in range(kcmax):
                if kc + 1 < kcmax and emitted[slot] < kc + 1:
                    emit_S(kc + 1)
                    emit_exp(kc + 1)
                    emitted[slot] = kc + 1
                den_pv(kc)
            if slot + 1 < NTT:
                nb, ns, ne, _, _ = slots_o[slot + 1]["fns"]
                nb()
                ns(0)
                ne(0)
                ns(1)
                ne(1)
                emitted[slot + 1] = 1
            epilogue()
        t_end[h] = sT.n
        pk_end[h] = sPk.n

    ctx.wait("tensor", sOF)
    if dbg:
        sDbg = ctx.sem("dbg")
        ctx.wait("sync", sOF)
        ctx.op("sync", lambda e: e.dma_start(
            out=io["dbg_oall"], in_=oall[:]), inc=(sDbg, 16))
        ctx.op("sync", lambda e: e.dma_start(
            out=io["dbg_b2"][:, 0:2], in_=lamnb[:, 0:2]),
            inc=(sDbg, 16))
        ctx.op("gpsimd", lambda e: e.dma_start(
            out=io["dbg_p"][0:16, 0:4], in_=lamt[:]), inc=(sDbg, 16))
        ctx.op("gpsimd", lambda e: e.dma_start(
            out=io["dbg_p"][0:1, 8:24], in_=lrow[:]), inc=(sDbg, 16))
        ctx.wait("sync", sDbg)
    sTo = ctx.sem("to")
    sEo = ctx.sem("eo")
    for mc in range(DC):
        base = (mc % 2) * NTT
        if mc >= 2:
            ctx.wait("tensor", sEo, (mc - 1) * NTT)
        for hc in range(16):
            for tt in range(NTT):
                ctx.op("tensor", lambda e, mc=mc, hc=hc, tt=tt,
                       base=base: e.matmul(
                           pb[base + tt][:, 0:512],
                           wsb[:, (hc * DC + mc) * 128:
                               (hc * DC + mc) * 128 + 128],
                           oall[:, hc * NT + tt * 512:
                                hc * NT + tt * 512 + 512],
                           start=(hc == 0), stop=(hc == 15)),
                       inc=(sTo, 1) if (hc == 15 and tt == NTT - 1)
                       else None)
        ctx.wait("scalar", sTo, mc + 1)
        for tt in range(NTT):
            tile, slot = stage.next("scalar")
            ctx.op("scalar", lambda e, tile=tile, tt=tt, base=base:
                   e.activation(out=tile, in_=pb[base + tt][:, 0:512],
                                func=AF.Identity), inc=(sEo, 1))
            ctx.wait("sync", sEo)
            stage.store("sync", lambda e, tile=tile, mc=mc, tt=tt:
                        e.dma_start(
                            out=io["s1"].rearrange("(c p) t -> p c t",
                                                   p=128)
                            [:, mc:mc + 1, tt * 512:tt * 512 + 512],
                            in_=tile), slot)
    stage.final_waits("sync")


def build_launch_F(dbg=False, nheads=16, mode='full'):
    return _wrap(
        lambda di, dm, do: decl_F(di, dm, do, dbg=dbg),
        lambda nc, ctx, io, en: emit_F(nc, ctx, io, en, dbg=dbg, nheads=nheads, mode=mode))


def stage_inputs_F(core, W, Q, Kf, Vf):
    r = core % 2
    lam4 = np.concatenate([W["L1_attn_lambda_q1"], W["L1_attn_lambda_k1"],
                           W["L1_attn_lambda_q2"], W["L1_attn_lambda_k2"]],
                          axis=1)
    return dict(Q=Q, Kf=Kf, Vf=Vf, lam4=np.ascontiguousarray(lam4),
                rsc=np.ascontiguousarray(
                    W["L1_attn_rms_scale"].reshape(128, 1)),
                wo=np.ascontiguousarray(W["L1_attn_Wo"]),
                c_maskt=mask_table(r),
                c_id16=np.eye(16, dtype=np.float32),
                c_e64c=e64c_mat(),
                c_cfix=np.full((128, 2), -4.0, np.float32),
                c_ones_bf=np.ones((128, 1), np.float32),
                c_ones_row=np.ones((1, 128), np.float32))





def rope_consts(rows):
    inv = 1.0 / 10000.0 ** (np.arange(0, 64, 2, dtype=np.float64) / 64)
    f = np.outer(rows.astype(np.float64), inv)
    cos = np.cos(f).astype(np.float32)
    sin = np.sin(f).astype(np.float32)
    C = np.zeros((128, NT), np.float32)
    S = np.zeros((128, NT), np.float32)
    for p in range(128):
        i = p % 32
        up = (p % 64) >= 32
        C[p] = cos[:, i]
        S[p] = sin[:, i] if up else -sin[:, i]
    return C, S


def e2t_mat():
    E = np.zeros((2, 128), np.float32)
    E[0, :64] = 1.0
    E[1, 64:] = 1.0
    return E


def swap_mat():
    Smat = np.zeros((128, 128), np.float32)
    for p in range(128):
        blk, off = p // 64, p % 64
        src = blk * 64 + ((off + 32) % 64)
        Smat[src, p] = 1.0
    return Smat


def decl_G1(din, dmid, dout):
    dmid("xn1", [D, NT], F32)
    din("ln1w", [128, 8])
    din("wq", [D, D])
    din("bq", [128, 8])
    din("wk", [D, 256])
    din("bk", [128, 2])
    din("wv", [D, 256])
    din("bvr", [1, 256])
    din("qnw", [128, 1])
    din("knw", [128, 1])
    din("c_cos", [128, NT])
    din("c_sin", [128, NT])
    din("c_e2t", [2, 128])
    din("c_e64c", [128, 2])
    din("c_swp", [128, 128])
    din("c_ones_bf", [128, 1])
    din("c_ones_row", [1, 128])
    dout("qg", [D, NT], BF16)
    dout("kg", [256, NT], BF16)
    dout("vg", [NT, 256], BF16)


def emit_G1(nc, ctx, io, en):
    xbuf = en(nc.sbuf_tensor("xbuf", [128, DC * NT], F32))
    hbf = en(nc.sbuf_tensor("hbf", [128, DC * NT], BF16))
    wsb = en(nc.sbuf_tensor("wsb", [128, DC * 1024], BF16))
    wkv = en(nc.sbuf_tensor("wkv", [128, DC * 512], BF16))
    kbuf = en(nc.sbuf_tensor("kbuf", [128, 2 * NT], F32))
    cosb = en(nc.sbuf_tensor("cosb", [128, NT], F32))
    sinb = en(nc.sbuf_tensor("sinb", [128, NT], F32))
    qcs = en(nc.sbuf_tensor("qcs", [128, NT], F32))
    sqsc = en(nc.sbuf_tensor("sqsc", [128, NT], BF16))
    rows2 = en(nc.sbuf_tensor("rows2", [2, NT], F32))
    rows = en(nc.sbuf_tensor("rows", [1, 2 * NT + 512], F32))
    par = {nm: en(nc.sbuf_tensor(nm + "_sb", [128, w], F32))
           for nm, w in (("ln1w", 8), ("bq", 8), ("bk", 2),
                         ("qnw", 1), ("knw", 1))}
    bvr = en(nc.sbuf_tensor("bvr_sb", [1, 256], BF16))
    e2t = en(nc.sbuf_tensor("e2t", [2, 128], F32))
    e64c = en(nc.sbuf_tensor("e64c", [128, 2], BF16))
    swp = en(nc.sbuf_tensor("swp", [128, 128], F32))
    ones_bf = en(nc.sbuf_tensor("ones_bf", [128, 1], BF16))
    ones_row = en(nc.sbuf_tensor("ones_row", [1, 128], F32))
    ones_rbf = en(nc.sbuf_tensor("ones_rbf", [1, 128], BF16))
    stg = en(nc.sbuf_tensor("stg", [128, 4 * 512], BF16))
    pb = [en(nc.psum_tensor(f"pb{i}", [128, 512], F32)) for i in range(8)]

    sLd = ctx.sem("ld")
    sLg = ctx.sem("ldg")
    sT = ctx.sem("t")
    sV = ctx.sem("v")
    sS = ctx.sem("s")
    consts = dict(ones_bf=ones_bf, ones_row=ones_row)
    stage = Stage(ctx, stg, 4, 512, "sg")

    ctx.op("gpsimd", lambda e: e.dma_start(out=ones_rbf[:],
                                           in_=io["c_ones_row"]),
           inc=(sLg, 16))
    ctx.op("sync", lambda e: e.dma_start(
        out=xbuf[:].rearrange("p (c t) -> p c t", c=DC),
        in_=io["xn1"].rearrange("(c p) t -> p c t", p=128)),
        inc=(sLd, 16))
    for nm in ("ln1w", "bq", "bk", "qnw", "knw"):
        ctx.op("sync", lambda e, nm=nm: e.dma_start(
            out=par[nm][:], in_=io[nm]), inc=(sLd, 16))
    ctx.op("sync", lambda e: e.dma_start(out=cosb[:], in_=io["c_cos"]),
           inc=(sLd, 16))
    ctx.op("sync", lambda e: e.dma_start(out=sinb[:], in_=io["c_sin"]),
           inc=(sLd, 16))
    ctx.op("sync", lambda e: e.dma_start(out=ones_row[:],
                                         in_=io["c_ones_row"]),
           inc=(sLd, 16))
    ctx.op("sync", lambda e: e.dma_start(out=e2t[:], in_=io["c_e2t"]),
           inc=(sLd, 16))
    ctx.op("sync", lambda e: e.dma_start(out=swp[:], in_=io["c_swp"]),
           inc=(sLd, 16))
    for nm, dst in (("c_ones_bf", ones_bf), ("c_e64c", e64c),
                    ("bvr", bvr)):
        ctx.op("gpsimd", lambda e, nm=nm, dst=dst: e.dma_start(
            out=dst[:], in_=io[nm]), inc=(sLg, 16))
    ctx.op("gpsimd", lambda e: e.dma_start(
        out=wkv[:, 0:DC * 256].rearrange("p (c m) -> p c m", c=DC),
        in_=io["wk"].rearrange("(c p) m -> p c m", p=128)),
        inc=(sLg, 16))
    ctx.op("gpsimd", lambda e: e.dma_start(
        out=wkv[:, DC * 256:DC * 512].rearrange("p (c m) -> p c m",
                                                c=DC),
        in_=io["wv"].rearrange("(c p) m -> p c m", p=128)),
        inc=(sLg, 16))
    ctx.op("gpsimd", lambda e: e.dma_start(
        out=wsb[:].rearrange("p (c m) -> p c m", c=DC),
        in_=io["wq"].rearrange("(c p) m -> p c m", p=128)),
        inc=(sLg, 16))

    ln_done = emit_ln_fm(ctx, consts, xbuf, hbf, rows, pb, NT, RMS_EPS,
                         pre=[("scalar", sLd, None), ("tensor", sLg,
                                                      None)],
                         tag="r1", rms_only=True)
    sW = ctx.sem("wapply")
    sHbf = ctx.sem("hbfs")
    ctx.wait("vector", ln_done)
    ctx.wait("vector", sLd)
    for c in range(DC):
        ctx.op("vector", lambda e, c=c: e.tensor_scalar_mul(
            out=xbuf[:, c * NT:(c + 1) * NT],
            in0=xbuf[:, c * NT:(c + 1) * NT],
            scalar1=par["ln1w"][:, c:c + 1]), inc=(sW, 1))
    ctx.wait("scalar", sW)
    ctx.op("scalar", lambda e: e.activation(
        out=hbf[:], in_=xbuf[:], func=AF.Identity), inc=(sHbf, 1))

    def rope_norm(buf, nchunks, wvec, hbm, tagr):
        sTx = ctx.sem("t" + tagr)
        sEx = ctx.sem("e" + tagr)
        ctx.wait("vector", sLd)
        for c in range(nchunks):
            co = c * NT
            for tt in range(NTT):
                bank = 4 + tt % 2
                if tt >= 2:
                    ctx.wait("tensor", sV)
                ctx.op("tensor", lambda e, co=co, tt=tt, buf=buf,
                       bank=bank: e.matmul(
                           pb[bank][:, 0:512], swp[:],
                           buf[:, co + tt * 512:co + tt * 512 + 512],
                           start=True, stop=True), inc=(sTx, 1))
                ctx.wait("vector", sTx)
                ctx.op("vector", lambda e, tt=tt, bank=bank:
                       e.tensor_mul(
                           out=qcs[:, tt * 512:tt * 512 + 512],
                           in0=pb[bank][:, 0:512],
                           in1=sinb[:, tt * 512:tt * 512 + 512]),
                       inc=(sV, 1))
            ctx.op("vector", lambda e, co=co, buf=buf: e.tensor_mul(
                out=buf[:, co:co + NT], in0=buf[:, co:co + NT],
                in1=cosb[:]))
            ctx.op("vector", lambda e, co=co, buf=buf: e.tensor_add(
                out=buf[:, co:co + NT], in0=buf[:, co:co + NT],
                in1=qcs[:, 0:NT]), inc=(sV, 1))
            ctx.wait("scalar", sV)
            ctx.op("scalar", lambda e, co=co, buf=buf: e.activation(
                out=sqsc[:, 0:NT], in_=buf[:, co:co + NT],
                func=AF.Square), inc=(sS, 1))
            ctx.wait("tensor", sS)
            for tt in range(NTT):
                if tt >= 1 or c >= 1:
                    ctx.wait("tensor", sV)
                ctx.op("tensor", lambda e, tt=tt: e.matmul(
                    pb[6][0:2, 0:512], e64c[:],
                    sqsc[:, tt * 512:tt * 512 + 512],
                    start=True, stop=True), inc=(sTx, 1))
                ctx.wait("vector", sTx)
                ctx.op("vector", lambda e, tt=tt: e.tensor_scalar(
                    out=rows2[0:2, tt * 512:tt * 512 + 512],
                    in0=pb[6][0:2, 0:512], scalar1=1.0 / 64.0,
                    scalar2=float(RMS_EPS), op0=ALU.mult, op1=ALU.add))
                ctx.op("vector", lambda e, tt=tt: e.reciprocal(
                    out=rows2[0:2, tt * 512:tt * 512 + 512],
                    in_=rows2[0:2, tt * 512:tt * 512 + 512]),
                    inc=(sV, 1))
            ctx.wait("scalar", sV)
            ctx.op("scalar", lambda e: e.activation(
                out=rows2[0:2, 0:NT], in_=rows2[0:2, 0:NT],
                func=AF.Sqrt), inc=(sS, 1))
            ctx.wait("tensor", sS)
            for tt in range(NTT):
                ctx.wait("tensor", sEx)
                ctx.op("tensor", lambda e, tt=tt: e.matmul(
                    pb[7][:, 0:512], e2t[:],
                    rows2[0:2, tt * 512:tt * 512 + 512],
                    start=True, stop=True), inc=(sTx, 1))
                ctx.wait("vector", sTx)
                tile, slot = stage.next("vector")
                ctx.op("vector", lambda e, co=co, tt=tt, buf=buf,
                       wvec=wvec, tile=tile: e.scalar_tensor_tensor(
                           out=tile,
                           in0=buf[:, co + tt * 512:co + tt * 512 + 512],
                           scalar=wvec[:, 0:1], in1=pb[7][:, 0:512],
                           op0=ALU.mult, op1=ALU.mult), inc=(sEx, 1))
                ctx.wait("sync", sEx)
                stage.store("sync", lambda e, c=c, tt=tt, tile=tile,
                            hbm=hbm: e.dma_start(
                                out=hbm.rearrange("(c p) t -> p c t",
                                                  p=128)
                                [:, c:c + 1, tt * 512:tt * 512 + 512],
                                in_=tile), slot)
        return sEx

    sTk = ctx.sem("tk")
    sEk = ctx.sem("ek")
    ctx.wait("tensor", sHbf)
    ctx.wait("tensor", sLg)
    for mc in range(2):
        for dc in range(DC):
            for tt in range(NTT):
                ctx.op("tensor", lambda e, mc=mc, dc=dc, tt=tt:
                       e.matmul(
                           pb[mc * NTT + tt][:, 0:512],
                           wkv[:, (dc * 2 + mc) * 128:
                               (dc * 2 + mc) * 128 + 128],
                           hbf[:, dc * NT + tt * 512:
                               dc * NT + tt * 512 + 512],
                           start=(dc == 0), stop=(dc == DC - 1)),
                       inc=(sTk, 1) if (dc == DC - 1 and tt == NTT - 1)
                       else None)
    ctx.wait("vector", sTk, 2)
    for mc in range(2):
        for tt in range(NTT):
            ctx.op("vector", lambda e, mc=mc, tt=tt: e.tensor_scalar_add(
                out=kbuf[:, mc * NT + tt * 512:mc * NT + tt * 512 + 512],
                in0=pb[mc * NTT + tt][:, 0:512],
                scalar1=par["bk"][:, mc:mc + 1]), inc=(sEk, 1))
    ctx.wait("vector", sEk)
    ctx.wait("tensor", sEk)
    sExk = rope_norm(kbuf, 2, par["knw"], io["kg"], "rk")

    sTv = ctx.sem("tv")
    sEv = ctx.sem("ev")
    for ts in range(16):
        bank = ts % 4
        if ts >= 4:
            ctx.wait("tensor", sEv, ts - 3)
        for dc in range(DC):
            ctx.op("tensor", lambda e, ts=ts, dc=dc, bank=bank:
                   e.matmul(
                       pb[bank][:, 0:256],
                       hbf[:, dc * NT + ts * 128:dc * NT + ts * 128
                           + 128],
                       wkv[:, DC * 256 + dc * 256:DC * 256 + dc * 256
                           + 256],
                       start=(dc == 0), stop=False))
        ctx.op("tensor", lambda e, ts=ts, bank=bank: e.matmul(
            pb[bank][:, 0:256], ones_rbf[0:1, :], bvr[:],
            start=False, stop=True), inc=(sTv, 1))
        ctx.wait("scalar", sTv)
        tile, slot = stage.next("scalar")
        ctx.op("scalar", lambda e, bank=bank, tile=tile: e.activation(
            out=tile[:, 0:256], in_=pb[bank][:, 0:256],
            func=AF.Identity), inc=(sEv, 1))
        ctx.wait("sync", sEv)
        stage.store("sync", lambda e, ts=ts, tile=tile: e.dma_start(
            out=io["vg"][ts * 128:ts * 128 + 128, :],
            in_=tile[:, 0:256]), slot)

    sTq = ctx.sem("tq")
    sEq = ctx.sem("eq")
    ctx.wait("tensor", sExk)
    ctx.wait("tensor", sEv)
    for mc in range(DC):
        base = (mc % 2) * NTT
        if mc >= 2:
            ctx.wait("tensor", sEq, (mc - 1) * NTT)
        for dc in range(DC):
            for tt in range(NTT):
                ctx.op("tensor", lambda e, mc=mc, dc=dc, tt=tt,
                       base=base: e.matmul(
                           pb[base + tt][:, 0:512],
                           wsb[:, (dc * DC + mc) * 128:
                               (dc * DC + mc) * 128 + 128],
                           hbf[:, dc * NT + tt * 512:
                               dc * NT + tt * 512 + 512],
                           start=(dc == 0), stop=(dc == DC - 1)),
                       inc=(sTq, 1) if (dc == DC - 1 and tt == NTT - 1)
                       else None)
        ctx.wait("vector", sTq, mc + 1)
        ctx.wait("vector", sHbf, 1)
        for tt in range(NTT):
            ctx.op("vector", lambda e, mc=mc, tt=tt, base=base:
                   e.tensor_scalar_add(
                       out=xbuf[:, mc * NT + tt * 512:
                                mc * NT + tt * 512 + 512],
                       in0=pb[base + tt][:, 0:512],
                       scalar1=par["bq"][:, mc:mc + 1]), inc=(sEq, 1))
    ctx.wait("vector", sEq)
    ctx.wait("tensor", sEq)
    rope_norm(xbuf, 8, par["qnw"], io["qg"], "rq")

    stage.final_waits("sync")


def build_launch_G1():
    return _wrap(decl_G1, emit_G1)


def stage_inputs_G1(core, W):
    r = core % 2
    rows = own_rows(r)
    C, S = rope_consts(rows)
    return dict(
        xn1=None,
        ln1w=stage_bias(W["L1_ffn_ln1_w"]),
        wq=np.ascontiguousarray(W["L1_ffn_Wq"]),
        bq=stage_bias(W["L1_ffn_bq"]),
        wk=np.ascontiguousarray(W["L1_ffn_Wk"]),
        bk=stage_bias(W["L1_ffn_bk"]),
        wv=np.ascontiguousarray(W["L1_ffn_Wv"]),
        bvr=np.ascontiguousarray(W["L1_ffn_bv"].reshape(1, 256)),
        qnw=np.ascontiguousarray(
            np.concatenate([W["L1_ffn_qn_w"], W["L1_ffn_qn_w"]]).reshape(
                128, 1)),
        knw=np.ascontiguousarray(
            np.concatenate([W["L1_ffn_kn_w"], W["L1_ffn_kn_w"]]).reshape(
                128, 1)),
        c_cos=C, c_sin=S, c_e2t=e2t_mat(), c_e64c=e64c_mat(),
        c_swp=swap_mat(),
        c_ones_bf=np.ones((128, 1), np.float32),
        c_ones_row=np.ones((1, 128), np.float32))





def decl_G2(din, dmid, dout, dbg=False):
    dmid("qg", [D, NT], BF16)
    dmid("kgf", [8 * 256, 512], BF16)
    dmid("vgf", [T, 256], BF16)
    din("wg", [64, 64])
    din("bg", [64, 1])
    din("wo", [D, D])
    din("bo", [128, 8])
    dmid("xn1", [D, NT], F32)
    din("ln2w", [128, 8])
    din("wr", [D, 8])
    din("br", [8, 1])
    din("c_maskt", [128, 32 * 512])
    din("c_cfix", [128, 1])
    din("c_ones_bf", [128, 1])
    din("c_ones_row", [1, 128])
    dout("xp", [D, NT], F32)
    if dbg:
        dout("dbg_og", [128, DC * NT], BF16)
    dout("h2b", [D, NT], BF16)
    dout("logits", [8, NT], F32)


def emit_G2(nc, ctx, io, en, dbg=False):
    xbuf = en(nc.sbuf_tensor("xbuf", [128, DC * NT], F32))
    og = en(nc.sbuf_tensor("og", [128, DC * NT], BF16))
    masks = en(nc.sbuf_tensor("masks", [128, 32 * 512],
                              mybir.dt.int8))
    kbuf = en(nc.sbuf_tensor("kbuf", [64, 4 * T], BF16))
    vaug = en(nc.sbuf_tensor("vaug", [128, 32 * 66], BF16))
    qgh = en(nc.sbuf_tensor("qgh", [64, NT], BF16))
    wsb = en(nc.sbuf_tensor("wsb", [128, DC * 1024], BF16))
    wg_sb = en(nc.sbuf_tensor("wg_sb", [64, 64], BF16))
    pbuf = en(nc.sbuf_tensor("pbuf", [128, 4 * 512], BF16))
    sqs = en(nc.sbuf_tensor("sqs", [128, NT], BF16))
    osb = en(nc.sbuf_tensor("osb", [128, 512], BF16))
    gate = en(nc.sbuf_tensor("gate", [64, 512], BF16))
    gsc = en(nc.sbuf_tensor("gsc", [64, 512], F32))
    dinv = en(nc.sbuf_tensor("dinv", [64, 512], F32))
    h2f = en(nc.sbuf_tensor("h2f", [128, 2 * 512], F32))
    stgf = en(nc.sbuf_tensor("stgf", [128, 512], F32))
    rows = en(nc.sbuf_tensor("rows", [1, NT + 64], F32))
    rrow = en(nc.sbuf_tensor("rrow", [1, 512], F32))
    par = {nm: en(nc.sbuf_tensor(nm + "_sb", [128, w], F32))
           for nm, w in (("bo", 8), ("ln2w", 8))}
    wr_sb = en(nc.sbuf_tensor("wr_sb", [128, DC * 8], F32))
    br_sb = en(nc.sbuf_tensor("br_sb", [8, 1], F32))
    bg_sb = en(nc.sbuf_tensor("bg_sb", [64, 1], F32))
    bias_col = en(nc.sbuf_tensor("bias_col", [128, 1], F32))
    ones_bf = en(nc.sbuf_tensor("ones_bf", [128, 1], BF16))
    ones_row = en(nc.sbuf_tensor("ones_row", [1, 128], F32))
    stg = en(nc.sbuf_tensor("stg", [128, 2 * 512], BF16))
    pb = [en(nc.psum_tensor(f"pb{i}", [128, 512], F32)) for i in range(8)]

    sLd = ctx.sem("ld")
    sLg = ctx.sem("ldg")
    sLvg = ctx.sem("ldvg")
    sLqg = ctx.sem("ldqg")
    sT = ctx.sem("t")
    sV = ctx.sem("v")
    sS = ctx.sem("s")
    sPk = ctx.sem("pk")
    sOF = ctx.sem("of")
    sStP = ctx.sem("stp")
    sStL = ctx.sem("stl")
    stage = Stage(ctx, stg, 2, 512, "sg")

    ctx.op("sync", lambda e: e.dma_start(
        out=xbuf[:].rearrange("p (c t) -> p c t", c=DC),
        in_=io["xn1"].rearrange("(c p) t -> p c t", p=128)),
        inc=(sLd, 16))
    for k_ in range(8):
        ctx.op("sync", lambda e, k_=k_: e.dma_start(
            out=kbuf[:].rearrange("p (g k t) -> p g k t", g=4, k=8)
            [:, :, k_, :],
            in_=io["kgf"].rearrange("(k g p) t -> k p g t", g=4, p=64)
            [k_]), inc=(sLd, 16))
    for nm in ("bo", "ln2w"):
        ctx.op("sync", lambda e, nm=nm: e.dma_start(
            out=par[nm][:], in_=io[nm]), inc=(sLd, 16))
    ctx.op("sync", lambda e: e.dma_start(out=br_sb[:], in_=io["br"]),
           inc=(sLd, 16))
    ctx.op("sync", lambda e: e.dma_start(out=bg_sb[:], in_=io["bg"]),
           inc=(sLd, 16))
    ctx.op("sync", lambda e: e.dma_start(
        out=wr_sb[:].rearrange("p (c m) -> p c m", c=DC),
        in_=io["wr"].rearrange("(c p) m -> p c m", p=128)),
        inc=(sLd, 16))
    ctx.op("sync", lambda e: e.dma_start(out=ones_row[:],
                                         in_=io["c_ones_row"]),
           inc=(sLd, 16))
    ctx.op("gpsimd", lambda e: e.dma_start(out=masks[:],
                                           in_=io["c_maskt"]),
           inc=(sLg, 16))
    ctx.op("gpsimd", lambda e: e.dma_start(out=ones_bf[:],
                                           in_=io["c_ones_bf"]),
           inc=(sLg, 16))
    ctx.op("gpsimd", lambda e: e.dma_start(out=wg_sb[:], in_=io["wg"]),
           inc=(sLg, 16))
    ctx.op("gpsimd", lambda e: e.dma_start(
        out=wsb[:].rearrange("p (c m) -> p c m", c=DC),
        in_=io["wo"].rearrange("(c p) m -> p c m", p=128)),
        inc=(sLg, 16))
    ctx.op("sync", lambda e: e.dma_start(out=bias_col[:],
                                         in_=io["c_cfix"]),
           inc=(sLd, 16))
    sMs = ctx.sem("ms")
    ctx.op("vector", lambda e: e.memset(vaug[:], 1.0), inc=(sMs, 1))

    ctx.wait("vector", sLg)
    ctx.wait("tensor", sLd)
    for g in range(4):
        ctx.wait("scalar", sMs, 1)
        ctx.wait("scalar", sPk)
        ctx.op("scalar", lambda e, g=g: e.dma_start(
            out=vaug[:].rearrange("p (kc w) -> p kc w", w=66)
            [:, :, 0:64],
            in_=io["vgf"][:, g * 64:g * 64 + 64].rearrange(
                "(kc p) c -> p kc c", p=128)), inc=(sLvg, 16))
        ctx.wait("scalar", sLd)
        ctx.wait("tensor", sLg)
        for hh in range(4):
            h = g * 4 + hh
            ctx.wait("scalar", sT)
            ctx.op("scalar", lambda e, h=h: e.dma_start(
                out=qgh[:], in_=io["qg"][h * 64:h * 64 + 64, :]),
                inc=(sLqg, 16))
            ctx.wait("scalar", sLqg)
            ctx.wait("tensor", sLqg)
            if hh == 0:
                ctx.wait("tensor", sLvg)
            for slot in range(NTT):
                qo = slot * 512
                kcmax = KCMAX[slot]
                lo = MASKLO[slot]
                base_pk = sPk.n
                SB = (0, 1, 6, 7)
                ctx.wait("scalar", sV)
                ctx.wait("tensor", sS)
                tS = {}
                eS = {}
                mV = {}

                def emit_S(kc, qo=qo, g=g):
                    sb_ = SB[kc % 4]
                    if kc >= 4:
                        ctx.wait("tensor", sS, eS[kc - 4])
                    ctx.op("tensor", lambda e, kc=kc, sb_=sb_, qo=qo,
                           g=g: e.matmul(
                               pb[sb_][:, 0:512],
                               kbuf[0:64, g * T + kc * 128:
                                    g * T + kc * 128 + 128],
                               qgh[0:64, qo:qo + 512],
                               start=True, stop=True), inc=(sT, 1))
                    tS[kc] = sT.n

                def emit_exp(kc, slot=slot, lo=lo, base_pk=base_pk):
                    if kc >= 4:
                        ctx.wait("scalar", sPk, base_pk + kc - 3)
                    elif base_pk > 0:
                        ctx.wait("scalar", sPk, base_pk)
                    ctx.wait("scalar", sT, tS[kc])
                    sb_ = SB[kc % 4]
                    ps = kc % 4
                    ctx.op("scalar", lambda e, sb_=sb_, ps=ps:
                           e.activation(
                               out=pbuf[:, ps * 512:ps * 512 + 512],
                               in_=pb[sb_][:, 0:512], func=AF.Exp,
                               bias=bias_col[:, 0:1], scale=1.0 / 8.0),
                           inc=(sS, 1))
                    eS[kc] = sS.n
                    if kc >= lo:
                        mi = slot * 8 + (kc - lo)
                        ctx.wait("vector", sS, eS[kc])
                        ctx.op("vector", lambda e, ps=ps, mi=mi:
                               e.tensor_mul(
                                   out=pbuf[:, ps * 512:ps * 512
                                            + 512],
                                   in0=pbuf[:, ps * 512:ps * 512
                                            + 512],
                                   in1=masks[:, mi * 512:mi * 512
                                             + 512]), inc=(sV, 1))
                        mV[kc] = sV.n
                    else:
                        mV[kc] = None

                for j in range(min(3, kcmax)):
                    emit_S(j)
                    emit_exp(j)
                for kc in range(kcmax):
                    if kc + 3 < kcmax:
                        emit_S(kc + 3)
                        emit_exp(kc + 3)
                    if mV[kc] is not None:
                        ctx.wait("tensor", sV, mV[kc])
                    else:
                        ctx.wait("tensor", sS, eS[kc])
                    ps = kc % 4
                    ctx.op("tensor", lambda e, kc=kc, ps=ps,
                           kcmax=kcmax: e.matmul(
                               pb[2][0:65, 0:512],
                               vaug[:, kc * 66:kc * 66 + 65],
                               pbuf[:, ps * 512:ps * 512 + 512],
                               start=(kc == 0), stop=(kc == kcmax - 1)),
                           inc=(sPk, 1))
                ctx.wait("vector", sPk)
                ctx.wait("vector", sT)
                ctx.op("vector", lambda e: e.tensor_copy(
                    out=osb[0:65, 0:512], in_=pb[2][0:65, 0:512]))
                ctx.op("vector", lambda e: e.reciprocal(
                    out=rrow[:], in_=pb[2][64:65, 0:512]), inc=(sV, 1))
                ctx.wait("tensor", sV)
                ctx.op("tensor", lambda e: e.matmul(
                    pb[3][0:64, 0:512], ones_row[0:1, 0:64], rrow[:],
                    start=True, stop=True))
                ctx.op("tensor", lambda e: e.matmul(
                    pb[4][0:64, 0:512], wg_sb[:],
                    osb[0:64, 0:512], start=True, stop=True),
                    inc=(sT, 1))
                ctx.wait("vector", sT)
                ctx.op("vector", lambda e: e.tensor_copy(
                    out=dinv[:], in_=pb[3][0:64, 0:512]))
                ctx.op("vector", lambda e: e.tensor_mul(
                    out=gsc[:], in0=pb[4][0:64, 0:512], in1=dinv[:]),
                    inc=(sV, 1))
                ctx.wait("scalar", sV)
                ctx.op("scalar", lambda e: e.activation(
                    out=gate[:], in_=gsc[:], func=AF.Sigmoid,
                    bias=bg_sb[:, 0:1], scale=1.0), inc=(sS, 1))
                ctx.wait("vector", sS)
                ctx.op("vector", lambda e: e.tensor_mul(
                    out=gate[:], in0=osb[0:64, 0:512], in1=gate[:]))
                ctx.op("vector", lambda e, h=h, qo=qo: e.tensor_mul(
                    out=og[(h % 2) * 64:(h % 2) * 64 + 64,
                           (h // 2) * NT + qo:(h // 2) * NT + qo + 512],
                    in0=gate[:], in1=dinv[:]), inc=(sOF, 1))
                ctx.wait("tensor", sOF)

    sTo = ctx.sem("to")
    sEo = ctx.sem("eo")
    sVh = ctx.sem("vh")
    ctx.wait("tensor", sOF)
    if dbg:
        sDg = ctx.sem("dbgog")
        ctx.wait("sync", sOF)
        ctx.op("sync", lambda e: e.dma_start(
            out=io["dbg_og"], in_=og[:]), inc=(sDg, 16))
        ctx.wait("sync", sDg)
    for mc in range(DC):
        base = (mc % 2) * NTT
        if mc >= 2:
            ctx.wait("tensor", sVh, (mc - 1) * NTT)
        for dc in range(DC):
            for tt in range(NTT):
                ctx.op("tensor", lambda e, mc=mc, dc=dc, tt=tt,
                       base=base: e.matmul(
                           pb[base + tt][:, 0:512],
                           wsb[:, (dc * DC + mc) * 128:
                               (dc * DC + mc) * 128 + 128],
                           og[:, dc * NT + tt * 512:
                              dc * NT + tt * 512 + 512],
                           start=(dc == 0), stop=(dc == DC - 1)),
                       inc=(sTo, 1) if (dc == DC - 1 and tt == NTT - 1)
                       else None)
        ctx.wait("scalar", sTo, mc + 1)
        for tt in range(NTT):
            nv = mc * NTT + tt
            if nv >= 1:
                ctx.wait("scalar", sVh, nv)
            ctx.op("scalar", lambda e, mc=mc, tt=tt,
                   base=base: e.activation(
                       out=stgf[:, 0:512],
                       in_=pb[base + tt][:, 0:512], func=AF.Identity,
                       bias=par["bo"][:, mc:mc + 1], scale=1.0),
                   inc=(sEo, 1))
            ctx.wait("vector", sEo)
            ctx.op("vector", lambda e, mc=mc, tt=tt:
                   e.tensor_add(
                       out=xbuf[:, mc * NT + tt * 512:
                                mc * NT + tt * 512 + 512],
                       in0=stgf[:, 0:512],
                       in1=xbuf[:, mc * NT + tt * 512:
                                mc * NT + tt * 512 + 512]),
                   inc=(sVh, 1))
    ctx.wait("sync", sVh)
    ctx.op("sync", lambda e: e.dma_start(
        out=io["xp"].rearrange("(c p) t -> p c t", p=128),
        in_=xbuf[:].rearrange("p (c t) -> p c t", c=DC)),
        inc=(sStP, 16))

    sS2 = ctx.sem("s2")
    sT2 = ctx.sem("t2")
    sV2 = ctx.sem("v2")
    ctx.wait("scalar", sVh)
    ctx.op("scalar", lambda e: e.activation(
        out=og[:], in_=xbuf[:], func=AF.Square), inc=(sS2, 1))
    ctx.wait("tensor", sS2)
    for tt in range(NTT):
        o = tt * 512
        for c in range(DC):
            ctx.op("tensor", lambda e, c=c, o=o, tt=tt: e.matmul(
                pb[tt][0:1, 0:512], ones_bf[:, 0:1],
                og[:, c * NT + o:c * NT + o + 512],
                start=(c == 0), stop=(c == DC - 1)),
                inc=(sT2, 1) if c == DC - 1 else None)
    ctx.wait("vector", sT2)
    for tt in range(NTT):
        o = tt * 512
        ctx.op("vector", lambda e, o=o, tt=tt: e.tensor_scalar(
            out=rows[0:1, o:o + 512], in0=pb[tt][0:1, 0:512],
            scalar1=1.0 / D, scalar2=float(RMS_EPS),
            op0=ALU.mult, op1=ALU.add))
        ctx.op("vector", lambda e, o=o: e.reciprocal(
            out=rows[0:1, o:o + 512], in_=rows[0:1, o:o + 512]),
            inc=(sV2, 1))
    ctx.wait("scalar", sV2)
    ctx.op("scalar", lambda e: e.activation(
        out=rows[0:1, 0:NT], in_=rows[0:1, 0:NT], func=AF.Sqrt),
        inc=(sS2, 1))
    ctx.wait("tensor", sS2)
    for tt in range(NTT):
        o = tt * 512
        if tt >= 1:
            ctx.wait("tensor", sV2)
            ctx.wait("vector", sStL)
        ctx.op("tensor", lambda e, o=o: e.matmul(
            pb[4][:, 0:512], ones_row[0:1, :], rows[0:1, o:o + 512],
            start=True, stop=True), inc=(sT2, 1))
        ctx.wait("vector", sT2)
        for c in range(DC):
            slot = c % 2
            if tt * DC + c >= 2:
                ctx.wait("vector", sT2)
                ctx.wait("vector", sS2)
            ctx.op("vector", lambda e, c=c, o=o, slot=slot:
                   e.tensor_mul(
                       out=h2f[:, slot * 512:slot * 512 + 512],
                       in0=xbuf[:, c * NT + o:c * NT + o + 512],
                       in1=pb[4][:, 0:512]))
            ctx.op("vector", lambda e, c=c, slot=slot:
                   e.tensor_scalar_mul(
                       out=h2f[:, slot * 512:slot * 512 + 512],
                       in0=h2f[:, slot * 512:slot * 512 + 512],
                       scalar1=par["ln2w"][:, c:c + 1]), inc=(sV2, 1))
            ctx.wait("scalar", sV2)
            tile2, slot2 = stage.next("scalar")
            ctx.op("scalar", lambda e, tile2=tile2, slot=slot:
                   e.activation(
                       out=tile2,
                       in_=h2f[:, slot * 512:slot * 512 + 512],
                       func=AF.Identity), inc=(sS2, 1))
            ctx.wait("sync", sS2)
            stage.store("sync", lambda e, tile2=tile2, c=c, o=o:
                        e.dma_start(
                            out=io["h2b"].rearrange(
                                "(c p) t -> p c t", p=128)
                            [:, c:c + 1, o:o + 512], in_=tile2),
                        slot2)
            ctx.wait("tensor", sV2)
            ctx.op("tensor", lambda e, c=c, slot=slot: e.matmul(
                pb[5][0:8, 0:512], wr_sb[:, c * 8:c * 8 + 8],
                h2f[:, slot * 512:slot * 512 + 512],
                start=(c == 0), stop=(c == DC - 1)),
                inc=(sT2, 1))
        ctx.wait("vector", sT2)
        ctx.op("vector", lambda e, o=o, tt=tt: e.tensor_scalar_add(
            out=rrow[0:1, 0:512], in0=pb[5][0:8, 0:512],
            scalar1=br_sb[:]), inc=(sV2, 1))
        ctx.rec["vector"].pop()
        sV2.n -= 1
        ctx.op("vector", lambda e: e.tensor_scalar_add(
            out=h2f[0:8, 512:1024], in0=pb[5][0:8, 0:512],
            scalar1=br_sb[:]), inc=(sV2, 1))
        ctx.wait("sync", sV2)
        ctx.op("sync", lambda e, o=o: e.dma_start(
            out=io["logits"][:, o:o + 512], in_=h2f[0:8, 512:1024]),
            inc=(sStL, 16))
    ctx.wait("sync", sStP)
    ctx.wait("sync", sStL)
    stage.final_waits("sync")


def build_launch_G2(dbg=False):
    return _wrap(
        lambda di, dm, do: decl_G2(di, dm, do, dbg=dbg),
        lambda nc, ctx, io, en: emit_G2(nc, ctx, io, en, dbg=dbg))





def selg_mat():
    Smat = np.zeros((16, 16 * 128), np.float32)
    for g in range(16):
        Smat[g, g * 128:(g + 1) * 128] = 1.0
    return Smat


def decl_H1(din, dmid, dout):
    dmid("h2b", [D, NT], BF16)
    dmid("logits", [8, NT], F32)
    din("b1s", [128, 8 * 32])
    din("w1", [8, D, 4096])
    din("w2", [8, 2048, D])
    din("b2r", [1, 8192])
    din("c_selg", [16, 16 * 128])
    din("c_id8", [8, 8])
    din("c_idL", [128, 128])
    dout("y", [D, NT], BF16)


def emit_H1(nc, ctx, io, en):
    h2 = en(nc.sbuf_tensor("h2", [128, DC * NT], BF16))
    mb = en(nc.sbuf_tensor("mb", [128, 8 * NT], BF16))
    yacc = en(nc.sbuf_tensor("yacc", [128, DC * NT], BF16))
    actb = en(nc.sbuf_tensor("actb", [128, 16 * 512], BF16))
    w1a = en(nc.sbuf_tensor("w1a", [128, 2 * DC * 512], BF16))
    w1b = en(nc.sbuf_tensor("w1b", [128, 2 * DC * 512], BF16))
    w2s = en(nc.sbuf_tensor("w2s", [128, 2 * 4 * 1024], BF16))
    sgb = en(nc.sbuf_tensor("sgb", [128, 2 * 512], BF16))
    sgm = en(nc.sbuf_tensor("sgm", [128, 2 * 512], BF16))
    lbuf = en(nc.sbuf_tensor("lbuf", [8, NT], F32))
    ltm = en(nc.sbuf_tensor("ltm", [128, 128], F32))
    mxb = en(nc.sbuf_tensor("mxb", [128, 16], F32))
    geb = en(nc.sbuf_tensor("geb", [128, 128], F32))
    runb = en(nc.sbuf_tensor("runb", [128, 16], F32))
    met = en(nc.sbuf_tensor("met", [128, 128], F32))
    tscr = en(nc.sbuf_tensor("tscr", [128, 16], F32))
    mrow = en(nc.sbuf_tensor("mrow", [16, 8 * 128], BF16))
    selg = en(nc.sbuf_tensor("selg", [16, 16 * 128], BF16))
    id8 = en(nc.sbuf_tensor("id8", [8, 8], F32))
    idL = en(nc.sbuf_tensor("idL", [128, 128], F32))
    b1s = en(nc.sbuf_tensor("b1s_sb", [128, 8 * 32], F32))
    b2r = en(nc.sbuf_tensor("b2r_sb", [1, 8192], BF16))
    pb = [en(nc.psum_tensor(f"pb{i}", [128, 512], F32)) for i in range(8)]

    sLd = ctx.sem("ld")
    sLg = ctx.sem("ldg")
    sT = ctx.sem("t")
    sV = ctx.sem("v")
    sS = ctx.sem("s")
    sAct = ctx.sem("act")
    sEY = ctx.sem("ey")
    sSt = ctx.sem("st")

    ctx.op("sync", lambda e: e.dma_start(
        out=h2[:].rearrange("p (c t) -> p c t", c=DC),
        in_=io["h2b"].rearrange("(c p) t -> p c t", p=128)),
        inc=(sLd, 16))
    ctx.op("sync", lambda e: e.dma_start(out=lbuf[:], in_=io["logits"]),
           inc=(sLd, 16))
    ctx.op("sync", lambda e: e.dma_start(out=b1s[:], in_=io["b1s"]),
           inc=(sLd, 16))
    ctx.op("sync", lambda e: e.dma_start(out=id8[:], in_=io["c_id8"]),
           inc=(sLd, 16))
    ctx.op("sync", lambda e: e.dma_start(out=idL[:], in_=io["c_idL"]),
           inc=(sLd, 16))
    ctx.op("gpsimd", lambda e: e.dma_start(out=b2r[:], in_=io["b2r"]),
           inc=(sLg, 16))
    ctx.op("gpsimd", lambda e: e.dma_start(out=selg[:],
                                           in_=io["c_selg"]),
           inc=(sLg, 16))

    ctx.wait("tensor", sLd)
    for tg in range(16):
        ctx.op("tensor", lambda e, tg=tg: e.transpose(
            pb[0][:, tg * 8:tg * 8 + 8], lbuf[0:8, tg * 128:tg * 128
                                              + 128], id8[:]),
            inc=(sT, 1) if tg == 15 else None)
    ctx.wait("vector", sT)
    ctx.op("vector", lambda e: e.tensor_copy(out=ltm[:],
                                             in_=pb[0][:, 0:128]),
           inc=(sV, 1))
    ctx.wait("vector", sV)
    ctx.op("vector", lambda e: e.reduce_max(
        out=mxb[:], in_=ltm[:].rearrange("p (g e) -> p g e", e=8),
        axis=AX.X), inc=(sV, 1))
    ctx.wait("vector", sV)
    mxv = mxb[:].rearrange("p (g a) -> p g a", a=1).broadcast_to(
        [128, 16, 8])
    ctx.op("vector", lambda e, mxv=mxv: e.tensor_tensor(
        out=geb[:].rearrange("p (g e) -> p g e", e=8),
        in0=ltm[:].rearrange("p (g e) -> p g e", e=8), in1=mxv,
        op=ALU.is_ge), inc=(sV, 1))
    ctx.wait("vector", sV)
    ctx.op("vector", lambda e: e.memset(runb[:], 0.0), inc=(sV, 1))
    for e_ in range(8):
        gev = geb[:].rearrange("p (g e) -> p g e", e=8)[:, :, e_]
        mev = met[:].rearrange("p (g e) -> p g e", e=8)[:, :, e_]
        ctx.wait("vector", sV)
        ctx.op("vector", lambda e, gev=gev: e.tensor_mul(
            out=tscr[:], in0=gev, in1=runb[:]), inc=(sV, 1))
        ctx.wait("vector", sV)
        ctx.op("vector", lambda e, gev=gev, mev=mev: e.tensor_sub(
            out=mev, in0=gev, in1=tscr[:]), inc=(sV, 1))
        ctx.wait("vector", sV)
        ctx.op("vector", lambda e, mev=mev: e.tensor_add(
            out=runb[:], in0=runb[:], in1=mev), inc=(sV, 1))
    ctx.wait("tensor", sV)
    base_v = sV.n
    for e_ in range(8):
        bank = 1 + e_ % 2
        if e_ >= 2:
            ctx.wait("tensor", sV, base_v + e_ - 1)
        ctx.op("tensor", lambda e, e_=e_, bank=bank: e.transpose(
            pb[bank][0:16, 0:128],
            met[:].rearrange("p (g e) -> p g e", e=8)[:, :, e_],
            idL[:]), inc=(sT, 1))
        ctx.wait("vector", sT)
        ctx.op("vector", lambda e, e_=e_, bank=bank: e.tensor_copy(
            out=mrow[0:16, e_ * 128:e_ * 128 + 128],
            in_=pb[bank][0:16, 0:128]), inc=(sV, 1))
    ctx.wait("tensor", sV)
    ctx.wait("tensor", sLg)
    sMb = ctx.sem("mbv")
    for e_ in range(8):
        for gq in range(4):
            bank = 3 + gq % 2
            if e_ * 4 + gq >= 2:
                ctx.wait("tensor", sMb, e_ * 4 + gq - 1)
            for gi in range(4):
                g = gq * 4 + gi
                ctx.op("tensor", lambda e, e_=e_, g=g, gi=gi, bank=bank:
                       e.matmul(
                           pb[bank][:, gi * 128:gi * 128 + 128],
                           selg[:, g * 128:g * 128 + 128],
                           mrow[0:16, e_ * 128:e_ * 128 + 128],
                           start=True, stop=True),
                       inc=(sT, 1) if gi == 3 else None)
            ctx.wait("vector", sT)
            ctx.op("vector", lambda e, e_=e_, gq=gq, bank=bank:
                   e.tensor_copy(
                       out=mb[:, e_ * NT + gq * 512:e_ * NT + gq * 512
                              + 512],
                       in_=pb[bank][:, 0:512]), inc=(sMb, 1))

    ctx.wait("scalar", sLd)
    ctx.wait("vector", sLd)
    ctx.op("vector", lambda e: e.memset(yacc[:], 0.0), inc=(sV, 1))
    wld = {"a": DBuf(ctx, 2, "wla"), "b": DBuf(ctx, 2, "wlb"),
           "2": DBuf(ctx, 2, "wl2")}
    for e_ in range(8):
        for cp in range(4):
            it = e_ * 4 + cp
            half = (it % 2)
            if it >= 2:
                ctx.wait("gpsimd", sT)
            ctx.op("gpsimd", lambda e, e_=e_, cp=cp, half=half:
                   e.dma_start(
                       out=w1a[:, half * DC * 512:(half + 1) * DC * 512]
                       .rearrange("p (c m) -> p c m", c=DC),
                       in_=io["w1"][e_, :, cp * 512:cp * 512 + 512]
                       .rearrange("(c p) m -> p c m", p=128)),
                   inc=(wld["a"].sems[half], 16))
            wld["a"].idx += 1
            ctx.op("gpsimd", lambda e, e_=e_, cp=cp, half=half:
                   e.dma_start(
                       out=w1b[:, half * DC * 512:(half + 1) * DC * 512]
                       .rearrange("p (c m) -> p c m", c=DC),
                       in_=io["w1"][e_, :, 2048 + cp * 512:2048 + cp
                                    * 512 + 512]
                       .rearrange("(c p) m -> p c m", p=128)),
                   inc=(wld["b"].sems[half], 16))
            wld["b"].idx += 1
            ctx.op("gpsimd", lambda e, e_=e_, cp=cp, half=half:
                   e.dma_start(
                       out=w2s[:, half * 4096:(half + 1) * 4096]
                       .rearrange("p (r m) -> p r m", r=4),
                       in_=io["w2"][e_, cp * 512:cp * 512 + 512, :]
                       .rearrange("(r p) m -> p r m", p=128)),
                   inc=(wld["2"].sems[half], 16))
            wld["2"].idx += 1
            ctx.wait("tensor", wld["a"].sems[half])
            ctx.wait("tensor", wld["b"].sems[half])
            act_base = sAct.n
            v_base = sV.n
            for uc in range(4):
                for tt in range(NTT):
                    ui = uc * 4 + tt
                    bk1 = ui % 2
                    bk2 = 2 + ui % 2
                    if ui >= 2:
                        ctx.wait("tensor", sAct, act_base + ui - 1)
                    for dc in range(DC):
                        ctx.op("tensor", lambda e, dc=dc, uc=uc, tt=tt,
                               half=half, bk1=bk1: e.matmul(
                                   pb[bk1][:, 0:512],
                                   w1a[:, half * DC * 512 + dc * 512
                                       + uc * 128:half * DC * 512
                                       + dc * 512 + uc * 128 + 128],
                                   h2[:, dc * NT + tt * 512:
                                      dc * NT + tt * 512 + 512],
                                   start=(dc == 0), stop=(dc == DC - 1)))
                    for dc in range(DC):
                        ctx.op("tensor", lambda e, dc=dc, uc=uc, tt=tt,
                               half=half, bk2=bk2: e.matmul(
                                   pb[bk2][:, 0:512],
                                   w1b[:, half * DC * 512 + dc * 512
                                       + uc * 128:half * DC * 512
                                       + dc * 512 + uc * 128 + 128],
                                   h2[:, dc * NT + tt * 512:
                                      dc * NT + tt * 512 + 512],
                                   start=(dc == 0),
                                   stop=(dc == DC - 1)),
                               inc=(sT, 1) if dc == DC - 1 else None)
                    ctx.wait("scalar", sT)
                    if ui >= 2:
                        ctx.wait("scalar", sV, v_base + ui - 1)
                    ch1 = e_ * 32 + cp * 4 + uc
                    ch2 = e_ * 32 + 16 + cp * 4 + uc
                    ctx.op("scalar", lambda e, bk2=bk2, ui=ui, ch2=ch2:
                           e.activation(
                               out=sgb[:, (ui % 2) * 512:(ui % 2) * 512
                                       + 512],
                               in_=pb[bk2][:, 0:512], func=AF.Sigmoid,
                               bias=b1s[:, ch2:ch2 + 1], scale=1.0),
                           inc=(sS, 1))
                    ctx.wait("vector", sS)
                    ctx.op("vector", lambda e, ui=ui, e_=e_, tt=tt:
                           e.tensor_mul(
                               out=sgm[:, (ui % 2) * 512:(ui % 2) * 512
                                       + 512],
                               in0=sgb[:, (ui % 2) * 512:(ui % 2) * 512
                                       + 512],
                               in1=mb[:, e_ * NT + tt * 512:
                                      e_ * NT + tt * 512 + 512]),
                           inc=(sV, 1))
                    ctx.wait("vector", sV)
                    ctx.op("vector", lambda e, ui=ui, bk1=bk1, ch1=ch1:
                           e.scalar_tensor_tensor(
                               out=actb[:, ui * 512:ui * 512 + 512],
                               in0=pb[bk1][:, 0:512],
                               scalar=b1s[:, ch1:ch1 + 1],
                               in1=sgm[:, (ui % 2) * 512:(ui % 2) * 512
                                       + 512],
                               op0=ALU.add, op1=ALU.mult),
                           inc=(sAct, 1))
            ctx.wait("tensor", wld["2"].sems[half])
            ctx.wait("tensor", sAct)
            ey_base = sEY.n
            for dc in range(DC):
                for tt in range(NTT):
                    wi = dc * 4 + tt
                    bank = 4 + wi % 4
                    if wi >= 4:
                        ctx.wait("tensor", sEY, ey_base + wi - 3)
                    for rr in range(4):
                        last = (rr == 3 and cp != 0)
                        ctx.op("tensor", lambda e, dc=dc, tt=tt, rr=rr,
                               half=half, bank=bank, last=last:
                               e.matmul(
                                   pb[bank][:, 0:512],
                                   w2s[:, half * 4096 + rr * 1024
                                       + dc * 128:half * 4096
                                       + rr * 1024 + dc * 128 + 128],
                                   actb[:, (rr * 4 + tt) * 512:
                                        (rr * 4 + tt) * 512 + 512],
                                   start=(rr == 0), stop=last),
                               inc=(sT, 1) if last else None)
                    if cp == 0:
                        ctx.op("tensor", lambda e, e_=e_, dc=dc, tt=tt,
                               bank=bank: e.matmul(
                                   pb[bank][:, 0:512],
                                   b2r[0:1, e_ * 1024 + dc * 128:
                                       e_ * 1024 + dc * 128 + 128],
                                   mb[0:1, e_ * NT + tt * 512:
                                      e_ * NT + tt * 512 + 512],
                                   start=False, stop=True),
                               inc=(sT, 1))
                    ctx.wait("vector", sT)
                    ctx.op("vector", lambda e, dc=dc, tt=tt, bank=bank:
                           e.tensor_add(
                               out=yacc[:, dc * NT + tt * 512:
                                        dc * NT + tt * 512 + 512],
                               in0=pb[bank][:, 0:512],
                               in1=yacc[:, dc * NT + tt * 512:
                                        dc * NT + tt * 512 + 512]),
                           inc=(sEY, 1))

    ctx.wait("sync", sEY)
    ctx.op("sync", lambda e: e.dma_start(
        out=io["y"].rearrange("(c p) t -> p c t", p=128),
        in_=yacc[:].rearrange("p (c t) -> p c t", c=DC)),
        inc=(sSt, 16))
    ctx.wait("sync", sSt)


def build_launch_H1():
    return _wrap(decl_H1, emit_H1)





def decl_H2(din, dmid, dout):
    dmid("x1", [D, NT], F32)
    dmid("xp", [D, NT], F32)
    dmid("y", [D, NT], BF16)
    dmid("s1", [D, NT], BF16)
    dout("outp", [D, NT], F32)


def emit_H2(nc, ctx, io, en):
    x1b = en(nc.sbuf_tensor("x1b", [128, DC * NT], F32))
    xpb = en(nc.sbuf_tensor("xpb", [128, DC * NT], F32))
    yb = en(nc.sbuf_tensor("yb", [128, DC * NT], BF16))
    s1b = en(nc.sbuf_tensor("s1b", [128, DC * NT], BF16))
    sLd = ctx.sem("ld")
    sV = ctx.sem("v")
    sS = ctx.sem("s")
    sSt = ctx.sem("st")
    for nm, dst in (("x1", x1b), ("xp", xpb)):
        ctx.op("sync", lambda e, nm=nm, dst=dst: e.dma_start(
            out=dst[:].rearrange("p (c t) -> p c t", c=DC),
            in_=io[nm].rearrange("(c p) t -> p c t", p=128)),
            inc=(sLd, 16))
    for nm, dst in (("y", yb), ("s1", s1b)):
        ctx.op("sync", lambda e, nm=nm, dst=dst: e.dma_start(
            out=dst[:].rearrange("p (c t) -> p c t", c=DC),
            in_=io[nm].rearrange("(c p) t -> p c t", p=128)),
            inc=(sLd, 16))
    ctx.wait("vector", sLd)
    ctx.op("vector", lambda e: e.tensor_add(
        out=xpb[:], in0=xpb[:], in1=yb[:]), inc=(sV, 1))
    ctx.wait("scalar", sV)
    ctx.op("scalar", lambda e: e.activation(
        out=yb[:], in_=xpb[:], func=AF.Silu), inc=(sS, 1))
    ctx.wait("vector", sS)
    ctx.op("vector", lambda e: e.tensor_mul(
        out=xpb[:], in0=s1b[:], in1=yb[:]))
    ctx.op("vector", lambda e: e.tensor_add(
        out=x1b[:], in0=x1b[:], in1=xpb[:]), inc=(sV, 1))
    ctx.wait("sync", sV)
    ctx.op("sync", lambda e: e.dma_start(
        out=io["outp"].rearrange("(c p) t -> p c t", p=128),
        in_=x1b[:].rearrange("p (c t) -> p c t", c=DC)),
        inc=(sSt, 16))
    ctx.wait("sync", sSt)


def build_launch_H2():
    return _wrap(decl_H2, emit_H2)



def exchange_indices(core):
    r = core % 2
    KC = OWN[r]
    ix = np.zeros((128, 8), np.int32)
    for sidx in range(128):
        ix[sidx, 0] = KC[sidx // 32] * 32 + (sidx % 32)
        ix[sidx, 1] = (sidx // 16) * 32 + r * 16 + (sidx % 16)
    for b in range(256):
        ix[b % 128, 2 + b // 128] = KC[b // 64] * 64 + (b % 64)
        ix[b % 128, 4 + b // 128] = KC[b // 64] * 64 + (b % 64)
    for sidx in range(32):
        ix[sidx, 6] = KC[sidx // 8] * 8 + (sidx % 8)
    for sidx in range(64):
        ix[sidx, 7] = KC[sidx // 16] * 16 + (sidx % 16)
    ist = (np.arange(16) + r * 16).astype(np.int32).reshape(16, 1)
    return dict(x_ix=ix, x_idx_st=ist)


def _xc_stamp_poll(ctx, xs, sno, st_ap):
    sfl = ctx.sem("xfl")
    non = xs["non"]
    ctx.op("gpsimd", lambda e: e.indirect_dma_start(
        out=st_ap, out_offset=IndirectOffsetOnAxis(xs["ist"][:], 0),
        in_=non[:, sno * 16:(sno + 1) * 16], in_offset=None),
        inc=(sfl, 16))
    ctx.wait("gpsimd", sfl)
    flags = xs["flags"]
    poll_h = xs["polls"][sno]

    def poll(e):
        pf = f"x{sno}_"
        with e.register(pf + "cond") as cond, \
                e.register(pf + "cnt") as cnt, \
                e.register(pf + "rM") as rM, e.register(pf + "r0") as r0, \
                e.register(pf + "a") as a, e.register(pf + "bb") as bb, \
                e.register(pf + "acc") as acc:
            e.reg_load(rM, non[0:1, sno * 16:sno * 16 + 1])
            e.reg_mov(cond, 1)
            e.reg_mov(cnt, 0)
            with e.While(cond):
                e.reg_add(cnt, cnt, 16)
                e.dma_start(
                    out=flags[:],
                    in_=st_ap.rearrange("(a q) c -> a q c", q=16)[:, 0, :]
                ).then_inc(poll_h, 16)
                e.wait_ge(poll_h, cnt)
                e.reg_mov(acc, 1)
                for pp in range(2):
                    for w in range(2):
                        e.reg_load(r0, flags[pp:pp + 1, w:w + 1])
                        e.reg_alu(a, r0, rM, ALU.is_ge)
                        e.reg_alu(bb, rM, r0, ALU.is_ge)
                        e.reg_alu(a, a, bb, ALU.mult)
                        e.reg_alu(acc, acc, a, ALU.mult)
                e.reg_alu(cond, acc, 1, ALU.is_lt)
    ctx.op("gpsimd", poll)


def _xc_load_fm(ctx, buf, priv, ncp, sld):
    ng = ncp * 32 // 4
    for i in range(4):
        ctx.op("sync", lambda e, i=i: e.dma_start(
            out=buf[i * ng:(i + 1) * ng, 0:16384].rearrange(
                "s (q t) -> s q t", q=32),
            in_=priv.rearrange("(g q) (i t) -> g i q t", q=32, i=4)
            [:, i, :, :]), inc=(sld, 16))


def _xc_scatter(ctx, xs, buf, sh_view, col, n, ssc, half=0):
    ctx.op("gpsimd", lambda e: e.indirect_dma_start(
        out=sh_view,
        out_offset=IndirectOffsetOnAxis(xs["ixs"][0:n, col:col + 1], 0),
        in_=buf[0:n, half * 16384:half * 16384 + 16384],
        in_offset=None), inc=(ssc, 16))


def _v32(sh):
    return sh.rearrange("(k g q) t -> (k g) (q t)", g=32, q=32)


def emit_xchg1(nc, ctx, en, xs, P, SH):
    bv = en(nc.sbuf_tensor("bv", [128, 16384], BF16))
    bk = en(nc.sbuf_tensor("bk", [128, 16384], BF16))
    bfv = en(nc.sbuf_tensor("bfv", [128, 16384], BF16))
    bg = en(nc.sbuf_tensor("bg", [128, 16384], BF16))
    sld = ctx.sem("xld")
    ssc = ctx.sem("xsc")
    sga = ctx.sem("xga")
    sst = ctx.sem("xst")
    ctx.wait("sync", xs["sldi"])
    ctx.wait("gpsimd", xs["sldi"])
    _xc_load_fm(ctx, bv, P["v"], 4, sld)
    _xc_load_fm(ctx, bk, P["fk"], 4, sld)
    ctx.op("sync", lambda e: e.dma_start(
        out=bfv[:], in_=P["fv"].rearrange("(s q) d -> s (q d)", q=16)),
        inc=(sld, 16))
    ctx.wait("gpsimd", sld)
    _xc_scatter(ctx, xs, bv, _v32(SH["sh_v"]), 0, 128, ssc)
    _xc_scatter(ctx, xs, bk, _v32(SH["sh_fk"]), 0, 128, ssc)
    _xc_scatter(ctx, xs, bfv,
                SH["sh_fv"].rearrange("(s q) d -> s (q d)", q=16),
                0, 128, ssc)
    ctx.wait("gpsimd", ssc)
    _xc_stamp_poll(ctx, xs, 0, SH["st0"])
    ctx.op("gpsimd", lambda e: e.indirect_dma_start(
        out=bg[:], out_offset=None, in_=_v32(SH["sh_v"]),
        in_offset=IndirectOffsetOnAxis(xs["ixs"][:, 1:2], 0)),
        inc=(sga, 16))
    ctx.wait("sync", sga)
    for k in range(8):
        ctx.op("sync", lambda e, k=k: e.dma_start(
            out=P["v_ch"].rearrange("(g q) (k t) -> k g q t",
                                    q=32, k=8)[k],
            in_=bg[k * 16:(k + 1) * 16, :].rearrange(
                "s (q t) -> s q t", q=32)), inc=(sst, 16))
    ctx.wait("sync", sst)


def emit_xchg2(nc, ctx, en, xs, P, SH):
    bsy = en(nc.sbuf_tensor("bsy", [128, 16384], BF16))
    bg = en(nc.sbuf_tensor("bg", [128, 16384], BF16))
    sld = ctx.sem("xld")
    ssc = ctx.sem("xsc")
    sga = ctx.sem("xga")
    sst = ctx.sem("xst")
    for k in range(8):
        ctx.op("sync", lambda e, k=k: e.dma_start(
            out=bsy[k * 16:(k + 1) * 16, :].rearrange(
                "s (q t) -> s q t", q=32),
            in_=P["sy"].rearrange("(g q) (k t) -> g k q t", q=32, k=8)
            [:, k, :, :]), inc=(sld, 16))
    ctx.wait("gpsimd", sld)
    _xc_scatter(ctx, xs, bsy, _v32(SH["sh_sy"]), 1, 128, ssc)
    ctx.wait("gpsimd", ssc)
    _xc_stamp_poll(ctx, xs, 1, SH["st1"])
    ctx.op("gpsimd", lambda e: e.indirect_dma_start(
        out=bg[:], out_offset=None, in_=_v32(SH["sh_sy"]),
        in_offset=IndirectOffsetOnAxis(xs["ixs"][:, 0:1], 0)),
        inc=(sga, 16))
    ctx.wait("sync", sga)
    for i in range(4):
        ctx.op("sync", lambda e, i=i: e.dma_start(
            out=P["syr"].rearrange("(g q) (i t) -> g i q t",
                                   q=32, i=4)[:, i, :, :],
            in_=bg[i * 32:(i + 1) * 32, :].rearrange(
                "s (q t) -> s q t", q=32)), inc=(sst, 16))
    ctx.wait("sync", sst)


def emit_xchg3(nc, ctx, en, xs, P, SH):
    bK = en(nc.sbuf_tensor("bK", [128, 2 * 16384], BF16))
    bV = en(nc.sbuf_tensor("bV", [128, 2 * 16384], BF16))
    sld = ctx.sem("xld")
    ssc = ctx.sem("xsc")
    for i in range(4):
        ctx.op("sync", lambda e, i=i: e.dma_start(
            out=bK[(i % 2) * 64:(i % 2) * 64 + 64,
                   (i // 2) * 16384:(i // 2) * 16384 + 16384].rearrange(
                "s (q t) -> s q t", q=32),
            in_=P["Kr"].rearrange("(g q) (i t) -> g i q t", q=32, i=4)
            [:, i, :, :]), inc=(sld, 16))
    for hf in range(2):
        ctx.op("sync", lambda e, hf=hf: e.dma_start(
            out=bV[:, hf * 16384:hf * 16384 + 16384],
            in_=P["Vr"].rearrange("(s q) d -> s (q d)", q=8)
            [hf * 128:hf * 128 + 128, :]), inc=(sld, 16))
    ctx.wait("gpsimd", sld)
    shK = SH["sh_K"].rearrange("(k g q) t -> (k g) (q t)", g=64, q=32)
    shV = SH["sh_V"].rearrange("(s q) d -> s (q d)", q=8)
    _xc_scatter(ctx, xs, bK, shK, 4, 128, ssc, half=0)
    _xc_scatter(ctx, xs, bK, shK, 5, 128, ssc, half=1)
    _xc_scatter(ctx, xs, bV, shV, 2, 128, ssc, half=0)
    _xc_scatter(ctx, xs, bV, shV, 3, 128, ssc, half=1)
    ctx.wait("gpsimd", ssc)
    _xc_stamp_poll(ctx, xs, 2, SH["st2"])


def emit_xchg4(nc, ctx, en, xs, P, SH):
    bkg = en(nc.sbuf_tensor("bkg", [32, 16384], BF16))
    bvg = en(nc.sbuf_tensor("bvg", [64, 8192], BF16))
    sld = ctx.sem("xld")
    ssc = ctx.sem("xsc")
    for i in range(4):
        ctx.op("sync", lambda e, i=i: e.dma_start(
            out=bkg[i * 8:(i + 1) * 8, :].rearrange(
                "s (q t) -> s q t", q=32),
            in_=P["kg"].rearrange("(g q) (i t) -> g i q t", q=32, i=4)
            [:, i, :, :]), inc=(sld, 16))
    ctx.op("sync", lambda e: e.dma_start(
        out=bvg[:], in_=P["vg"].rearrange("(s q) d -> s (q d)", q=32)),
        inc=(sld, 16))
    ctx.wait("gpsimd", sld)
    shkg = SH["sh_kg"].rearrange("(k g q) t -> (k g) (q t)", g=8, q=32)
    shvg = SH["sh_vg"].rearrange("(s q) d -> s (q d)", q=32)
    ctx.op("gpsimd", lambda e: e.indirect_dma_start(
        out=shkg,
        out_offset=IndirectOffsetOnAxis(xs["ixs"][0:32, 6:7], 0),
        in_=bkg[:], in_offset=None), inc=(ssc, 16))
    ctx.op("gpsimd", lambda e: e.indirect_dma_start(
        out=shvg,
        out_offset=IndirectOffsetOnAxis(xs["ixs"][0:64, 7:8], 0),
        in_=bvg[:], in_offset=None), inc=(ssc, 16))
    ctx.wait("gpsimd", ssc)
    _xc_stamp_poll(ctx, xs, 3, SH["st3"])


def build_fused():
    nc = bass.Bass("TRN2", target_bir_lowering=False)
    ios = {}

    def din_for(ph):
        io = ios.setdefault(ph, {})

        def din(name, shape, dtype=F32):
            io[name] = nc.dram_tensor(ph + "_" + name, shape, dtype,
                                      kind="ExternalInput").ap()
        return din

    def route_for(ph, table):
        io = ios.setdefault(ph, {})

        def r(name, shape, dtype=F32):
            io[name] = table[name]
        return r

    def P_(name, shape, dtype):
        return nc.dram_tensor("p_" + name, shape, dtype,
                              kind="Internal").ap()

    def SH_(name, shape, dtype):
        return nc.dram_tensor(name, shape, dtype, kind="Internal",
                              addr_space="Shared").ap()

    P = {
        "xn0": P_("xn0", [D, NT], F32), "u": P_("u", [D, NT], BF16),
        "v": P_("v", [D, NT], BF16), "fq": P_("fq", [D, NT], BF16),
        "fk": P_("fk", [D, NT], BF16), "fv": P_("fv", [NT, D], BF16),
        "v_ch": P_("v_ch", [512, T], BF16), "sy": P_("sy", [512, T], BF16),
        "syr": P_("syr", [D, NT], BF16), "g": P_("g", [D, NT], BF16),
        "s0": P_("s0", [D, NT], F32), "x1": P_("x1", [D, NT], F32),
        "xn1": P_("xn1", [D, NT], F32),
        "xn1_bf": P_("xn1_bf", [D, NT], BF16),
        "Q": P_("Q", [2 * D, NT], BF16), "Kr": P_("Kr", [2 * D, NT], BF16),
        "Vr": P_("Vr", [NT, 2 * D], BF16), "s1": P_("s1", [D, NT], BF16),
        "qg": P_("qg", [D, NT], BF16), "kg": P_("kg", [256, NT], BF16),
        "vg": P_("vg", [NT, 256], BF16), "xp": P_("xp", [D, NT], F32),
        "h2b": P_("h2b", [D, NT], BF16),
        "logits": P_("logits", [8, NT], F32), "y": P_("y", [D, NT], BF16),
    }
    SH = {
        "sh_v": SH_("sh_v", [8 * D, 512], BF16),
        "sh_fk": SH_("sh_fk", [8 * D, 512], BF16),
        "sh_fv": SH_("sh_fv", [T, D], BF16),
        "sh_sy": SH_("sh_sy", [8 * D, 512], BF16),
        "sh_K": SH_("sh_K", [16 * D, 512], BF16),
        "sh_V": SH_("sh_V", [T, 2 * D], BF16),
        "sh_kg": SH_("sh_kg", [8 * 256, 512], BF16),
        "sh_vg": SH_("sh_vg", [T, 256], BF16),
    }
    for i in range(4):
        SH[f"st{i}"] = SH_(f"st{i}", [32, 16], I32)
    outp = nc.dram_tensor("outp", [D, NT], F32, kind="ExternalOutput").ap()

    xin = {}
    for nm, shp in (("x_ix", [128, 8]),
                    ("x_idx_st", [16, 1]), ("x_nonce", [16, 64])):
        xin[nm] = nc.dram_tensor(nm, shp, I32, kind="ExternalInput").ap()

    decl_A(din_for("A"), None, route_for("A", P))
    decl_B(din_for("B"), route_for("B", P), route_for("B", P))
    decl_C(din_for("C"),
           route_for("C", {"fq": P["fq"], "kf": SH["sh_fk"],
                           "vf": SH["sh_fv"]}),
           route_for("C", P))
    decl_D1(din_for("D1"), route_for("D1", P), route_for("D1", P))
    decl_D2(din_for("D2"), route_for("D2", P), route_for("D2", P))
    decl_E(din_for("E"), route_for("E", P), route_for("E", P))
    decl_F(din_for("F"),
           route_for("F", {"Q": P["Q"], "Kf": SH["sh_K"],
                           "Vf": SH["sh_V"]}),
           route_for("F", P))
    decl_G1(din_for("G1"), route_for("G1", P), route_for("G1", P))
    decl_G2(din_for("G2"),
            route_for("G2", {"qg": P["qg"], "kgf": SH["sh_kg"],
                             "vgf": SH["sh_vg"], "xn1": P["xn1"]}),
            route_for("G2", P))
    decl_H1(din_for("H1"), route_for("H1", P), route_for("H1", P))
    decl_H2(din_for("H2"), route_for("H2", P),
            route_for("H2", {"outp": outp}))

    with contextlib.ExitStack() as stack:
        ctx = Ctx(nc, stack)
        enp = stack.enter_context
        xs = {
            "ixs": enp(nc.sbuf_tensor("xixs", [128, 8], I32)),
            "ist": enp(nc.sbuf_tensor("xists", [16, 1], I32)),
            "non": enp(nc.sbuf_tensor("xnons", [16, 64], I32)),
            "flags": enp(nc.sbuf_tensor("xflags", [2, 16], I32)),
            "polls": [enp(nc.semaphore(f"xpoll{i}")) for i in range(4)],
        }
        ctx.new_phase("X0_")
        sldi = ctx.sem("ldi")
        for nm, dst in (("x_ix", "ixs"),
                        ("x_idx_st", "ist"), ("x_nonce", "non")):
            ctx.op("sync", lambda e, nm=nm, dst=dst: e.dma_start(
                out=xs[dst][:], in_=xin[nm]), inc=(sldi, 16))
        ctx.wait("gpsimd", sldi)
        ctx.wait("sync", sldi)
        xs["sldi"] = sldi

        seq = [
            ("A_", lambda n, c, e: emit_A(n, c, ios["A"], e)),
            ("X1_", lambda n, c, e: emit_xchg1(n, c, e, xs, P, SH)),
            ("B_", lambda n, c, e: emit_B(n, c, ios["B"], e)),
            ("C_", lambda n, c, e: emit_C(n, c, ios["C"], e)),
            ("X2_", lambda n, c, e: emit_xchg2(n, c, e, xs, P, SH)),
            ("D1_", lambda n, c, e: emit_D1(n, c, ios["D1"], e)),
            ("D2_", lambda n, c, e: emit_D2(n, c, ios["D2"], e)),
            ("E_", lambda n, c, e: emit_E(n, c, ios["E"], e)),
            ("X3_", lambda n, c, e: emit_xchg3(n, c, e, xs, P, SH)),
            ("F_", lambda n, c, e: emit_F(n, c, ios["F"], e)),
            ("G1_", lambda n, c, e: emit_G1(n, c, ios["G1"], e)),
            ("X4_", lambda n, c, e: emit_xchg4(n, c, e, xs, P, SH)),
            ("G2_", lambda n, c, e: emit_G2(n, c, ios["G2"], e)),
            ("H1_", lambda n, c, e: emit_H1(n, c, ios["H1"], e)),
            ("H2_", lambda n, c, e: emit_H2(n, c, ios["H2"], e)),
        ]
        for name, fn in seq:
            ctx.new_phase(name)
            with contextlib.ExitStack() as pst:
                fn(NCProxy(nc, name), ctx, pst.enter_context)
            ctx.barrier()
        ctx.finalize()
    return nc


def stageW_B(core, W):
    r = core % 2
    sl = slice(r * 512, (r + 1) * 512)
    return {"cw0": stage_bias(W["L0_attn_conv_w"][sl, 0, 0]),
            "cw1": stage_bias(W["L0_attn_conv_w"][sl, 0, 1]),
            "cw2": stage_bias(W["L0_attn_conv_w"][sl, 0, 2]),
            "cb": stage_bias(W["L0_attn_conv_b"][sl]),
            "la": stage_bias(W["L0_attn_logit_a"][sl]),
            "sb_": stage_bias(W["L0_attn_ssm_b"][sl]),
            "sc_": stage_bias(W["L0_attn_ssm_c"][sl]),
            "sd_": stage_bias(W["L0_attn_ssm_d"][sl])}


def stage_fused(core, W, nonce_val):
    b, r = core // 2, core % 2
    rows = own_rows(r)
    cst = const_arrays()
    m = {}

    def add(ph, d):
        for k, v in d.items():
            m[ph + "_" + k] = np.ascontiguousarray(v)

    add("A", stage_inputs_A(core, W))
    add("B", stageW_B(core, W))
    add("C", dict(c_ones_bf=cst["c_ones_bf"], c_ones_row=cst["c_ones_row"],
                  c_cfix=np.full((128, 1), -8.0, np.float32)))
    add("D1", dict(wout=W["L0_attn_out_proj_w"],
                   b_out=stage_bias(W["L0_attn_out_proj_b"]),
                   lnw=stage_bias(W["L0_attn_ln_w"]),
                   lnb=stage_bias(W["L0_attn_ln_b"]),
                   c_ones_bf=cst["c_ones_bf"],
                   c_ones_row=cst["c_ones_row"]))
    add("D2", dict(x_fm=fmT(W["L0_attn_x"][b][rows]),
                   c_ones_bf=cst["c_ones_bf"],
                   c_ones_row=cst["c_ones_row"]))
    add("E", dict(wq=W["L1_attn_Wq"], wk=W["L1_attn_Wk"],
                  wv=W["L1_attn_Wv"]))
    lam4 = np.concatenate([W["L1_attn_lambda_q1"], W["L1_attn_lambda_k1"],
                           W["L1_attn_lambda_q2"], W["L1_attn_lambda_k2"]],
                          axis=1)
    add("F", dict(lam4=lam4,
                  rsc=W["L1_attn_rms_scale"].reshape(128, 1),
                  wo=W["L1_attn_Wo"], c_maskt=mask_table(r),
                  c_id16=np.eye(16, dtype=np.float32),
                  c_e64c=e64c_mat(),
                  c_cfix=np.full((128, 2), -4.0, np.float32),
                  c_ones_bf=cst["c_ones_bf"],
                  c_ones_row=cst["c_ones_row"]))
    dG1 = stage_inputs_G1(core, W)
    del dG1["xn1"]
    add("G1", dG1)
    add("G2", dict(wg=W["L1_ffn_Wg"], bg=W["L1_ffn_bg"].reshape(64, 1),
                   wo=W["L1_ffn_Wo"], bo=stage_bias(W["L1_ffn_bo"]),
                   ln2w=stage_bias(W["L1_ffn_ln2_w"]),
                   wr=W["L1_ffn_Wr"], br=W["L1_ffn_br"].reshape(8, 1),
                   c_maskt=mask_table(r),
                   c_cfix=np.full((128, 1), -4.0, np.float32),
                   c_ones_bf=cst["c_ones_bf"],
                   c_ones_row=cst["c_ones_row"]))
    b1stage = np.zeros((128, 8 * 32), np.float32)
    for e in range(8):
        b1stage[:, e * 32:(e + 1) * 32] = stage_bias(W["L1_ffn_b1"][e])
    add("H1", dict(b1s=b1stage, w1=W["L1_ffn_W1"], w2=W["L1_ffn_W2"],
                   b2r=W["L1_ffn_b2"].reshape(1, 8192),
                   c_selg=selg_mat(), c_id8=np.eye(8, dtype=np.float32),
                   c_idL=np.eye(128, dtype=np.float32)))
    m.update(exchange_indices(core))
    non = np.zeros((16, 64), np.int32)
    for s in range(4):
        non[:, s * 16:(s + 1) * 16] = nonce_val + s
    m["x_nonce"] = non
    return m



_BUILD_CACHE = {}


def _get(name, fn):
    if name not in _BUILD_CACHE:
        _BUILD_CACHE[name] = fn()
    return _BUILD_CACHE[name]


def kernel(**W):
    cores = list(range(8))
    nc = _get("FUSED", build_fused)
    nonce_val = int(np.random.randint(1, 2 ** 31 - 100))
    in_maps = [stage_fused(c, W, nonce_val) for c in cores]
    res = run_bass_kernel_spmd(nc, in_maps, core_ids=cores).results
    rows_of = {r: own_rows(r) for r in (0, 1)}
    out = np.zeros((B, T, D), np.float32)
    for c in cores:
        b, r = c // 2, c % 2
        out[b][rows_of[r]] = np.asarray(res[c]["outp"]).T
    return out


def kernel_multi(**W):
    BFc = BF
    cores = list(range(8))
    cst = const_arrays()

    def run(name, build_fn, in_maps):
        nc = _get(name, build_fn)
        return run_bass_kernel_spmd(nc, in_maps, core_ids=cores).results

    ra = run("A", build_launch_A, [stage_inputs_A(c, W) for c in cores])

    rows_of = {r: own_rows(r) for r in (0, 1)}

    def asm(key, tm=False, width=1024):
        full = {}
        for b in range(4):
            out = np.zeros((T, width), dtype=BFc)
            for r in (0, 1):
                cr = ra[2 * b + r][key]
                out[rows_of[r]] = np.asarray(cr).T if not tm else cr
            full[b] = out
        return full

    v_full = asm("v")
    k_full = asm("fk")
    vq_full = asm("fv", tm=True)

    inB = []
    for c in cores:
        b, r = c // 2, c % 2
        v_fm = np.ascontiguousarray(v_full[b].T)
        inB.append(stage_inputs_B(c, W, v_fm))
    rb = run("B", build_launch_B, inB)

    inC = []
    for c in cores:
        b = c // 2
        inC.append(stage_inputs_C(
            c, W, np.ascontiguousarray(ra[c]["fq"]),
            np.ascontiguousarray(v_full[b].T * 0 + k_full[b].T)
            if False else np.ascontiguousarray(k_full[b].T),
            np.ascontiguousarray(vq_full[b])))
    rc = run("C", build_launch_C, inC)

    inD1 = []
    for c in cores:
        b, r = c // 2, c % 2
        rws = rows_of[r]
        syr = np.zeros((1024, NT), dtype=BFc)
        for rr in (0, 1):
            ch0 = rr * 512
            syc = np.asarray(rb[2 * b + rr]["sy"])
            syr[ch0:ch0 + 512, :] = syc[:, rws]
        inD1.append(dict(
            xn0=np.asarray(ra[c]["xn0"]),
            u=np.asarray(ra[c]["u"]), syr=np.ascontiguousarray(syr),
            wout=np.ascontiguousarray(W["L0_attn_out_proj_w"]),
            b_out=stage_bias(W["L0_attn_out_proj_b"]),
            lnw=stage_bias(W["L0_attn_ln_w"]),
            lnb=stage_bias(W["L0_attn_ln_b"]),
            c_ones_bf=cst["c_ones_bf"], c_ones_row=cst["c_ones_row"]))
    rd1 = run("D1", build_launch_D1, inD1)

    inD2 = []
    for c in cores:
        b, r = c // 2, c % 2
        inD2.append(dict(
            x_fm=fmT(W["L0_attn_x"][b][rows_of[r]]),
            s0=np.asarray(rd1[c]["s0"]), g=np.asarray(rc[c]["g"]),
            c_ones_bf=cst["c_ones_bf"], c_ones_row=cst["c_ones_row"]))
    rd2 = run("D2", build_launch_D2, inD2)

    inE = [dict(xn1_bf=np.asarray(rd2[c]["xn1_bf"]),
                wq=np.ascontiguousarray(W["L1_attn_Wq"]),
                wk=np.ascontiguousarray(W["L1_attn_Wk"]),
                wv=np.ascontiguousarray(W["L1_attn_Wv"])) for c in cores]
    re_ = run("E", build_launch_E, inE)

    Kfull, Vfull = {}, {}
    for b in range(4):
        kf = np.zeros((T, 2048), dtype=BFc)
        vf = np.zeros((T, 2048), dtype=BFc)
        for r in (0, 1):
            kf[rows_of[r]] = np.asarray(re_[2 * b + r]["Kr"]).T
            vf[rows_of[r]] = np.asarray(re_[2 * b + r]["Vr"])
        Kfull[b], Vfull[b] = kf, vf

    inF = [stage_inputs_F(c, W, np.asarray(re_[c]["Q"]),
                          np.ascontiguousarray(Kfull[c // 2].T),
                          np.ascontiguousarray(Vfull[c // 2]))
           for c in cores]
    rf = run("F", build_launch_F, inF)

    inG1 = []
    for c in cores:
        d = stage_inputs_G1(c, W)
        d["xn1"] = np.asarray(rd2[c]["xn1"])
        inG1.append(d)
    rg1 = run("G1", build_launch_G1, inG1)

    kgf, vgf = {}, {}
    for b in range(4):
        kg = np.zeros((T, 256), dtype=BFc)
        vg = np.zeros((T, 256), dtype=BFc)
        for r in (0, 1):
            kg[rows_of[r]] = np.asarray(rg1[2 * b + r]["kg"]).T
            vg[rows_of[r]] = np.asarray(rg1[2 * b + r]["vg"])
        kgf[b], vgf[b] = kg, vg

    inG2 = []
    for c in cores:
        b, r = c // 2, c % 2
        inG2.append(dict(
            qg=np.asarray(rg1[c]["qg"]),
            kgf=np.ascontiguousarray(kgf[b].T),
            vgf=np.ascontiguousarray(vgf[b]),
            wg=np.ascontiguousarray(W["L1_ffn_Wg"]),
            bg=np.ascontiguousarray(W["L1_ffn_bg"].reshape(64, 1)),
            wo=np.ascontiguousarray(W["L1_ffn_Wo"]),
            bo=stage_bias(W["L1_ffn_bo"]),
            xn1=np.asarray(rd2[c]["xn1"]),
            ln2w=stage_bias(W["L1_ffn_ln2_w"]),
            wr=np.ascontiguousarray(W["L1_ffn_Wr"]),
            br=np.ascontiguousarray(W["L1_ffn_br"].reshape(8, 1)),
            c_maskt=mask_table(r),
            c_cfix=np.full((128, 1), -4.0, np.float32),
            c_ones_bf=cst["c_ones_bf"], c_ones_row=cst["c_ones_row"]))
    rg2 = run("G2", build_launch_G2, inG2)

    b1stage = np.zeros((128, 8 * 32), np.float32)
    for e in range(8):
        b1stage[:, e * 32:(e + 1) * 32] = stage_bias(W["L1_ffn_b1"][e])
    inH1 = []
    for c in cores:
        inH1.append(dict(
            h2b=np.asarray(rg2[c]["h2b"]),
            logits=np.asarray(rg2[c]["logits"]),
            b1s=b1stage,
            w1=np.ascontiguousarray(W["L1_ffn_W1"]),
            w2=np.ascontiguousarray(W["L1_ffn_W2"]),
            b2r=np.ascontiguousarray(W["L1_ffn_b2"].reshape(1, 8192)),
            c_selg=selg_mat(),
            c_id8=np.eye(8, dtype=np.float32),
            c_idL=np.eye(128, dtype=np.float32)))
    rh1 = run("H1", build_launch_H1, inH1)

    inH2 = []
    for c in cores:
        b, r = c // 2, c % 2
        inH2.append(dict(
            x1=np.asarray(rd2[c]["x1"]), xp=np.asarray(rg2[c]["xp"]),
            y=np.asarray(rh1[c]["y"]), s1=np.asarray(rf[c]["s1"])))
    rh2 = run("H2", build_launch_H2, inH2)

    out = np.zeros((B, T, D), np.float32)
    for c in cores:
        b, r = c // 2, c % 2
        out[b][rows_of[r]] = np.asarray(rh2[c]["outp"]).T
    return out



# version 09
# speedup vs baseline: 1.1015x; 1.1015x over previous
import contextlib
import numpy as np
import ml_dtypes
import concourse.bass as bass
import concourse.mybir as mybir
from concourse.bass_core_rust import IndirectOffsetOnAxis
from concourse.bass_utils import run_bass_kernel_spmd

F32 = mybir.dt.float32
BF16 = mybir.dt.bfloat16
I32 = mybir.dt.int32
AF = mybir.ActivationFunctionType
ALU = mybir.AluOpType
AX = mybir.AxisListType
BF = ml_dtypes.bfloat16

B, T, D = 4, 4096, 1024
NBLK, BLK = 8, 512
OWN = {0: [0, 2, 5, 7], 1: [1, 3, 4, 6]}
NT = 2048
NTT = NT // 512
DC = D // 128

LN_EPS = 1e-5
RMS_EPS = 1e-6


def own_rows(r):
    idx = []
    for j in OWN[r]:
        idx.extend(range(j * BLK, (j + 1) * BLK))
    return np.array(idx)



ENGINES = ("tensor", "vector", "scalar", "gpsimd", "sync")


class SemL:
    def __init__(self, h):
        self.h = h
        self.n = 0
        self.base = 0


class PhaseSem:

    def __init__(self, pool):
        self._p = pool
        self.base = pool.n

    @property
    def h(self):
        return self._p.h

    @property
    def n(self):
        return self._p.n - self.base

    @n.setter
    def n(self, v):
        self._p.n = self.base + v


class Ctx:
    def __init__(self, nc, stack):
        self.nc = nc
        self.stack = stack
        self.rec = {e: [] for e in ENGINES}
        self.sems = {}
        self.pslots = []
        self._phase = {}
        self.prefix = ""
        self._barn = 0

    def new_phase(self, prefix):
        self.prefix = prefix
        self._phase = {}

    def sem(self, name):
        key = self.prefix + name
        if key not in self._phase:
            slot = len(self._phase)
            if slot >= len(self.pslots):
                h = self.stack.enter_context(
                    self.nc.semaphore(f"pool{slot}"))
                self.pslots.append(SemL(h))
            self._phase[key] = PhaseSem(self.pslots[slot])
        return self._phase[key]

    def gsem(self, name):
        if name not in self.sems:
            h = self.stack.enter_context(self.nc.semaphore(name))
            self.sems[name] = SemL(h)
        return self.sems[name]

    def barrier(self):
        self._barn += 1
        bars = {e: self.gsem("bar_" + e) for e in ENGINES}
        for e in ENGINES:
            sem = bars[e]
            sem.n += 1
            self.rec[e].append(lambda eng, s=sem.h: eng.nop().then_inc(s, 1))
        for e in ENGINES:
            for e2 in ENGINES:
                if e2 == e:
                    continue
                self.wait(e, bars[e2])

    def op(self, eng, fn, inc=None):
        if inc is None:
            self.rec[eng].append(fn)
        else:
            sem, k = inc
            sem.n += k
            self.rec[eng].append(
                lambda e, fn=fn, s=sem.h, k=k: fn(e).then_inc(s, k))

    def wait(self, eng, sem, val=None):
        v = sem.n if val is None else val
        if v > 0:
            va = v + sem.base
            self.rec[eng].append(lambda e, s=sem.h, v=va: e.wait_ge(s, v))

    def finalize(self):
        with self.nc.Block() as block:
            for eng in ENGINES:
                ops = self.rec[eng]
                if not ops:
                    continue

                def runner(e, ops=ops):
                    for f in ops:
                        f(e)
                getattr(block, eng)(runner)


class Stage:

    def __init__(self, ctx, buf, nslots, width, name="stg"):
        self.ctx = ctx
        self.buf = buf
        self.nslots = nslots
        self.width = width
        self.sems = [ctx.sem(f"{name}{i}") for i in range(nslots)]
        self.idx = 0

    def next(self, writer_eng):
        s = self.idx % self.nslots
        self.idx += 1
        if self.sems[s].n > 0:
            self.ctx.wait(writer_eng, self.sems[s])
        return self.buf[:, s * self.width:(s + 1) * self.width], s

    def store(self, queue_eng, fn, s):
        self.ctx.op(queue_eng, fn, inc=(self.sems[s], 16))

    def final_waits(self, eng):
        for sem in self.sems:
            self.ctx.wait(eng, sem)


class DBuf:

    def __init__(self, ctx, nslots, name):
        self.ctx = ctx
        self.sems = [ctx.sem(f"{name}{i}") for i in range(nslots)]
        self.nslots = nslots
        self.idx = 0

    def load(self, queue_eng, fn):
        s = self.idx % self.nslots
        self.idx += 1
        self.ctx.op(queue_eng, fn, inc=(self.sems[s], 16))
        return s

    def wait(self, eng, s=None):
        if s is None:
            s = (self.idx - 1) % self.nslots
        self.ctx.wait(eng, self.sems[s])


class NCProxy:

    def __init__(self, nc, prefix):
        self._nc = nc
        self._prefix = prefix

    def sbuf_tensor(self, name, *a, **kw):
        return self._nc.sbuf_tensor(self._prefix + name, *a, **kw)

    def psum_tensor(self, name, *a, **kw):
        return self._nc.psum_tensor(self._prefix + name, *a, **kw)

    def __getattr__(self, k):
        return getattr(self._nc, k)


def _wrap(decl, emit):
    nc = bass.Bass("TRN2", target_bir_lowering=False)
    io = {}

    def din(name, shape, dtype=F32):
        io[name] = nc.dram_tensor(name, shape, dtype,
                                  kind="ExternalInput").ap()

    def dout(name, shape, dtype=F32):
        io[name] = nc.dram_tensor(name, shape, dtype,
                                  kind="ExternalOutput").ap()

    decl(din, din, dout)
    with contextlib.ExitStack() as stack:
        ctx = Ctx(nc, stack)
        emit(nc, ctx, io, stack.enter_context)
        ctx.finalize()
    return nc



def fmT(a):
    return np.ascontiguousarray(a.T)


def stage_bias(b):
    k = b.shape[0] // 128
    return np.ascontiguousarray(b.reshape(k, 128).T)


def fff_host_stage(p, node_w, node_b, w1, b1, w2, b2):
    return {
        p + "_nw": fmT(node_w),
        p + "_nb": np.ascontiguousarray(node_b.reshape(15, 1)),
        p + "_w1p": np.ascontiguousarray(np.stack(
            [np.concatenate([w1[2 * i], w1[2 * i + 1]], axis=1)
             for i in range(8)])),
        p + "_b1p": np.ascontiguousarray(np.stack(
            [np.concatenate([b1[2 * i], b1[2 * i + 1]]) for i in range(8)],
            axis=1)),
        p + "_w2p": np.ascontiguousarray(np.stack(
            [np.concatenate([w2[2 * i], w2[2 * i + 1]], axis=0)
             for i in range(8)])),
        p + "_b2": np.ascontiguousarray(b2),
    }


def fff_amat():
    A = np.zeros((15, 16), np.float32)
    thr = np.zeros((16, 1), np.float32)
    for leaf in range(16):
        d0, d1 = (leaf >> 3) & 1, (leaf >> 2) & 1
        d2, d3 = (leaf >> 1) & 1, leaf & 1
        nodes = [0, 1 + d0, 3 + 2 * d0 + d1, 7 + 4 * d0 + 2 * d1 + d2]
        nleft = 0
        for n, d in zip(nodes, (d0, d1, d2, d3)):
            A[n, leaf] = 1.0 if d else -1.0
            nleft += (0 if d else 1)
        thr[leaf, 0] = 3.5 - nleft
    return A, thr


def e16_mat():
    E = np.zeros((16, 8 * 128), np.float32)
    for lp in range(8):
        E[2 * lp, lp * 128: lp * 128 + 64] = 1.0
        E[2 * lp + 1, lp * 128 + 64: lp * 128 + 128] = 1.0
    return E


def const_arrays():
    A, thr = fff_amat()
    return {
        "c_ones_bf": np.ones((128, 1), np.float32),
        "c_ones_row": np.ones((1, 128), np.float32),
        "c_amat": A,
        "c_thr": thr,
        "c_e16": e16_mat(),
    }



def emit_ln_fm(ctx, consts, xbuf, scr, rows, pb, nt, eps,
               pre=(), tag="", rms_only=False):
    sS = ctx.sem("lnS" + tag)
    sT = ctx.sem("lnT" + tag)
    sV = ctx.sem("lnV" + tag)
    done = ctx.sem("lnD" + tag)
    ntt = nt // 512
    ones_bf, ones_row = consts["ones_bf"], consts["ones_row"]

    for eng, sem, val in pre:
        ctx.wait(eng, sem, val)
    ctx.op("scalar", lambda e: e.activation(
        out=scr[:, 0:DC * nt], in_=xbuf[:, 0:DC * nt],
        func=AF.Identity), inc=(sS, 1))
    if not rms_only:
        ctx.wait("tensor", sS)
        for tt in range(ntt):
            o = tt * 512
            for c in range(DC):
                ctx.op("tensor", lambda e, c=c, o=o, tt=tt: e.matmul(
                    pb[tt][0:1, 0:512], ones_bf[:, 0:1],
                    scr[:, c * nt + o: c * nt + o + 512],
                    start=(c == 0), stop=(c == DC - 1)),
                    inc=(sT, 1) if c == DC - 1 else None)
        ctx.wait("vector", sT)
        for tt in range(ntt):
            o = tt * 512
            ctx.op("vector", lambda e, o=o, tt=tt: e.tensor_scalar_mul(
                out=rows[0:1, o:o + 512], in0=pb[tt][0:1, 0:512],
                scalar1=1.0 / D), inc=(sV, 1))
    ctx.wait("scalar", sT)
    ctx.op("scalar", lambda e: e.activation(
        out=scr[:, 0:DC * nt], in_=scr[:, 0:DC * nt], func=AF.Square),
        inc=(sS, 1))
    ctx.wait("tensor", sS)
    if not rms_only:
        ctx.wait("tensor", sV)
    for tt in range(ntt):
        o = tt * 512
        for c in range(DC):
            ctx.op("tensor", lambda e, c=c, o=o, tt=tt: e.matmul(
                pb[tt][0:1, 0:512], ones_bf[:, 0:1],
                scr[:, c * nt + o: c * nt + o + 512],
                start=(c == 0), stop=(c == DC - 1)),
                inc=(sT, 1) if c == DC - 1 else None)
    ctx.wait("vector", sT)
    for tt in range(ntt):
        o = tt * 512
        ctx.op("vector", lambda e, o=o, tt=tt: e.tensor_scalar_mul(
            out=rows[0:1, nt + o:nt + o + 512], in0=pb[tt][0:1, 0:512],
            scalar1=1.0 / D))
        if not rms_only:
            ctx.op("vector", lambda e, o=o: e.tensor_mul(
                out=rows[0:1, 2 * nt:2 * nt + 512],
                in0=rows[0:1, o:o + 512], in1=rows[0:1, o:o + 512]))
            ctx.op("vector", lambda e, o=o: e.tensor_sub(
                out=rows[0:1, nt + o:nt + o + 512],
                in0=rows[0:1, nt + o:nt + o + 512],
                in1=rows[0:1, 2 * nt:2 * nt + 512]))
        ctx.op("vector", lambda e, o=o: e.tensor_scalar_add(
            out=rows[0:1, nt + o:nt + o + 512],
            in0=rows[0:1, nt + o:nt + o + 512], scalar1=float(eps)))
        ctx.op("vector", lambda e, o=o: e.reciprocal(
            out=rows[0:1, nt + o:nt + o + 512],
            in_=rows[0:1, nt + o:nt + o + 512]), inc=(sV, 1))
    ctx.wait("scalar", sV)
    ctx.op("scalar", lambda e: e.activation(
        out=rows[0:1, nt:nt + nt], in_=rows[0:1, nt:nt + nt],
        func=AF.Sqrt), inc=(sS, 1))
    ctx.wait("tensor", sS)
    for tt in range(ntt):
        o = tt * 512
        if tt >= 1:
            ctx.wait("tensor", done, tt)
        if not rms_only:
            ctx.op("tensor", lambda e, o=o: e.matmul(
                pb[0][:, 0:512], ones_row[0:1, :], rows[0:1, o:o + 512],
                start=True, stop=True))
        ctx.op("tensor", lambda e, o=o: e.matmul(
            pb[1][:, 0:512], ones_row[0:1, :],
            rows[0:1, nt + o:nt + o + 512], start=True, stop=True),
            inc=(sT, 1))
        ctx.wait("vector", sT)
        xs = xbuf[:, 0:DC * nt].rearrange("p (c t) -> p c t",
                                          c=DC)[:, :, o:o + 512]
        mub = pb[0][:, 0:512].rearrange("p (a t) -> p a t",
                                        a=1).broadcast_to([128, DC, 512])
        rsb = pb[1][:, 0:512].rearrange("p (a t) -> p a t",
                                        a=1).broadcast_to([128, DC, 512])
        if not rms_only:
            ctx.op("vector", lambda e, xs=xs, mub=mub: e.tensor_tensor(
                out=xs, in0=xs, in1=mub, op=ALU.subtract))
        ctx.op("vector", lambda e, xs=xs, rsb=rsb: e.tensor_tensor(
            out=xs, in0=xs, in1=rsb, op=ALU.mult), inc=(done, 1))
    return done



def decl_A(din, dmid, dout):
    din("x_fm", [D, NT])
    din("w_in", [D, 2 * D])
    din("b_in", [128, 16])
    for f in ("q", "k", "v"):
        din(f + "_nw", [D, 15])
        din(f + "_nb", [15, 1])
        din(f + "_w1p", [8, D, 128])
        din(f + "_b1p", [128, 8])
        din(f + "_w2p", [8, 128, D])
        din(f + "_b2", [16, D])
    din("c_ones_bf", [128, 1])
    din("c_ones_row", [1, 128])
    din("c_amat", [15, 16])
    din("c_thr", [16, 1])
    din("c_e16", [16, 8 * 128])
    dout("xn0", [D, NT], F32)
    dout("u", [D, NT], BF16)
    dout("v", [D, NT], BF16)
    dout("fq", [D, NT], BF16)
    dout("fk", [D, NT], BF16)
    dout("fv", [NT, D], BF16)


def emit_A(nc, ctx, io, en):
    xbuf = en(nc.sbuf_tensor("xbuf", [128, DC * NT], F32))
    bufa = en(nc.sbuf_tensor("bufa", [128, DC * NT], BF16))
    bufb = en(nc.sbuf_tensor("bufb", [128, DC * NT], BF16))
    wsb = en(nc.sbuf_tensor("wsb", [128, DC * NT], BF16))
    stg = en(nc.sbuf_tensor("stg", [128, 8 * 512], BF16))
    relu_t = en(nc.sbuf_tensor("relu_t", [128, 4 * 512], BF16))
    rows = en(nc.sbuf_tensor("rows", [1, 2 * NT + 512], F32))
    ones_bf = en(nc.sbuf_tensor("ones_bf", [128, 1], BF16))
    ones_row = en(nc.sbuf_tensor("ones_row", [1, 128], F32))
    b_in = en(nc.sbuf_tensor("b_in_sb", [128, 16], F32))
    nw = en(nc.sbuf_tensor("nw", [128, DC * 15], F32))
    nbneg = en(nc.sbuf_tensor("nbneg", [15, 1], F32))
    b1p = en(nc.sbuf_tensor("b1p", [128, 8], F32))
    b2sb = en(nc.sbuf_tensor("b2sb", [16, D], BF16))
    amat = en(nc.sbuf_tensor("amat", [15, 16], BF16))
    thr = en(nc.sbuf_tensor("thr", [16, 1], F32))
    e16 = en(nc.sbuf_tensor("e16", [16, 8 * 128], BF16))
    sgn = en(nc.sbuf_tensor("sgn", [15, NT], BF16))
    sel = en(nc.sbuf_tensor("sel", [16, NT], BF16))
    pb = [en(nc.psum_tensor(f"pb{i}", [128, 512], F32)) for i in range(8)]

    sLdX = ctx.sem("ldx")
    sLdC = ctx.sem("ldc")
    sLgC = ctx.sem("lgc")
    sLgW = ctx.sem("lgw")
    sStX = ctx.sem("stxn")
    consts = dict(ones_bf=ones_bf, ones_row=ones_row)
    stage = Stage(ctx, stg, 8, 512, "sg")

    ctx.op("sync", lambda e: e.dma_start(
        out=xbuf[:, 0:DC * NT].rearrange("p (c t) -> p c t", c=DC),
        in_=io["x_fm"].rearrange("(c p) t -> p c t", p=128)),
        inc=(sLdX, 16))
    ctx.op("sync", lambda e: e.dma_start(out=ones_row[:],
                                         in_=io["c_ones_row"]),
           inc=(sLdC, 16))
    ctx.op("sync", lambda e: e.dma_start(out=thr[:], in_=io["c_thr"]),
           inc=(sLdC, 16))
    ctx.op("sync", lambda e: e.dma_start(out=b_in[:], in_=io["b_in"]),
           inc=(sLdC, 16))
    for nm, dst in (("c_ones_bf", ones_bf), ("c_amat", amat),
                    ("c_e16", e16)):
        ctx.op("gpsimd", lambda e, nm=nm, dst=dst: e.dma_start(
            out=dst[:], in_=io[nm]), inc=(sLgC, 16))
    ctx.op("gpsimd", lambda e: e.dma_start(
        out=wsb[:, 0:DC * NT].rearrange("p (c m) -> p c m", c=DC),
        in_=io["w_in"].rearrange("(c p) m -> p c m", p=128)),
        inc=(sLgW, 16))

    ln_done = emit_ln_fm(
        ctx, consts, xbuf, bufa, rows, pb, NT, LN_EPS,
        pre=[("scalar", sLdX, 16), ("tensor", sLdC, 48),
             ("tensor", sLgC, 48)], tag="0")
    sXnb = ctx.sem("xnb")
    ctx.wait("scalar", ln_done)
    ctx.op("scalar", lambda e: e.activation(
        out=bufa[:, 0:DC * NT], in_=xbuf[:, 0:DC * NT],
        func=AF.Identity), inc=(sXnb, 1))
    ctx.wait("sync", ln_done)
    ctx.op("sync", lambda e: e.dma_start(
        out=io["xn0"].rearrange("(c p) t -> p c t", p=128),
        in_=xbuf[:, 0:DC * NT].rearrange("p (c t) -> p c t", c=DC)),
        inc=(sStX, 16))

    sTi = ctx.sem("Tinp")
    sEi = ctx.sem("Einp")
    ctx.wait("tensor", sXnb)
    ctx.wait("tensor", sLgW)
    ctx.wait("scalar", sLdC)
    for mc in range(16):
        base = (mc % 2) * NTT
        if mc >= 2:
            ctx.wait("tensor", sEi, (mc - 1) * NTT)
        for dc in range(DC):
            for tt in range(NTT):
                ctx.op("tensor", lambda e, mc=mc, dc=dc, tt=tt, base=base:
                       e.matmul(
                           pb[base + tt][:, 0:512],
                           wsb[:, (dc * 16 + mc) * 128:
                               (dc * 16 + mc) * 128 + 128],
                           bufa[:, dc * NT + tt * 512:
                                dc * NT + tt * 512 + 512],
                           start=(dc == 0), stop=(dc == DC - 1)),
                       inc=(sTi, 1) if (dc == DC - 1 and tt == NTT - 1)
                       else None)
        ctx.wait("scalar", sTi, mc + 1)
        hbm = io["u"] if mc < 8 else io["v"]
        mcl = mc % 8
        func = AF.Identity if mc < 8 else AF.Silu
        for tt in range(NTT):
            tile, slot = stage.next("scalar")
            ctx.op("scalar", lambda e, tile=tile, mc=mc, tt=tt,
                   base=base, func=func: e.activation(
                       out=tile, in_=pb[base + tt][:, 0:512], func=func,
                       bias=b_in[:, mc:mc + 1], scale=1.0),
                   inc=(sEi, 1))
            ctx.wait("sync", sEi)
            stage.store("sync", lambda e, tile=tile, hbm=hbm, mcl=mcl,
                        tt=tt: e.dma_start(
                            out=hbm.rearrange("(c p) t -> p c t", p=128)
                            [:, mcl:mcl + 1, tt * 512:tt * 512 + 512],
                            in_=tile), slot)

    prev = dict(T=sTi, V=None, S=None, E=sEi)
    out_map = {"q": ("fq", "fm"), "k": ("fk", "fm"), "v": ("fv", "tm")}
    for fi, f in enumerate(("q", "k", "v")):
        tg = "f" + f
        sG2 = ctx.sem("ldg" + tg)
        sT = ctx.sem("T" + tg)
        sE = ctx.sem("E" + tg)
        sVf = ctx.sem("V" + tg)
        sSf = ctx.sem("S" + tg)
        sMk = ctx.sem("M" + tg)
        ctx.wait("gpsimd", prev["T"])
        if prev["V"] is not None:
            ctx.wait("gpsimd", prev["V"])
        if prev["S"] is not None:
            ctx.wait("gpsimd", prev["S"])
        ctx.op("gpsimd", lambda e, f=f: e.dma_start(
            out=nw[:, 0:DC * 15].rearrange("p (c n) -> p c n", c=DC),
            in_=io[f + "_nw"].rearrange("(c p) n -> p c n", p=128)),
            inc=(sG2, 16))
        ctx.op("gpsimd", lambda e, f=f: e.dma_start(
            out=wsb[:, 0:8 * 1024].rearrange("p (l c m) -> p l c m",
                                             l=8, c=DC),
            in_=io[f + "_w1p"].rearrange("l (c p) m -> p l c m",
                                         p=128)), inc=(sG2, 16))
        ctx.op("gpsimd", lambda e, f=f: e.dma_start(
            out=wsb[:, 8 * 1024:16 * 1024].rearrange(
                "p (l m) -> p l m", l=8),
            in_=io[f + "_w2p"].rearrange("l p m -> p l m")),
            inc=(sG2, 16))
        ctx.op("gpsimd", lambda e, f=f: e.dma_start(
            out=b1p[:], in_=io[f + "_b1p"]), inc=(sG2, 16))
        ctx.op("gpsimd", lambda e, f=f: e.dma_start(
            out=b2sb[:], in_=io[f + "_b2"]), inc=(sG2, 16))
        ctx.op("gpsimd", lambda e, f=f: e.dma_start(
            out=nbneg[:], in_=io[f + "_nb"]), inc=(sG2, 16))
        ctx.wait("vector", sG2)
        ctx.wait("vector", prev["T"])
        ctx.op("vector", lambda e: e.tensor_scalar_mul(
            out=nbneg[:], in0=nbneg[:], scalar1=-1.0), inc=(sVf, 1))
        ctx.wait("vector", sVf)

        ctx.wait("tensor", sG2)
        ctx.wait("tensor", prev["E"])
        if fi == 0:
            ctx.wait("vector", sLdC)
        for tt in range(NTT):
            o = tt * 512
            if tt >= 1:
                ctx.wait("tensor", sVf, 1 + 2 * tt)
            for c in range(DC):
                ctx.op("tensor", lambda e, c=c, o=o: e.matmul(
                    pb[0][0:15, 0:512], nw[:, c * 15:c * 15 + 15],
                    xbuf[:, c * NT + o:c * NT + o + 512],
                    start=(c == 0), stop=(c == DC - 1)),
                    inc=(sT, 1) if c == DC - 1 else None)
            ctx.wait("vector", sT)
            ctx.op("vector", lambda e, o=o: e.tensor_single_scalar(
                out=sgn[:, o:o + 512], in_=pb[0][0:15, 0:512],
                scalar=nbneg[:], op=ALU.is_ge), inc=(sVf, 1))
            ctx.wait("tensor", sVf)
            ctx.op("tensor", lambda e, o=o: e.matmul(
                pb[1][0:16, 0:512], amat[:], sgn[:, o:o + 512],
                start=True, stop=True), inc=(sT, 1))
            ctx.wait("vector", sT)
            ctx.op("vector", lambda e, o=o: e.tensor_single_scalar(
                out=sel[:, o:o + 512], in_=pb[1][0:16, 0:512],
                scalar=thr[:], op=ALU.is_ge), inc=(sVf, 1))

        ctx.wait("tensor", sVf)
        for lp in range(8):
            if lp >= 1:
                ctx.wait("tensor", sSf, 4 * lp)
            for dc in range(DC):
                for tt in range(NTT):
                    ctx.op("tensor",
                           lambda e, lp=lp, dc=dc, tt=tt: e.matmul(
                               pb[2 + tt][:, 0:512],
                               wsb[:, lp * 1024 + dc * 128:
                                   lp * 1024 + dc * 128 + 128],
                               bufa[:, dc * NT + tt * 512:
                                    dc * NT + tt * 512 + 512],
                               start=(dc == 0), stop=(dc == DC - 1)),
                           inc=(sT, 1)
                           if (dc == DC - 1 and tt == NTT - 1) else None)
            for tt in range(NTT):
                idx = lp * NTT + tt
                o = tt * 512
                ctx.wait("scalar", sT)
                if idx >= 4:
                    ctx.wait("scalar", sMk, idx - 4 + 1)
                ctx.op("scalar", lambda e, lp=lp, tt=tt, idx=idx:
                       e.activation(
                           out=relu_t[:, (idx % 4) * 512:
                                      (idx % 4) * 512 + 512],
                           in_=pb[2 + tt][:, 0:512], func=AF.Relu,
                           bias=b1p[:, lp:lp + 1], scale=1.0),
                       inc=(sSf, 1))
                if idx >= 2:
                    ctx.wait("tensor", sMk, idx - 2 + 1)
                ctx.op("tensor", lambda e, lp=lp, o=o, idx=idx: e.matmul(
                    pb[6 + idx % 2][:, 0:512],
                    e16[:, lp * 128:lp * 128 + 128],
                    sel[:, o:o + 512],
                    start=True, stop=True), inc=(sT, 1))
                ctx.wait("vector", sT)
                ctx.wait("vector", sSf)
                ctx.op("vector", lambda e, lp=lp, tt=tt, idx=idx:
                       e.tensor_mul(
                           out=bufb[:, idx * 512:idx * 512 + 512],
                           in0=relu_t[:, (idx % 4) * 512:
                                      (idx % 4) * 512 + 512],
                           in1=pb[6 + idx % 2][:, 0:512]),
                       inc=(sMk, 1))

        hbm_name, kind = out_map[f]
        hbm = io[hbm_name]
        ctx.wait("tensor", sMk)
        if kind == "fm":
            for dc in range(DC):
                base = (dc % 2) * NTT
                if dc >= 2:
                    ctx.wait("tensor", sE, (dc - 1) * NTT)
                for lp in range(8):
                    for tt in range(NTT):
                        ctx.op("tensor",
                               lambda e, dc=dc, lp=lp, tt=tt, base=base:
                               e.matmul(
                                   pb[base + tt][:, 0:512],
                                   wsb[:, 8192 + lp * 1024 + dc * 128:
                                       8192 + lp * 1024 + dc * 128 + 128],
                                   bufb[:, (lp * NTT + tt) * 512:
                                        (lp * NTT + tt) * 512 + 512],
                                   start=(lp == 0), stop=False))
                for tt in range(NTT):
                    ctx.op("tensor", lambda e, dc=dc, tt=tt, base=base:
                           e.matmul(
                               pb[base + tt][:, 0:512],
                               b2sb[:, dc * 128:dc * 128 + 128],
                               sel[:, tt * 512:tt * 512 + 512],
                               start=False, stop=True),
                           inc=(sT, 1) if tt == NTT - 1 else None)
                ctx.wait("scalar", sT)
                for tt in range(NTT):
                    tile, slot = stage.next("scalar")
                    ctx.op("scalar", lambda e, tile=tile, tt=tt,
                           base=base: e.activation(
                               out=tile, in_=pb[base + tt][:, 0:512],
                               func=AF.Identity), inc=(sE, 1))
                    ctx.wait("sync", sE)
                    stage.store("sync", lambda e, tile=tile, hbm=hbm,
                                dc=dc, tt=tt: e.dma_start(
                                    out=hbm.rearrange(
                                        "(c p) t -> p c t", p=128)
                                    [:, dc:dc + 1,
                                     tt * 512:tt * 512 + 512],
                                    in_=tile), slot)
        else:
            for ts in range(16):
                base = (ts % 2) * 2
                if ts >= 2:
                    ctx.wait("tensor", sE, (ts - 1) * 2)
                for dh in range(2):
                    for lp in range(8):
                        ctx.op("tensor",
                               lambda e, ts=ts, dh=dh, lp=lp, base=base:
                               e.matmul(
                                   pb[base + dh][:, 0:512],
                                   bufb[:, lp * NT + ts * 128:
                                        lp * NT + ts * 128 + 128],
                                   wsb[:, 8192 + lp * 1024 + dh * 512:
                                       8192 + lp * 1024 + dh * 512
                                       + 512],
                                   start=(lp == 0), stop=False))
                    ctx.op("tensor", lambda e, ts=ts, dh=dh, base=base:
                           e.matmul(
                               pb[base + dh][:, 0:512],
                               sel[:, ts * 128:ts * 128 + 128],
                               b2sb[:, dh * 512:dh * 512 + 512],
                               start=False, stop=True),
                           inc=(sT, 1) if dh == 1 else None)
                ctx.wait("scalar", sT)
                for dh in range(2):
                    tile, slot = stage.next("scalar")
                    ctx.op("scalar", lambda e, tile=tile, dh=dh,
                           base=base: e.activation(
                               out=tile, in_=pb[base + dh][:, 0:512],
                               func=AF.Identity), inc=(sE, 1))
                    ctx.wait("sync", sE)
                    stage.store("sync", lambda e, tile=tile, ts=ts,
                                dh=dh: e.dma_start(
                                    out=io["fv"][ts * 128:ts * 128
                                                 + 128,
                                                 dh * 512:dh * 512
                                                 + 512],
                                    in_=tile), slot)
        prev = dict(T=sT, V=sMk, S=sSf, E=sE)

    ctx.wait("sync", sStX)
    stage.final_waits("sync")


def build_launch_A():
    return _wrap(decl_A, emit_A)


def stage_inputs_A(core, W):
    b, r = core // 2, core % 2
    rows = own_rows(r)
    d = {"x_fm": fmT(W["L0_attn_x"][b][rows]),
         "w_in": np.ascontiguousarray(W["L0_attn_in_proj_w"]),
         "b_in": stage_bias(W["L0_attn_in_proj_b"])}
    for f in ("q", "k", "v"):
        d.update(fff_host_stage(
            f,
            W[f"L0_ffn_{f}_node_w"], W[f"L0_ffn_{f}_node_b"],
            W[f"L0_ffn_{f}_w1"], W[f"L0_ffn_{f}_b1"],
            W[f"L0_ffn_{f}_w2"], W[f"L0_ffn_{f}_b2"]))
    d.update(const_arrays())
    return d





def decl_B(din, dmid, dout):
    dmid("v_ch", [512, T], BF16)
    for nm in ("cw0", "cw1", "cw2", "cb", "la", "sb_", "sc_", "sd_"):
        din(nm, [128, 4])
    dout("sy", [512, T], BF16)


def emit_B(nc, ctx, io, en):
    vbuf = en(nc.sbuf_tensor("vbuf", [128, 2 * T], BF16))
    vloc = en(nc.sbuf_tensor("vloc", [128, T], F32))
    hbuf = en(nc.sbuf_tensor("hbuf", [128, T], F32))
    sybuf = en(nc.sbuf_tensor("sybuf", [128, 2 * T], BF16))
    par = {}
    for nm in ("cw0", "cw1", "cw2", "cb", "la", "sb_", "sc_", "sd_"):
        par[nm] = en(nc.sbuf_tensor(nm + "_sb", [128, 4], F32))
    sLdP = ctx.sem("ldp")
    sV = ctx.sem("v")
    sS = ctx.sem("s")
    vload = DBuf(ctx, 2, "vl")
    sStS = [ctx.sem("sts0"), ctx.sem("sts1")]
    for nm in ("cw0", "cw1", "cw2", "cb", "la", "sb_", "sc_", "sd_"):
        ctx.op("sync", lambda e, nm=nm: e.dma_start(
            out=par[nm][:], in_=io[nm]), inc=(sLdP, 16))
    ctx.wait("scalar", sLdP)
    ctx.op("scalar", lambda e: e.activation(
        out=par["la"][:], in_=par["la"][:], func=AF.Sigmoid),
        inc=(sS, 1))
    for cc in range(4):
        buf = vbuf[:, (cc % 2) * T:(cc % 2) * T + T]
        syb = sybuf[:, (cc % 2) * T:(cc % 2) * T + T]
        if cc >= 2:
            ctx.wait("sync", sV, 7 * cc - 7)
        vload.load("sync", lambda e, cc=cc, buf=buf: e.dma_start(
            out=buf, in_=io["v_ch"][cc * 128:cc * 128 + 128, :]))
        vload.wait("vector")
        ctx.wait("vector", sS, 1)
        ctx.op("vector", lambda e, cc=cc, buf=buf: e.tensor_scalar_mul(
            out=vloc[:], in0=buf, scalar1=par["cw1"][:, cc:cc + 1]))
        ctx.op("vector", lambda e, cc=cc, buf=buf:
               e.scalar_tensor_tensor(
                   out=vloc[:, 1:T], in0=buf[:, 0:T - 1],
                   scalar=par["cw0"][:, cc:cc + 1], in1=vloc[:, 1:T],
                   op0=ALU.mult, op1=ALU.add))
        ctx.op("vector", lambda e, cc=cc, buf=buf:
               e.scalar_tensor_tensor(
                   out=vloc[:, 0:T - 1], in0=buf[:, 1:T],
                   scalar=par["cw2"][:, cc:cc + 1], in1=vloc[:, 0:T - 1],
                   op0=ALU.mult, op1=ALU.add))
        ctx.op("vector", lambda e, cc=cc: e.tensor_scalar_add(
            out=vloc[:], in0=vloc[:], scalar1=par["cb"][:, cc:cc + 1]))
        ctx.op("vector", lambda e, cc=cc: e.tensor_scalar_mul(
            out=hbuf[:], in0=vloc[:], scalar1=par["sb_"][:, cc:cc + 1]))
        a_b = par["la"][:, cc:cc + 1].broadcast_to([128, T])
        ctx.op("vector", lambda e, cc=cc, a_b=a_b: e.tensor_tensor_scan(
            out=hbuf[:], data0=a_b, data1=hbuf[:],
            initial=0.0, op0=ALU.mult, op1=ALU.add))
        ctx.op("vector", lambda e, cc=cc: e.tensor_scalar_mul(
            out=vloc[:], in0=vloc[:], scalar1=par["sd_"][:, cc:cc + 1]))
        ctx.op("vector", lambda e, cc=cc: e.scalar_tensor_tensor(
            out=vloc[:], in0=hbuf[:], scalar=par["sc_"][:, cc:cc + 1],
            in1=vloc[:], op0=ALU.mult, op1=ALU.add), inc=(sV, 7))
        ctx.wait("scalar", sV)
        if cc >= 2:
            ctx.wait("scalar", sStS[cc % 2], 16 * (cc // 2))
        ctx.op("scalar", lambda e, syb=syb: e.activation(
            out=syb, in_=vloc[:], func=AF.Silu), inc=(sS, 1))
        ctx.wait("sync", sS)
        ctx.op("sync", lambda e, cc=cc, syb=syb: e.dma_start(
            out=io["sy"][cc * 128:cc * 128 + 128, :], in_=syb),
            inc=(sStS[cc % 2], 16))
    ctx.wait("sync", sStS[0])
    ctx.wait("sync", sStS[1])


def build_launch_B():
    return _wrap(decl_B, emit_B)


def stage_inputs_B(core, W, v_full_fm):
    b, r = core // 2, core % 2
    ch0 = r * 512
    d = {"v_ch": np.ascontiguousarray(v_full_fm[ch0:ch0 + 512, :])}
    sl = slice(ch0, ch0 + 512)
    d["cw0"] = stage_bias(W["L0_attn_conv_w"][sl, 0, 0])
    d["cw1"] = stage_bias(W["L0_attn_conv_w"][sl, 0, 1])
    d["cw2"] = stage_bias(W["L0_attn_conv_w"][sl, 0, 2])
    d["cb"] = stage_bias(W["L0_attn_conv_b"][sl])
    d["la"] = stage_bias(W["L0_attn_logit_a"][sl])
    d["sb_"] = stage_bias(W["L0_attn_ssm_b"][sl])
    d["sc_"] = stage_bias(W["L0_attn_ssm_c"][sl])
    d["sd_"] = stage_bias(W["L0_attn_ssm_d"][sl])
    return d





def decl_C(din, dmid, dout):
    dmid("fq", [D, NT], BF16)
    dmid("kf", [8 * D, 512], BF16)
    dmid("vf", [T, D], BF16)
    din("c_ones_bf", [128, 1])
    din("c_ones_row", [1, 128])
    din("c_cfix", [128, 1])
    dout("g", [D, NT], BF16)


def emit_C(nc, ctx, io, en):
    KC = T // 128
    qbuf = en(nc.sbuf_tensor("qbuf", [128, DC * NT], BF16))
    ksb = en(nc.sbuf_tensor("ksb", [128, DC * T], BF16))
    psb = en(nc.sbuf_tensor("psb", [128, KC * 512], BF16))
    vbuf = en(nc.sbuf_tensor("vbuf", [128, 2 * 512], BF16))
    sqscr = en(nc.sbuf_tensor("sqscr", [128, 2 * 512], BF16))
    dinv = en(nc.sbuf_tensor("dinv", [128, 512], F32))
    onorm = en(nc.sbuf_tensor("onorm", [128, 512], F32))
    stg = en(nc.sbuf_tensor("stg", [128, 4 * 512], BF16))
    rows = en(nc.sbuf_tensor("rows", [1, 64], F32))
    bias_col = en(nc.sbuf_tensor("bias_col", [128, 1], F32))
    ones_bf = en(nc.sbuf_tensor("ones_bf", [128, 1], BF16))
    ones_row = en(nc.sbuf_tensor("ones_row", [1, 128], F32))
    pb = [en(nc.psum_tensor(f"pb{i}", [128, 512], F32)) for i in range(8)]

    sLd = ctx.sem("ld")
    sLg = ctx.sem("ldg")
    vload = DBuf(ctx, 2, "vl")
    sT = ctx.sem("t")
    sV = ctx.sem("v")
    sS = ctx.sem("s")
    stage = Stage(ctx, stg, 4, 512, "sg")

    ctx.op("sync", lambda e: e.dma_start(
        out=qbuf[:].rearrange("p (c t) -> p c t", c=DC),
        in_=io["fq"].rearrange("(c p) t -> p c t", p=128)),
        inc=(sLd, 16))
    for k_ in range(8):
        ctx.op("sync", lambda e, k_=k_: e.dma_start(
            out=ksb[:].rearrange("p (c k t) -> p c k t", c=DC, k=8)
            [:, :, k_, :],
            in_=io["kf"].rearrange("(k c p) t -> k p c t", c=DC, p=128)
            [k_]), inc=(sLd, 16))
    ctx.op("gpsimd", lambda e: e.dma_start(out=ones_bf[:],
                                           in_=io["c_ones_bf"]),
           inc=(sLg, 16))
    ctx.op("sync", lambda e: e.dma_start(out=ones_row[:],
                                         in_=io["c_ones_row"]),
           inc=(sLd, 16))

    ctx.wait("scalar", sLd)
    ctx.wait("tensor", sLd)
    ctx.wait("tensor", sLg)
    ctx.op("sync", lambda e: e.dma_start(out=bias_col[:],
                                         in_=io["c_cfix"]),
           inc=(sLd, 16))
    sPV = ctx.sem("pv")
    sDen = ctx.sem("den")
    for qb in range(NTT):
        qo = qb * 512
        ctx.wait("scalar", sLd)
        ctx.wait("scalar", sV)
        if qb >= 1:
            ctx.wait("scalar", sPV)
            ctx.wait("tensor", sV)
        tS = {}
        eS = {}

        def emit_S(kc, qo=qo):
            sb_ = kc % 2
            if kc >= 2:
                ctx.wait("tensor", sS, eS[kc - 2])
            for dc in range(DC):
                ctx.op("tensor", lambda e, kc=kc, dc=dc, qo=qo, sb_=sb_:
                       e.matmul(
                           pb[sb_][:, 0:512],
                           ksb[:, dc * T + kc * 128:dc * T + kc * 128
                               + 128],
                           qbuf[:, dc * NT + qo:dc * NT + qo + 512],
                           start=(dc == 0), stop=(dc == DC - 1)),
                       inc=(sT, 1) if dc == DC - 1 else None)
            tS[kc] = sT.n

        def emit_exp(kc):
            ctx.wait("scalar", sT, tS[kc])
            sb_ = kc % 2
            ctx.op("scalar", lambda e, kc=kc, sb_=sb_: e.activation(
                out=psb[:, kc * 512:kc * 512 + 512],
                in_=pb[sb_][:, 0:512], func=AF.Exp,
                bias=bias_col[:, 0:1], scale=1.0 / 32.0), inc=(sS, 1))
            eS[kc] = sS.n

        emit_S(0)
        emit_exp(0)
        for kc in range(KC):
            if kc + 1 < KC:
                emit_S(kc + 1)
                emit_exp(kc + 1)
            ctx.wait("tensor", sS, eS[kc])
            ctx.op("tensor", lambda e, kc=kc: e.matmul(
                pb[2][0:1, 0:512], ones_bf[:, 0:1],
                psb[:, kc * 512:kc * 512 + 512],
                start=(kc == 0), stop=(kc == KC - 1)),
                inc=(sDen, 1) if kc == KC - 1 else None)
        ctx.wait("vector", sDen)
        ctx.op("vector", lambda e: e.reciprocal(
            out=dinv[0:1, 0:512], in_=pb[2][0:1, 0:512]), inc=(sV, 1))
        ctx.wait("tensor", sV)
        ctx.op("tensor", lambda e: e.matmul(
            pb[7][:, 0:512], ones_row[0:1, :], dinv[0:1, 0:512],
            start=True, stop=True), inc=(sT, 1))
        ctx.wait("vector", sT)
        ctx.op("vector", lambda e: e.tensor_copy(
            out=dinv[:, 0:512], in_=pb[7][:, 0:512]), inc=(sV, 1))
        for half in range(2):
            ctx.wait("tensor", sV)
            lslot = {}

            def emit_vload(kc, half=half, qb=qb):
                nld = qb * 2 * KC + half * KC + kc
                if nld >= 2:
                    ctx.wait("scalar", sPV, (nld - 1) * 4)
                lslot[kc] = vload.load(
                    "scalar", lambda e, kc=kc, half=half: e.dma_start(
                        out=vbuf[:, (kc % 2) * 512:
                                 (kc % 2) * 512 + 512],
                        in_=io["vf"][kc * 128:kc * 128 + 128,
                                     half * 512:half * 512 + 512]))

            emit_vload(0)
            for kc in range(KC):
                if kc + 1 < KC:
                    emit_vload(kc + 1)
                vload.wait("tensor", lslot[kc])
                for du in range(4):
                    ctx.op("tensor", lambda e, kc=kc, du=du:
                           e.matmul(
                               pb[3 + du][:, 0:512],
                               vbuf[:, (kc % 2) * 512 + du * 128:
                                    (kc % 2) * 512 + du * 128 + 128],
                               psb[:, kc * 512:kc * 512 + 512],
                               start=(kc == 0), stop=(kc == KC - 1)),
                           inc=(sPV, 1))
            ctx.wait("vector", sPV)
            for du in range(4):
                dc = half * 4 + du
                ctx.wait("vector", sS)
                ctx.op("vector", lambda e, du=du: e.tensor_mul(
                    out=onorm[:, 0:512], in0=pb[3 + du][:, 0:512],
                    in1=dinv[:, 0:512]), inc=(sV, 1))
                ctx.wait("scalar", sV)
                tile, slot = stage.next("scalar")
                ctx.op("scalar", lambda e, tile=tile: e.activation(
                    out=tile, in_=onorm[:, 0:512], func=AF.Silu),
                    inc=(sS, 1))
                ctx.wait("sync", sS)
                stage.store("sync", lambda e, tile=tile, dc=dc, qo=qo:
                            e.dma_start(
                                out=io["g"].rearrange(
                                    "(c p) t -> p c t", p=128)
                                [:, dc:dc + 1, qo:qo + 512],
                                in_=tile), slot)
    stage.final_waits("sync")


def build_launch_C():
    return _wrap(decl_C, emit_C)


def stage_inputs_C(core, W, fq_mine, kf_full, vf_full):
    d = {"fq": fq_mine, "kf": kf_full, "vf": vf_full}
    d["c_ones_bf"] = np.ones((128, 1), np.float32)
    d["c_ones_row"] = np.ones((1, 128), np.float32)
    d["c_cfix"] = np.full((128, 1), -8.0, np.float32)
    return d





def decl_D1(din, dmid, dout):
    dmid("xn0", [D, NT], F32)
    dmid("u", [D, NT], BF16)
    dmid("syr", [D, NT], BF16)
    din("wout", [D, D])
    for nm in ("b_out", "lnw", "lnb"):
        din(nm, [128, 8])
    din("c_ones_bf", [128, 1])
    din("c_ones_row", [1, 128])
    dout("s0", [D, NT], F32)


def emit_D1(nc, ctx, io, en):
    hbuf = en(nc.sbuf_tensor("hbuf", [128, DC * NT], F32))
    ubuf = en(nc.sbuf_tensor("ubuf", [128, DC * NT], BF16))
    sybuf = en(nc.sbuf_tensor("sybuf", [128, DC * NT], BF16))
    wsb = en(nc.sbuf_tensor("wsb", [128, DC * 1024], BF16))
    stgf = en(nc.sbuf_tensor("stgf", [128, 2 * 512], F32))
    rows = en(nc.sbuf_tensor("rows", [1, 2 * NT + 512], F32))
    ones_bf = en(nc.sbuf_tensor("ones_bf", [128, 1], BF16))
    ones_row = en(nc.sbuf_tensor("ones_row", [1, 128], F32))
    par = {nm: en(nc.sbuf_tensor(nm + "_sb", [128, 8], F32))
           for nm in ("b_out", "lnw", "lnb")}
    pb = [en(nc.psum_tensor(f"pb{i}", [128, 512], F32)) for i in range(8)]

    sLd = ctx.sem("ld")
    sLg = ctx.sem("ldg")
    sZ = ctx.sem("z")
    sT = ctx.sem("t")
    sE = ctx.sem("evac")
    sVh = ctx.sem("vh")
    sSt = ctx.sem("st")
    consts = dict(ones_bf=ones_bf, ones_row=ones_row)

    ctx.op("sync", lambda e: e.dma_start(
        out=hbuf[:].rearrange("p (c t) -> p c t", c=DC),
        in_=io["xn0"].rearrange("(c p) t -> p c t", p=128)),
        inc=(sLd, 16))
    ctx.op("sync", lambda e: e.dma_start(
        out=ubuf[:].rearrange("p (c t) -> p c t", c=DC),
        in_=io["u"].rearrange("(c p) t -> p c t", p=128)),
        inc=(sLd, 16))
    ctx.op("sync", lambda e: e.dma_start(
        out=sybuf[:].rearrange("p (c t) -> p c t", c=DC),
        in_=io["syr"].rearrange("(c p) t -> p c t", p=128)),
        inc=(sLd, 16))
    ctx.op("sync", lambda e: e.dma_start(out=ones_row[:],
                                         in_=io["c_ones_row"]),
           inc=(sLd, 16))
    for nm in ("b_out", "lnw", "lnb"):
        ctx.op("sync", lambda e, nm=nm: e.dma_start(
            out=par[nm][:], in_=io[nm]), inc=(sLd, 16))
    ctx.op("gpsimd", lambda e: e.dma_start(out=ones_bf[:],
                                           in_=io["c_ones_bf"]),
           inc=(sLg, 16))
    ctx.op("gpsimd", lambda e: e.dma_start(
        out=wsb[:].rearrange("p (c m) -> p c m", c=DC),
        in_=io["wout"].rearrange("(c p) m -> p c m", p=128)),
        inc=(sLg, 16))

    ctx.wait("vector", sLd)
    ctx.op("vector", lambda e: e.tensor_mul(
        out=ubuf[:], in0=ubuf[:], in1=sybuf[:]), inc=(sZ, 1))

    ctx.wait("tensor", sZ)
    ctx.wait("tensor", sLg)
    for mc in range(DC):
        base = (mc % 2) * NTT
        if mc >= 2:
            ctx.wait("tensor", sE, (mc - 1) * NTT)
        for dc in range(DC):
            for tt in range(NTT):
                ctx.op("tensor", lambda e, mc=mc, dc=dc, tt=tt,
                       base=base: e.matmul(
                           pb[base + tt][:, 0:512],
                           wsb[:, (dc * DC + mc) * 128:
                               (dc * DC + mc) * 128 + 128],
                           ubuf[:, dc * NT + tt * 512:
                                dc * NT + tt * 512 + 512],
                           start=(dc == 0), stop=(dc == DC - 1)),
                       inc=(sT, 1) if (dc == DC - 1 and tt == NTT - 1)
                       else None)
        ctx.wait("scalar", sT, mc + 1)
        ctx.wait("scalar", sLd)
        for tt in range(NTT):
            slot = (mc * NTT + tt) % 2
            nv = mc * NTT + tt
            if nv >= 2:
                ctx.wait("scalar", sVh, nv - 1)
            ctx.op("scalar", lambda e, slot=slot, mc=mc, tt=tt,
                   base=base: e.activation(
                       out=stgf[:, slot * 512:slot * 512 + 512],
                       in_=pb[base + tt][:, 0:512], func=AF.Identity,
                       bias=par["b_out"][:, mc:mc + 1], scale=1.0),
                   inc=(sE, 1))
            ctx.wait("vector", sE)
            ctx.op("vector", lambda e, slot=slot, mc=mc, tt=tt:
                   e.tensor_add(
                       out=hbuf[:, mc * NT + tt * 512:
                                mc * NT + tt * 512 + 512],
                       in0=stgf[:, slot * 512:slot * 512 + 512],
                       in1=hbuf[:, mc * NT + tt * 512:
                                mc * NT + tt * 512 + 512]),
                   inc=(sVh, 1))

    ln_done = emit_ln_fm(
        ctx, consts, hbuf, sybuf, rows, pb, NT, LN_EPS,
        pre=[("scalar", sVh, 32), ("scalar", sZ, 1)], tag="s0")
    sFin = ctx.sem("fin")
    ctx.wait("vector", ln_done)
    for c in range(DC):
        ctx.op("vector", lambda e, c=c: e.tensor_scalar(
            out=hbuf[:, c * NT:(c + 1) * NT],
            in0=hbuf[:, c * NT:(c + 1) * NT],
            scalar1=par["lnw"][:, c:c + 1],
            scalar2=par["lnb"][:, c:c + 1],
            op0=ALU.mult, op1=ALU.add), inc=(sFin, 1))
    ctx.wait("sync", sFin)
    ctx.op("sync", lambda e: e.dma_start(
        out=io["s0"].rearrange("(c p) t -> p c t", p=128),
        in_=hbuf[:].rearrange("p (c t) -> p c t", c=DC)),
        inc=(sSt, 16))
    ctx.wait("sync", sSt)


def build_launch_D1():
    return _wrap(decl_D1, emit_D1)



def decl_D2(din, dmid, dout):
    din("x_fm", [D, NT])
    dmid("s0", [D, NT], F32)
    dmid("g", [D, NT], BF16)
    din("c_ones_bf", [128, 1])
    din("c_ones_row", [1, 128])
    dout("x1", [D, NT], F32)
    dout("xn1", [D, NT], F32)
    dout("xn1_bf", [D, NT], BF16)


def emit_D2(nc, ctx, io, en):
    xbuf = en(nc.sbuf_tensor("xbuf", [128, DC * NT], F32))
    sbuf0 = en(nc.sbuf_tensor("sbuf0", [128, DC * NT], F32))
    gbuf = en(nc.sbuf_tensor("gbuf", [128, DC * NT], BF16))
    rows = en(nc.sbuf_tensor("rows", [1, 2 * NT + 512], F32))
    ones_bf = en(nc.sbuf_tensor("ones_bf", [128, 1], BF16))
    ones_row = en(nc.sbuf_tensor("ones_row", [1, 128], F32))
    pb = [en(nc.psum_tensor(f"pb{i}", [128, 512], F32)) for i in range(4)]

    sLd = ctx.sem("ld")
    sLg = ctx.sem("ldg")
    sV = ctx.sem("v")
    sStX = ctx.sem("stx1")
    sStN = ctx.sem("stxn")
    sStB = ctx.sem("stxb")
    consts = dict(ones_bf=ones_bf, ones_row=ones_row)

    for nm, dst in (("x_fm", xbuf), ("s0", sbuf0)):
        ctx.op("sync", lambda e, nm=nm, dst=dst: e.dma_start(
            out=dst[:].rearrange("p (c t) -> p c t", c=DC),
            in_=io[nm].rearrange("(c p) t -> p c t", p=128)),
            inc=(sLd, 16))
    ctx.op("sync", lambda e: e.dma_start(
        out=gbuf[:].rearrange("p (c t) -> p c t", c=DC),
        in_=io["g"].rearrange("(c p) t -> p c t", p=128)),
        inc=(sLd, 16))
    ctx.op("sync", lambda e: e.dma_start(out=ones_row[:],
                                         in_=io["c_ones_row"]),
           inc=(sLd, 16))
    ctx.op("gpsimd", lambda e: e.dma_start(out=ones_bf[:],
                                           in_=io["c_ones_bf"]),
           inc=(sLg, 16))

    ctx.wait("vector", sLd)
    ctx.op("vector", lambda e: e.tensor_mul(
        out=sbuf0[:], in0=sbuf0[:], in1=gbuf[:]))
    ctx.op("vector", lambda e: e.tensor_add(
        out=xbuf[:], in0=xbuf[:], in1=sbuf0[:]), inc=(sV, 1))
    ctx.wait("sync", sV)
    ctx.op("sync", lambda e: e.dma_start(
        out=io["x1"].rearrange("(c p) t -> p c t", p=128),
        in_=xbuf[:].rearrange("p (c t) -> p c t", c=DC)),
        inc=(sStX, 16))
    ln_done = emit_ln_fm(
        ctx, consts, xbuf, gbuf, rows, pb, NT, LN_EPS,
        pre=[("scalar", sV, 1), ("tensor", sLd, 64),
             ("tensor", sLg, 16), ("vector", sStX, 16)], tag="1")
    sC = ctx.sem("cast")
    ctx.wait("sync", ln_done)
    ctx.op("sync", lambda e: e.dma_start(
        out=io["xn1"].rearrange("(c p) t -> p c t", p=128),
        in_=xbuf[:].rearrange("p (c t) -> p c t", c=DC)),
        inc=(sStN, 16))
    ctx.wait("scalar", ln_done)
    ctx.op("scalar", lambda e: e.activation(
        out=gbuf[:], in_=xbuf[:], func=AF.Identity), inc=(sC, 1))
    ctx.wait("sync", sC)
    ctx.op("sync", lambda e: e.dma_start(
        out=io["xn1_bf"].rearrange("(c p) t -> p c t", p=128),
        in_=gbuf[:].rearrange("p (c t) -> p c t", c=DC)),
        inc=(sStB, 16))
    ctx.wait("sync", sStX)
    ctx.wait("sync", sStN)
    ctx.wait("sync", sStB)


def build_launch_D2():
    return _wrap(decl_D2, emit_D2)



def decl_E(din, dmid, dout):
    dmid("xn1_bf", [D, NT], BF16)
    for nm in ("wq", "wk", "wv"):
        din(nm, [D, 2 * D])
    dout("Q", [2 * D, NT], BF16)
    dout("Kr", [2 * D, NT], BF16)
    dout("Vr", [NT, 2 * D], BF16)


def emit_E(nc, ctx, io, en):
    xnbuf = en(nc.sbuf_tensor("xnbuf", [128, DC * NT], BF16))
    wsb = en(nc.sbuf_tensor("wsb", [128, DC * 2048], BF16))
    stg = en(nc.sbuf_tensor("stg", [128, 8 * 512], BF16))
    pb = [en(nc.psum_tensor(f"pb{i}", [128, 512], F32)) for i in range(8)]

    sLd = ctx.sem("ld")
    sLw = [ctx.sem(f"ldw{i}") for i in range(3)]
    sT = ctx.sem("t")
    sE = ctx.sem("evac")
    stage = Stage(ctx, stg, 8, 512, "sg")

    ctx.op("sync", lambda e: e.dma_start(
        out=xnbuf[:].rearrange("p (c t) -> p c t", c=DC),
        in_=io["xn1_bf"].rearrange("(c p) t -> p c t", p=128)),
        inc=(sLd, 16))

    for wi, (wn, on) in enumerate((("wq", "Q"), ("wk", "Kr"),
                                   ("wv", "Vr"))):
        ctx.wait("gpsimd", sT)
        ctx.op("gpsimd", lambda e, wn=wn: e.dma_start(
            out=wsb[:].rearrange("p (c m) -> p c m", c=DC),
            in_=io[wn].rearrange("(c p) m -> p c m", p=128)),
            inc=(sLw[wi], 16))
        ctx.wait("tensor", sLw[wi])
        if wi == 0:
            ctx.wait("tensor", sLd, 16)
        if on != "Vr":
            hbm = io[on]
            for mc in range(16):
                base = (mc % 2) * NTT
                if mc >= 2:
                    ctx.wait("tensor", sE, sE.n - NTT)
                for dc in range(DC):
                    for tt in range(NTT):
                        ctx.op("tensor", lambda e, mc=mc, dc=dc, tt=tt,
                               base=base: e.matmul(
                                   pb[base + tt][:, 0:512],
                                   wsb[:, (dc * 16 + mc) * 128:
                                       (dc * 16 + mc) * 128 + 128],
                                   xnbuf[:, dc * NT + tt * 512:
                                         dc * NT + tt * 512 + 512],
                                   start=(dc == 0), stop=(dc == DC - 1)),
                               inc=(sT, 1)
                               if (dc == DC - 1 and tt == NTT - 1)
                               else None)
                ctx.wait("scalar", sT)
                for tt in range(NTT):
                    tile, slot = stage.next("scalar")
                    ctx.op("scalar", lambda e, tile=tile, tt=tt,
                           base=base: e.activation(
                               out=tile, in_=pb[base + tt][:, 0:512],
                               func=AF.Identity), inc=(sE, 1))
                    ctx.wait("sync", sE)
                    stage.store("sync", lambda e, tile=tile, hbm=hbm,
                                mc=mc, tt=tt: e.dma_start(
                                    out=hbm.rearrange(
                                        "(c p) t -> p c t", p=128)
                                    [:, mc:mc + 1,
                                     tt * 512:tt * 512 + 512],
                                    in_=tile), slot)
        else:
            for ts in range(16):
                base = (ts % 2) * NTT
                if ts >= 2:
                    ctx.wait("tensor", sE, sE.n - NTT)
                for ch in range(4):
                    for dc in range(DC):
                        ctx.op("tensor", lambda e, ts=ts, ch=ch, dc=dc,
                               base=base: e.matmul(
                                   pb[base + ch][:, 0:512],
                                   xnbuf[:, dc * NT + ts * 128:
                                         dc * NT + ts * 128 + 128],
                                   wsb[:, dc * 2048 + ch * 512:
                                       dc * 2048 + ch * 512 + 512],
                                   start=(dc == 0), stop=(dc == DC - 1)),
                               inc=(sT, 1)
                               if (dc == DC - 1 and ch == 3) else None)
                ctx.wait("scalar", sT)
                for ch in range(4):
                    tile, slot = stage.next("scalar")
                    ctx.op("scalar", lambda e, tile=tile, ch=ch,
                           base=base: e.activation(
                               out=tile, in_=pb[base + ch][:, 0:512],
                               func=AF.Identity), inc=(sE, 1))
                    ctx.wait("sync", sE)
                    stage.store("sync", lambda e, tile=tile, ts=ts,
                                ch=ch: e.dma_start(
                                    out=io["Vr"][ts * 128:ts * 128
                                                 + 128,
                                                 ch * 512:ch * 512
                                                 + 512],
                                    in_=tile), slot)
    stage.final_waits("sync")


def build_launch_E():
    return _wrap(decl_E, emit_E)





KCMAX = [8, 16, 24, 32]
MASKLO = [4 * min(OWN[0][i], OWN[1][i]) for i in range(4)]


def mask_table(role):
    out = np.zeros((128, 32 * 512), np.float32)
    pk = np.arange(128)[:, None]
    fq = np.arange(512)[None, :]
    for slot in range(4):
        jb = OWN[role][slot]
        lo = MASKLO[slot]
        for i, kc in enumerate(range(lo, KCMAX[slot])):
            if kc < 4 * jb:
                tile = np.ones((128, 512), np.float32)
            elif kc <= 4 * jb + 3:
                m = kc - 4 * jb
                tile = (pk <= fq - 128 * m).astype(np.float32)
            else:
                tile = np.zeros((128, 512), np.float32)
            out[:, (slot * 8 + i) * 512:(slot * 8 + i + 1) * 512] = tile
    return out


def e64c_mat():
    E = np.zeros((128, 2), np.float32)
    E[:64, 0] = 1.0
    E[64:, 1] = 1.0
    return E


def decl_F(din, dmid, dout, dbg=False):
    dmid("Q", [2 * D, NT], BF16)
    dmid("Kf", [16 * D, 512], BF16)
    dmid("Vf", [T, 2 * D], BF16)
    din("lam4", [16, 4 * 64])
    din("rsc", [128, 1])
    din("wo", [2 * D, D])
    din("c_maskt", [128, 32 * 512])
    din("c_id16", [16, 16])
    din("c_e64c", [128, 2])
    din("c_cfix", [128, 2])
    din("c_ones_bf", [128, 1])
    din("c_ones_row", [1, 128])
    dout("s1", [D, NT], BF16)
    if dbg:
        dout("dbg_oall", [128, 16 * NT], BF16)
        dout("dbg_p", [128, 1024], BF16)
        dout("dbg_b2", [128, 2], F32)
        dout("dbg_t", [128, 1024], F32)
        dout("dbg_r2", [2, 64], F32)
        dout("dbg_den", [2, 512], F32)
        dout("dbg_brow", [1, 2], F32)
        dout("dbg_bias2", [128, 2], F32)


def emit_F(nc, ctx, io, en, dbg=False, nheads=16, mode='full'):
    oall = en(nc.sbuf_tensor("oall", [128, 16 * NT], BF16))
    wsb = en(nc.sbuf_tensor("wsb", [128, 16 * 1024], BF16))
    masks = en(nc.sbuf_tensor("masks", [128, 32 * 512], BF16))
    qh = en(nc.sbuf_tensor("qh", [128, 2 * NT], BF16))
    kh = en(nc.sbuf_tensor("kh", [128, 2 * T], BF16))
    vh = en(nc.sbuf_tensor("vh", [128, 2 * T], BF16))
    pbuf = en(nc.sbuf_tensor("pbuf", [128, 4 * 512], BF16))
    sqs = en(nc.sbuf_tensor("sqs", [128, 2 * 512], BF16))
    combsq = en(nc.sbuf_tensor("combsq", [128, 512], BF16))
    t1b = en(nc.sbuf_tensor("t1b", [128, 512], F32))
    t2b = en(nc.sbuf_tensor("t2b", [128, 512], F32))
    d1s = en(nc.sbuf_tensor("d1s", [128, 512], F32))
    d2s = en(nc.sbuf_tensor("d2s", [128, 512], F32))
    combf = en(nc.sbuf_tensor("combf", [128, 512], F32))
    lam4 = en(nc.sbuf_tensor("lam4s", [16, 4 * 64], F32))
    lamt = en(nc.sbuf_tensor("lamt", [16, 4], F32))
    lrow = en(nc.sbuf_tensor("lrow", [1, 16], F32))
    lamnb = en(nc.sbuf_tensor("lamnb", [128, 16], F32))
    bias2 = en(nc.sbuf_tensor("bias2", [128, 2], F32))
    brow = en(nc.sbuf_tensor("brow", [1, 2], F32))
    rs02 = en(nc.sbuf_tensor("rs02", [128, 1], F32))
    id16 = en(nc.sbuf_tensor("id16", [16, 16], F32))
    e64c = en(nc.sbuf_tensor("e64c", [128, 2], BF16))
    ones_bf = en(nc.sbuf_tensor("ones_bf", [128, 1], BF16))
    ones_row = en(nc.sbuf_tensor("ones_row", [1, 128], F32))
    rows2 = en(nc.sbuf_tensor("rows2", [2, 64], F32))
    rrow = en(nc.sbuf_tensor("rrow", [1, 512], F32))
    stg = en(nc.sbuf_tensor("stg", [128, 4 * 512], BF16))
    pb = [en(nc.psum_tensor(f"pb{i}", [128, 512], F32)) for i in range(8)]

    sLd = ctx.sem("ld")
    sLg = ctx.sem("ldg")
    sLq = ctx.sem("ldq")
    sLk = ctx.sem("ldk")
    sLv = ctx.sem("ldv")
    sT = ctx.sem("t")
    sV = ctx.sem("v")
    sS = ctx.sem("s")
    sPk = ctx.sem("pk")
    sOF = ctx.sem("of")
    sE = ctx.sem("evac")
    stage = Stage(ctx, stg, 4, 512, "sg")

    ctx.op("sync", lambda e: e.dma_start(out=lam4[:], in_=io["lam4"]),
           inc=(sLd, 16))
    ctx.op("sync", lambda e: e.dma_start(out=rs02[:], in_=io["rsc"]),
           inc=(sLd, 16))
    ctx.op("sync", lambda e: e.dma_start(out=id16[:], in_=io["c_id16"]),
           inc=(sLd, 16))
    ctx.op("sync", lambda e: e.dma_start(out=ones_row[:],
                                         in_=io["c_ones_row"]),
           inc=(sLd, 16))
    ctx.op("sync", lambda e: e.dma_start(out=bias2[:],
                                         in_=io["c_cfix"]),
           inc=(sLd, 16))
    ctx.op("gpsimd", lambda e: e.dma_start(out=masks[:],
                                           in_=io["c_maskt"]),
           inc=(sLg, 16))
    ctx.op("gpsimd", lambda e: e.dma_start(out=e64c[:],
                                           in_=io["c_e64c"]),
           inc=(sLg, 16))
    ctx.op("gpsimd", lambda e: e.dma_start(out=ones_bf[:],
                                           in_=io["c_ones_bf"]),
           inc=(sLg, 16))
    ctx.op("gpsimd", lambda e: e.dma_start(
        out=wsb[:].rearrange("p (c m) -> p c m", c=16),
        in_=io["wo"].rearrange("(c p) m -> p c m", p=128)),
        inc=(sLg, 16))

    if dbg:
        ctx.op("vector", lambda e: e.memset(rows2[0:2, 0:64], 0.0))
    ctx.wait("vector", sLd)
    ctx.op("vector", lambda e: e.tensor_mul(
        out=lam4[:, 0:64], in0=lam4[:, 0:64], in1=lam4[:, 64:128]),
        inc=(sV, 1))
    ctx.op("vector", lambda e: e.tensor_mul(
        out=lam4[:, 128:192], in0=lam4[:, 128:192],
        in1=lam4[:, 192:256]), inc=(sV, 1))
    ctx.wait("vector", sV)
    ctx.op("vector", lambda e: e.reduce_sum(
        out=lamt[0:16, 0:1], in_=lam4[0:16, 0:64], axis=AX.X),
        inc=(sV, 1))
    ctx.op("vector", lambda e: e.reduce_sum(
        out=lamt[0:16, 1:2], in_=lam4[0:16, 128:192], axis=AX.X),
        inc=(sV, 1))
    ctx.wait("scalar", sV)
    ctx.op("scalar", lambda e: e.activation(
        out=lamt[0:16, 0:2], in_=lamt[0:16, 0:2], func=AF.Exp),
        inc=(sS, 1))
    ctx.wait("vector", sS)
    ctx.op("vector", lambda e: e.tensor_sub(
        out=lamt[0:16, 2:3], in0=lamt[0:16, 1:2], in1=lamt[0:16, 0:1]),
        inc=(sV, 1))
    ctx.wait("vector", sV)
    ctx.op("vector", lambda e: e.tensor_scalar_add(
        out=lamt[0:16, 2:3], in0=lamt[0:16, 2:3], scalar1=-0.8),
        inc=(sV, 1))
    ctx.wait("tensor", sV)
    ctx.wait("tensor", sLd)
    ctx.op("tensor", lambda e: e.transpose(
        pb[0][0:1, 0:16], lamt[0:16, 2:3], id16[:]), inc=(sT, 1))
    ctx.wait("vector", sT)
    ctx.op("vector", lambda e: e.tensor_copy(out=lrow[:],
                                             in_=pb[0][0:1, 0:16]),
           inc=(sV, 1))
    ctx.wait("tensor", sV)
    ctx.op("tensor", lambda e: e.matmul(
        pb[1][:, 0:16], ones_row[0:1, :], lrow[:], start=True,
        stop=True), inc=(sT, 1))
    ctx.wait("vector", sT)
    ctx.op("vector", lambda e: e.tensor_copy(out=lamnb[:],
                                             in_=pb[1][:, 0:16]),
           inc=(sV, 1))
    ctx.wait("vector", sV)
    ctx.op("vector", lambda e: e.tensor_scalar_mul(
        out=rs02[:], in0=rs02[:], scalar1=0.2), inc=(sV, 1))

    ctx.wait("tensor", sLg)
    ctx.wait("vector", sLg)
    t_end = {}
    pk_end = {}
    lqv = {}

    def emit_loads(h):
        hs = h % 2
        if h >= 2:
            ctx.wait("scalar", sT, t_end[h - 2])
            ctx.wait("scalar", sPk, pk_end[h - 2])
        ctx.op("scalar", lambda e, h=h, hs=hs: e.dma_start(
            out=qh[:, hs * NT:hs * NT + NT],
            in_=io["Q"][h * 128:h * 128 + 128, :]),
            inc=(sLq, 16))
        ctx.op("scalar", lambda e, h=h, hs=hs: e.dma_start(
            out=kh[:, hs * T:hs * T + T].rearrange(
                "p (k t) -> p k t", k=8),
            in_=io["Kf"].rearrange("(k c) t -> c k t", c=2048)
            [h * 128:h * 128 + 128]), inc=(sLk, 16))
        ctx.op("scalar", lambda e, h=h, hs=hs: e.dma_start(
            out=vh[:, hs * T:hs * T + T].rearrange(
                "p (kc c) -> p kc c", c=128),
            in_=io["Vf"][:, h * 128:h * 128 + 128].rearrange(
                "(kc p) c -> p kc c", p=128)), inc=(sLv, 16))
        lqv[h] = (sLq.n, sLk.n, sLv.n)

    ctx.wait("scalar", sLd)
    emit_loads(0)
    for h in range(nheads):
        hs = h % 2
        qho = hs * NT
        kho = hs * T
        ctx.wait("tensor", sLq, lqv[h][0])
        ctx.wait("tensor", sLk, lqv[h][1])
        ctx.wait("tensor", sLv, lqv[h][2])
        if h + 1 < nheads:
            emit_loads(h + 1)
        def mk_slot(slot, hs=hs, kho=kho, qho=qho, h=h):
            qo = slot * 512
            kcmax = KCMAX[slot]
            lo = MASKLO[slot]
            st = {"tS": {}, "eS": {}, "mV": {}, "base_pk": 0}

            def begin():
                st["base_pk"] = sPk.n
                ctx.wait("scalar", sV)
                ctx.wait("tensor", sS)

            def emit_S(kc):
                sb1, sb2 = kc % 2, 2 + kc % 2
                if kc >= 2:
                    ctx.wait("tensor", sS, st["eS"][kc - 2])
                for ci, (sb_, co) in enumerate(((sb1, 0), (sb2, 64))):
                    ctx.op("tensor", lambda e, kc=kc, sb_=sb_, co=co:
                           e.matmul(
                               pb[sb_][:, 0:512],
                               kh[co:co + 64, kho + kc * 128:
                                  kho + kc * 128 + 128],
                               qh[co:co + 64, qho + qo:qho + qo + 512],
                               start=True, stop=True), inc=(sT, 1))
                st["tS"][kc] = sT.n

            def emit_exp(kc):
                if kc >= 2:
                    ctx.wait("scalar", sPk, st["base_pk"] + kc - 1)
                elif st["base_pk"] > 0:
                    ctx.wait("scalar", sPk, st["base_pk"])
                ctx.wait("scalar", sT, st["tS"][kc])
                sb1, sb2 = kc % 2, 2 + kc % 2
                for ci, (sb_, ps) in enumerate(((sb1, 0), (sb2, 2))):
                    pslot = ps + kc % 2
                    ctx.op("scalar", lambda e, sb_=sb_, pslot=pslot,
                           ci=ci: e.activation(
                               out=pbuf[:, pslot * 512:pslot * 512
                                        + 512],
                               in_=pb[sb_][:, 0:512], func=AF.Exp,
                               bias=bias2[:, ci:ci + 1],
                               scale=1.0 / 8.0), inc=(sS, 1))
                st["eS"][kc] = sS.n
                if kc >= lo:
                    mi = slot * 8 + (kc - lo)
                    ctx.wait("vector", sS, st["eS"][kc])
                    for pslot in (kc % 2, 2 + kc % 2):
                        ctx.op("vector", lambda e, pslot=pslot, mi=mi:
                               e.tensor_mul(
                                   out=pbuf[:, pslot * 512:pslot * 512
                                            + 512],
                                   in0=pbuf[:, pslot * 512:pslot * 512
                                            + 512],
                                   in1=masks[:, mi * 512:mi * 512
                                             + 512]), inc=(sV, 1))
                    st["mV"][kc] = sV.n
                else:
                    st["mV"][kc] = None

            def den_pv(kc):
                if kc == 0:
                    ctx.wait("tensor", sOF)
                if st["mV"][kc] is not None:
                    ctx.wait("tensor", sV, st["mV"][kc])
                else:
                    ctx.wait("tensor", sS, st["eS"][kc])
                for ci, (pslot, dbank, obank) in enumerate(
                        ((kc % 2, 6, 4), (2 + kc % 2, 7, 5))):
                    ctx.op("tensor", lambda e, pslot=pslot, dbank=dbank,
                           kc=kc: e.matmul(
                               pb[dbank][0:1, 0:512], ones_bf[:, 0:1],
                               pbuf[:, pslot * 512:pslot * 512 + 512],
                               start=(kc == 0), stop=(kc == kcmax - 1)))
                    ctx.op("tensor", lambda e, pslot=pslot, obank=obank,
                           kc=kc: e.matmul(
                               pb[obank][:, 0:512],
                               vh[:, kho + kc * 128:kho + kc * 128
                                  + 128],
                               pbuf[:, pslot * 512:pslot * 512 + 512],
                               start=(kc == 0), stop=(kc == kcmax - 1)),
                           inc=(sPk, 1) if ci == 1 else None)

            def epilogue():
                ctx.wait("vector", sPk)
                ctx.op("vector", lambda e: e.reciprocal(
                    out=d1s[0:1, 0:512], in_=pb[6][0:1, 0:512]))
                ctx.op("vector", lambda e: e.reciprocal(
                    out=d2s[0:1, 0:512], in_=pb[7][0:1, 0:512]),
                    inc=(sV, 1))
                ctx.wait("tensor", sV)
                ctx.op("tensor", lambda e: e.matmul(
                    pb[6][:, 0:512], ones_row[0:1, :],
                    d1s[0:1, 0:512], start=True, stop=True))
                ctx.op("tensor", lambda e: e.matmul(
                    pb[7][:, 0:512], ones_row[0:1, :],
                    d2s[0:1, 0:512], start=True, stop=True),
                    inc=(sT, 1))
                ctx.wait("vector", sT)
                ctx.op("vector", lambda e: e.tensor_copy(
                    out=d1s[:], in_=pb[6][:, 0:512]))
                ctx.op("vector", lambda e: e.tensor_copy(
                    out=d2s[:], in_=pb[7][:, 0:512]))
                ctx.op("vector", lambda e: e.tensor_mul(
                    out=t1b[:], in0=pb[4][:, 0:512], in1=d1s[:]))
                ctx.op("vector", lambda e: e.tensor_mul(
                    out=t2b[:], in0=pb[5][:, 0:512], in1=d2s[:]))
                if mode == "full":
                    ctx.op("vector", lambda e, h=h:
                           e.scalar_tensor_tensor(
                               out=combf[:], in0=t2b[:],
                               scalar=lamnb[:, h:h + 1], in1=t1b[:],
                               op0=ALU.mult, op1=ALU.add), inc=(sV, 1))
                elif mode == "a1":
                    ctx.op("vector", lambda e: e.tensor_copy(
                        out=combf[:], in_=t1b[:]), inc=(sV, 1))
                else:
                    ctx.op("vector", lambda e: e.tensor_copy(
                        out=combf[:], in_=pb[4][:, 0:512]), inc=(sV, 1))
                ctx.wait("scalar", sV)
                ctx.op("scalar", lambda e: e.activation(
                    out=combsq[:], in_=combf[:], func=AF.Square),
                    inc=(sS, 1))
                ctx.wait("tensor", sS)
                ctx.op("tensor", lambda e: e.matmul(
                    pb[4][0:1, 0:512], ones_bf[:, 0:1], combsq[:],
                    start=True, stop=True), inc=(sT, 1))
                ctx.wait("vector", sT)
                ctx.op("vector", lambda e: e.tensor_scalar(
                    out=rrow[:], in0=pb[4][0:1, 0:512],
                    scalar1=1.0 / 128.0, scalar2=float(LN_EPS),
                    op0=ALU.mult, op1=ALU.add))
                ctx.op("vector", lambda e: e.reciprocal(
                    out=rrow[:], in_=rrow[:]), inc=(sV, 1))
                ctx.wait("scalar", sV)
                ctx.op("scalar", lambda e: e.activation(
                    out=rrow[:], in_=rrow[:], func=AF.Sqrt),
                    inc=(sS, 1))
                ctx.wait("tensor", sS)
                ctx.op("tensor", lambda e: e.matmul(
                    pb[5][:, 0:512], ones_row[0:1, :], rrow[:],
                    start=True, stop=True), inc=(sT, 1))
                ctx.wait("vector", sT)
                ctx.op("vector", lambda e, h=h, qo=qo:
                       e.scalar_tensor_tensor(
                           out=oall[:, h * NT + qo:h * NT + qo + 512],
                           in0=combf[:], scalar=rs02[:, 0:1],
                           in1=pb[5][:, 0:512], op0=ALU.mult,
                           op1=ALU.mult), inc=(sOF, 1))

            st["fns"] = (begin, emit_S, emit_exp, den_pv, epilogue)
            st["kcmax"] = kcmax
            return st

        slots_o = [mk_slot(s_) for s_ in range(NTT)]
        sl0 = slots_o[0]["fns"]
        sl0[0]()
        sl0[1](0)
        sl0[2](0)
        emitted = {0: 0}
        for slot in range(NTT):
            begin, emit_S, emit_exp, den_pv, epilogue = \
                slots_o[slot]["fns"]
            kcmax = slots_o[slot]["kcmax"]
            for kc in range(kcmax):
                if kc + 1 < kcmax and emitted[slot] < kc + 1:
                    emit_S(kc + 1)
                    emit_exp(kc + 1)
                    emitted[slot] = kc + 1
                den_pv(kc)
            if slot + 1 < NTT:
                nb, ns, ne, _, _ = slots_o[slot + 1]["fns"]
                nb()
                ns(0)
                ne(0)
                ns(1)
                ne(1)
                emitted[slot + 1] = 1
            epilogue()
        t_end[h] = sT.n
        pk_end[h] = sPk.n

    ctx.wait("tensor", sOF)
    if dbg:
        sDbg = ctx.sem("dbg")
        ctx.wait("sync", sOF)
        ctx.op("sync", lambda e: e.dma_start(
            out=io["dbg_oall"], in_=oall[:]), inc=(sDbg, 16))
        ctx.op("sync", lambda e: e.dma_start(
            out=io["dbg_b2"][:, 0:2], in_=lamnb[:, 0:2]),
            inc=(sDbg, 16))
        ctx.op("gpsimd", lambda e: e.dma_start(
            out=io["dbg_p"][0:16, 0:4], in_=lamt[:]), inc=(sDbg, 16))
        ctx.op("gpsimd", lambda e: e.dma_start(
            out=io["dbg_p"][0:1, 8:24], in_=lrow[:]), inc=(sDbg, 16))
        ctx.wait("sync", sDbg)
    sTo = ctx.sem("to")
    sEo = ctx.sem("eo")
    for mc in range(DC):
        base = (mc % 2) * NTT
        if mc >= 2:
            ctx.wait("tensor", sEo, (mc - 1) * NTT)
        for hc in range(16):
            for tt in range(NTT):
                ctx.op("tensor", lambda e, mc=mc, hc=hc, tt=tt,
                       base=base: e.matmul(
                           pb[base + tt][:, 0:512],
                           wsb[:, (hc * DC + mc) * 128:
                               (hc * DC + mc) * 128 + 128],
                           oall[:, hc * NT + tt * 512:
                                hc * NT + tt * 512 + 512],
                           start=(hc == 0), stop=(hc == 15)),
                       inc=(sTo, 1) if (hc == 15 and tt == NTT - 1)
                       else None)
        ctx.wait("scalar", sTo, mc + 1)
        for tt in range(NTT):
            tile, slot = stage.next("scalar")
            ctx.op("scalar", lambda e, tile=tile, tt=tt, base=base:
                   e.activation(out=tile, in_=pb[base + tt][:, 0:512],
                                func=AF.Identity), inc=(sEo, 1))
            ctx.wait("sync", sEo)
            stage.store("sync", lambda e, tile=tile, mc=mc, tt=tt:
                        e.dma_start(
                            out=io["s1"].rearrange("(c p) t -> p c t",
                                                   p=128)
                            [:, mc:mc + 1, tt * 512:tt * 512 + 512],
                            in_=tile), slot)
    stage.final_waits("sync")


def build_launch_F(dbg=False, nheads=16, mode='full'):
    return _wrap(
        lambda di, dm, do: decl_F(di, dm, do, dbg=dbg),
        lambda nc, ctx, io, en: emit_F(nc, ctx, io, en, dbg=dbg, nheads=nheads, mode=mode))


def stage_inputs_F(core, W, Q, Kf, Vf):
    r = core % 2
    lam4 = np.concatenate([W["L1_attn_lambda_q1"], W["L1_attn_lambda_k1"],
                           W["L1_attn_lambda_q2"], W["L1_attn_lambda_k2"]],
                          axis=1)
    return dict(Q=Q, Kf=Kf, Vf=Vf, lam4=np.ascontiguousarray(lam4),
                rsc=np.ascontiguousarray(
                    W["L1_attn_rms_scale"].reshape(128, 1)),
                wo=np.ascontiguousarray(W["L1_attn_Wo"]),
                c_maskt=mask_table(r),
                c_id16=np.eye(16, dtype=np.float32),
                c_e64c=e64c_mat(),
                c_cfix=np.full((128, 2), -4.0, np.float32),
                c_ones_bf=np.ones((128, 1), np.float32),
                c_ones_row=np.ones((1, 128), np.float32))





def rope_consts(rows):
    inv = 1.0 / 10000.0 ** (np.arange(0, 64, 2, dtype=np.float64) / 64)
    f = np.outer(rows.astype(np.float64), inv)
    cos = np.cos(f).astype(np.float32)
    sin = np.sin(f).astype(np.float32)
    C = np.zeros((128, NT), np.float32)
    S = np.zeros((128, NT), np.float32)
    for p in range(128):
        i = p % 32
        up = (p % 64) >= 32
        C[p] = cos[:, i]
        S[p] = sin[:, i] if up else -sin[:, i]
    return C, S


def e2t_mat():
    E = np.zeros((2, 128), np.float32)
    E[0, :64] = 1.0
    E[1, 64:] = 1.0
    return E


def swap_mat():
    Smat = np.zeros((128, 128), np.float32)
    for p in range(128):
        blk, off = p // 64, p % 64
        src = blk * 64 + ((off + 32) % 64)
        Smat[src, p] = 1.0
    return Smat


def decl_G1(din, dmid, dout):
    dmid("xn1", [D, NT], F32)
    din("ln1w", [128, 8])
    din("wq", [D, D])
    din("bq", [128, 8])
    din("wk", [D, 256])
    din("bk", [128, 2])
    din("wv", [D, 256])
    din("bvr", [1, 256])
    din("qnw", [128, 1])
    din("knw", [128, 1])
    din("c_cos", [128, NT])
    din("c_sin", [128, NT])
    din("c_e2t", [2, 128])
    din("c_e64c", [128, 2])
    din("c_swp", [128, 128])
    din("c_ones_bf", [128, 1])
    din("c_ones_row", [1, 128])
    dout("qg", [D, NT], BF16)
    dout("kg", [256, NT], BF16)
    dout("vg", [NT, 256], BF16)


def emit_G1(nc, ctx, io, en):
    xbuf = en(nc.sbuf_tensor("xbuf", [128, DC * NT], F32))
    hbf = en(nc.sbuf_tensor("hbf", [128, DC * NT], BF16))
    wsb = en(nc.sbuf_tensor("wsb", [128, DC * 1024], BF16))
    wkv = en(nc.sbuf_tensor("wkv", [128, DC * 512], BF16))
    kbuf = en(nc.sbuf_tensor("kbuf", [128, 2 * NT], F32))
    cosb = en(nc.sbuf_tensor("cosb", [128, NT], F32))
    sinb = en(nc.sbuf_tensor("sinb", [128, NT], F32))
    qcs = en(nc.sbuf_tensor("qcs", [128, NT], F32))
    sqsc = en(nc.sbuf_tensor("sqsc", [128, NT], BF16))
    rows2 = en(nc.sbuf_tensor("rows2", [2, NT], F32))
    rows = en(nc.sbuf_tensor("rows", [1, 2 * NT + 512], F32))
    par = {nm: en(nc.sbuf_tensor(nm + "_sb", [128, w], F32))
           for nm, w in (("ln1w", 8), ("bq", 8), ("bk", 2),
                         ("qnw", 1), ("knw", 1))}
    bvr = en(nc.sbuf_tensor("bvr_sb", [1, 256], BF16))
    e2t = en(nc.sbuf_tensor("e2t", [2, 128], F32))
    e64c = en(nc.sbuf_tensor("e64c", [128, 2], BF16))
    swp = en(nc.sbuf_tensor("swp", [128, 128], F32))
    ones_bf = en(nc.sbuf_tensor("ones_bf", [128, 1], BF16))
    ones_row = en(nc.sbuf_tensor("ones_row", [1, 128], F32))
    ones_rbf = en(nc.sbuf_tensor("ones_rbf", [1, 128], BF16))
    stg = en(nc.sbuf_tensor("stg", [128, 4 * 512], BF16))
    pb = [en(nc.psum_tensor(f"pb{i}", [128, 512], F32)) for i in range(8)]

    sLd = ctx.sem("ld")
    sLg = ctx.sem("ldg")
    sT = ctx.sem("t")
    sV = ctx.sem("v")
    sS = ctx.sem("s")
    consts = dict(ones_bf=ones_bf, ones_row=ones_row)
    stage = Stage(ctx, stg, 4, 512, "sg")

    ctx.op("gpsimd", lambda e: e.dma_start(out=ones_rbf[:],
                                           in_=io["c_ones_row"]),
           inc=(sLg, 16))
    ctx.op("sync", lambda e: e.dma_start(
        out=xbuf[:].rearrange("p (c t) -> p c t", c=DC),
        in_=io["xn1"].rearrange("(c p) t -> p c t", p=128)),
        inc=(sLd, 16))
    for nm in ("ln1w", "bq", "bk", "qnw", "knw"):
        ctx.op("sync", lambda e, nm=nm: e.dma_start(
            out=par[nm][:], in_=io[nm]), inc=(sLd, 16))
    ctx.op("sync", lambda e: e.dma_start(out=cosb[:], in_=io["c_cos"]),
           inc=(sLd, 16))
    ctx.op("sync", lambda e: e.dma_start(out=sinb[:], in_=io["c_sin"]),
           inc=(sLd, 16))
    ctx.op("sync", lambda e: e.dma_start(out=ones_row[:],
                                         in_=io["c_ones_row"]),
           inc=(sLd, 16))
    ctx.op("sync", lambda e: e.dma_start(out=e2t[:], in_=io["c_e2t"]),
           inc=(sLd, 16))
    ctx.op("sync", lambda e: e.dma_start(out=swp[:], in_=io["c_swp"]),
           inc=(sLd, 16))
    for nm, dst in (("c_ones_bf", ones_bf), ("c_e64c", e64c),
                    ("bvr", bvr)):
        ctx.op("gpsimd", lambda e, nm=nm, dst=dst: e.dma_start(
            out=dst[:], in_=io[nm]), inc=(sLg, 16))
    ctx.op("gpsimd", lambda e: e.dma_start(
        out=wkv[:, 0:DC * 256].rearrange("p (c m) -> p c m", c=DC),
        in_=io["wk"].rearrange("(c p) m -> p c m", p=128)),
        inc=(sLg, 16))
    ctx.op("gpsimd", lambda e: e.dma_start(
        out=wkv[:, DC * 256:DC * 512].rearrange("p (c m) -> p c m",
                                                c=DC),
        in_=io["wv"].rearrange("(c p) m -> p c m", p=128)),
        inc=(sLg, 16))
    ctx.op("gpsimd", lambda e: e.dma_start(
        out=wsb[:].rearrange("p (c m) -> p c m", c=DC),
        in_=io["wq"].rearrange("(c p) m -> p c m", p=128)),
        inc=(sLg, 16))

    ln_done = emit_ln_fm(ctx, consts, xbuf, hbf, rows, pb, NT, RMS_EPS,
                         pre=[("scalar", sLd, None), ("tensor", sLg,
                                                      None)],
                         tag="r1", rms_only=True)
    sW = ctx.sem("wapply")
    sHbf = ctx.sem("hbfs")
    ctx.wait("vector", ln_done)
    ctx.wait("vector", sLd)
    for c in range(DC):
        ctx.op("vector", lambda e, c=c: e.tensor_scalar_mul(
            out=xbuf[:, c * NT:(c + 1) * NT],
            in0=xbuf[:, c * NT:(c + 1) * NT],
            scalar1=par["ln1w"][:, c:c + 1]), inc=(sW, 1))
    ctx.wait("scalar", sW)
    ctx.op("scalar", lambda e: e.activation(
        out=hbf[:], in_=xbuf[:], func=AF.Identity), inc=(sHbf, 1))

    def rope_norm(buf, nchunks, wvec, hbm, tagr):
        sTx = ctx.sem("t" + tagr)
        sEx = ctx.sem("e" + tagr)
        ctx.wait("vector", sLd)
        for c in range(nchunks):
            co = c * NT
            for tt in range(NTT):
                bank = 4 + tt % 2
                if tt >= 2:
                    ctx.wait("tensor", sV)
                ctx.op("tensor", lambda e, co=co, tt=tt, buf=buf,
                       bank=bank: e.matmul(
                           pb[bank][:, 0:512], swp[:],
                           buf[:, co + tt * 512:co + tt * 512 + 512],
                           start=True, stop=True), inc=(sTx, 1))
                ctx.wait("vector", sTx)
                ctx.op("vector", lambda e, tt=tt, bank=bank:
                       e.tensor_mul(
                           out=qcs[:, tt * 512:tt * 512 + 512],
                           in0=pb[bank][:, 0:512],
                           in1=sinb[:, tt * 512:tt * 512 + 512]),
                       inc=(sV, 1))
            ctx.op("vector", lambda e, co=co, buf=buf: e.tensor_mul(
                out=buf[:, co:co + NT], in0=buf[:, co:co + NT],
                in1=cosb[:]))
            ctx.op("vector", lambda e, co=co, buf=buf: e.tensor_add(
                out=buf[:, co:co + NT], in0=buf[:, co:co + NT],
                in1=qcs[:, 0:NT]), inc=(sV, 1))
            ctx.wait("scalar", sV)
            ctx.op("scalar", lambda e, co=co, buf=buf: e.activation(
                out=sqsc[:, 0:NT], in_=buf[:, co:co + NT],
                func=AF.Square), inc=(sS, 1))
            ctx.wait("tensor", sS)
            for tt in range(NTT):
                if tt >= 1 or c >= 1:
                    ctx.wait("tensor", sV)
                ctx.op("tensor", lambda e, tt=tt: e.matmul(
                    pb[6][0:2, 0:512], e64c[:],
                    sqsc[:, tt * 512:tt * 512 + 512],
                    start=True, stop=True), inc=(sTx, 1))
                ctx.wait("vector", sTx)
                ctx.op("vector", lambda e, tt=tt: e.tensor_scalar(
                    out=rows2[0:2, tt * 512:tt * 512 + 512],
                    in0=pb[6][0:2, 0:512], scalar1=1.0 / 64.0,
                    scalar2=float(RMS_EPS), op0=ALU.mult, op1=ALU.add))
                ctx.op("vector", lambda e, tt=tt: e.reciprocal(
                    out=rows2[0:2, tt * 512:tt * 512 + 512],
                    in_=rows2[0:2, tt * 512:tt * 512 + 512]),
                    inc=(sV, 1))
            ctx.wait("scalar", sV)
            ctx.op("scalar", lambda e: e.activation(
                out=rows2[0:2, 0:NT], in_=rows2[0:2, 0:NT],
                func=AF.Sqrt), inc=(sS, 1))
            ctx.wait("tensor", sS)
            for tt in range(NTT):
                ctx.wait("tensor", sEx)
                ctx.op("tensor", lambda e, tt=tt: e.matmul(
                    pb[7][:, 0:512], e2t[:],
                    rows2[0:2, tt * 512:tt * 512 + 512],
                    start=True, stop=True), inc=(sTx, 1))
                ctx.wait("vector", sTx)
                tile, slot = stage.next("vector")
                ctx.op("vector", lambda e, co=co, tt=tt, buf=buf,
                       wvec=wvec, tile=tile: e.scalar_tensor_tensor(
                           out=tile,
                           in0=buf[:, co + tt * 512:co + tt * 512 + 512],
                           scalar=wvec[:, 0:1], in1=pb[7][:, 0:512],
                           op0=ALU.mult, op1=ALU.mult), inc=(sEx, 1))
                ctx.wait("sync", sEx)
                stage.store("sync", lambda e, c=c, tt=tt, tile=tile,
                            hbm=hbm: e.dma_start(
                                out=hbm.rearrange("(c p) t -> p c t",
                                                  p=128)
                                [:, c:c + 1, tt * 512:tt * 512 + 512],
                                in_=tile), slot)
        return sEx

    sTk = ctx.sem("tk")
    sEk = ctx.sem("ek")
    ctx.wait("tensor", sHbf)
    ctx.wait("tensor", sLg)
    for mc in range(2):
        for dc in range(DC):
            for tt in range(NTT):
                ctx.op("tensor", lambda e, mc=mc, dc=dc, tt=tt:
                       e.matmul(
                           pb[mc * NTT + tt][:, 0:512],
                           wkv[:, (dc * 2 + mc) * 128:
                               (dc * 2 + mc) * 128 + 128],
                           hbf[:, dc * NT + tt * 512:
                               dc * NT + tt * 512 + 512],
                           start=(dc == 0), stop=(dc == DC - 1)),
                       inc=(sTk, 1) if (dc == DC - 1 and tt == NTT - 1)
                       else None)
    ctx.wait("vector", sTk, 2)
    for mc in range(2):
        for tt in range(NTT):
            ctx.op("vector", lambda e, mc=mc, tt=tt: e.tensor_scalar_add(
                out=kbuf[:, mc * NT + tt * 512:mc * NT + tt * 512 + 512],
                in0=pb[mc * NTT + tt][:, 0:512],
                scalar1=par["bk"][:, mc:mc + 1]), inc=(sEk, 1))
    ctx.wait("vector", sEk)
    ctx.wait("tensor", sEk)
    sExk = rope_norm(kbuf, 2, par["knw"], io["kg"], "rk")

    sTv = ctx.sem("tv")
    sEv = ctx.sem("ev")
    for ts in range(16):
        bank = ts % 4
        if ts >= 4:
            ctx.wait("tensor", sEv, ts - 3)
        for dc in range(DC):
            ctx.op("tensor", lambda e, ts=ts, dc=dc, bank=bank:
                   e.matmul(
                       pb[bank][:, 0:256],
                       hbf[:, dc * NT + ts * 128:dc * NT + ts * 128
                           + 128],
                       wkv[:, DC * 256 + dc * 256:DC * 256 + dc * 256
                           + 256],
                       start=(dc == 0), stop=False))
        ctx.op("tensor", lambda e, ts=ts, bank=bank: e.matmul(
            pb[bank][:, 0:256], ones_rbf[0:1, :], bvr[:],
            start=False, stop=True), inc=(sTv, 1))
        ctx.wait("scalar", sTv)
        tile, slot = stage.next("scalar")
        ctx.op("scalar", lambda e, bank=bank, tile=tile: e.activation(
            out=tile[:, 0:256], in_=pb[bank][:, 0:256],
            func=AF.Identity), inc=(sEv, 1))
        ctx.wait("sync", sEv)
        stage.store("sync", lambda e, ts=ts, tile=tile: e.dma_start(
            out=io["vg"][ts * 128:ts * 128 + 128, :],
            in_=tile[:, 0:256]), slot)

    sTq = ctx.sem("tq")
    sEq = ctx.sem("eq")
    ctx.wait("tensor", sExk)
    ctx.wait("tensor", sEv)
    for mc in range(DC):
        base = (mc % 2) * NTT
        if mc >= 2:
            ctx.wait("tensor", sEq, (mc - 1) * NTT)
        for dc in range(DC):
            for tt in range(NTT):
                ctx.op("tensor", lambda e, mc=mc, dc=dc, tt=tt,
                       base=base: e.matmul(
                           pb[base + tt][:, 0:512],
                           wsb[:, (dc * DC + mc) * 128:
                               (dc * DC + mc) * 128 + 128],
                           hbf[:, dc * NT + tt * 512:
                               dc * NT + tt * 512 + 512],
                           start=(dc == 0), stop=(dc == DC - 1)),
                       inc=(sTq, 1) if (dc == DC - 1 and tt == NTT - 1)
                       else None)
        ctx.wait("vector", sTq, mc + 1)
        ctx.wait("vector", sHbf, 1)
        for tt in range(NTT):
            ctx.op("vector", lambda e, mc=mc, tt=tt, base=base:
                   e.tensor_scalar_add(
                       out=xbuf[:, mc * NT + tt * 512:
                                mc * NT + tt * 512 + 512],
                       in0=pb[base + tt][:, 0:512],
                       scalar1=par["bq"][:, mc:mc + 1]), inc=(sEq, 1))
    ctx.wait("vector", sEq)
    ctx.wait("tensor", sEq)
    rope_norm(xbuf, 8, par["qnw"], io["qg"], "rq")

    stage.final_waits("sync")


def build_launch_G1():
    return _wrap(decl_G1, emit_G1)


def stage_inputs_G1(core, W):
    r = core % 2
    rows = own_rows(r)
    C, S = rope_consts(rows)
    return dict(
        xn1=None,
        ln1w=stage_bias(W["L1_ffn_ln1_w"]),
        wq=np.ascontiguousarray(W["L1_ffn_Wq"]),
        bq=stage_bias(W["L1_ffn_bq"]),
        wk=np.ascontiguousarray(W["L1_ffn_Wk"]),
        bk=stage_bias(W["L1_ffn_bk"]),
        wv=np.ascontiguousarray(W["L1_ffn_Wv"]),
        bvr=np.ascontiguousarray(W["L1_ffn_bv"].reshape(1, 256)),
        qnw=np.ascontiguousarray(
            np.concatenate([W["L1_ffn_qn_w"], W["L1_ffn_qn_w"]]).reshape(
                128, 1)),
        knw=np.ascontiguousarray(
            np.concatenate([W["L1_ffn_kn_w"], W["L1_ffn_kn_w"]]).reshape(
                128, 1)),
        c_cos=C, c_sin=S, c_e2t=e2t_mat(), c_e64c=e64c_mat(),
        c_swp=swap_mat(),
        c_ones_bf=np.ones((128, 1), np.float32),
        c_ones_row=np.ones((1, 128), np.float32))





def decl_G2(din, dmid, dout, dbg=False):
    dmid("qg", [D, NT], BF16)
    dmid("kgf", [8 * 256, 512], BF16)
    dmid("vgf", [T, 256], BF16)
    din("wg", [64, 64])
    din("bg", [64, 1])
    din("wo", [D, D])
    din("bo", [128, 8])
    dmid("xn1", [D, NT], F32)
    din("ln2w", [128, 8])
    din("wr", [D, 8])
    din("br", [8, 1])
    din("c_maskt", [128, 32 * 512])
    din("c_cfix", [128, 1])
    din("c_ones_bf", [128, 1])
    din("c_ones_row", [1, 128])
    dout("xp", [D, NT], F32)
    if dbg:
        dout("dbg_og", [128, DC * NT], BF16)
    dout("h2b", [D, NT], BF16)
    dout("logits", [8, NT], F32)


def emit_G2(nc, ctx, io, en, dbg=False):
    xbuf = en(nc.sbuf_tensor("xbuf", [128, DC * NT], F32))
    og = en(nc.sbuf_tensor("og", [128, DC * NT], BF16))
    masks = en(nc.sbuf_tensor("masks", [128, 32 * 512],
                              mybir.dt.int8))
    kbuf = en(nc.sbuf_tensor("kbuf", [64, 4 * T], BF16))
    vaug = en(nc.sbuf_tensor("vaug", [128, 2 * 32 * 66], BF16))
    qgh = en(nc.sbuf_tensor("qgh", [64, 2 * NT], BF16))
    wsb = en(nc.sbuf_tensor("wsb", [128, DC * 1024], BF16))
    wg_sb = en(nc.sbuf_tensor("wg_sb", [64, 64], BF16))
    pbuf = en(nc.sbuf_tensor("pbuf", [128, 4 * 512], BF16))
    osb = en(nc.sbuf_tensor("osb", [128, 512], BF16))
    gate = en(nc.sbuf_tensor("gate", [64, 512], BF16))
    gsc = en(nc.sbuf_tensor("gsc", [64, 512], F32))
    dinv = en(nc.sbuf_tensor("dinv", [64, 512], F32))
    h2f = en(nc.sbuf_tensor("h2f", [128, 2 * 512], F32))
    stgf = en(nc.sbuf_tensor("stgf", [128, 512], F32))
    rows = en(nc.sbuf_tensor("rows", [1, NT + 64], F32))
    rrow = en(nc.sbuf_tensor("rrow", [1, 512], F32))
    par = {nm: en(nc.sbuf_tensor(nm + "_sb", [128, w], F32))
           for nm, w in (("bo", 8), ("ln2w", 8))}
    wr_sb = en(nc.sbuf_tensor("wr_sb", [128, DC * 8], F32))
    br_sb = en(nc.sbuf_tensor("br_sb", [8, 1], F32))
    bg_sb = en(nc.sbuf_tensor("bg_sb", [64, 1], F32))
    bias_col = en(nc.sbuf_tensor("bias_col", [128, 1], F32))
    ones_bf = en(nc.sbuf_tensor("ones_bf", [128, 1], BF16))
    ones_row = en(nc.sbuf_tensor("ones_row", [1, 128], F32))
    stg = en(nc.sbuf_tensor("stg", [128, 2 * 512], BF16))
    pb = [en(nc.psum_tensor(f"pb{i}", [128, 512], F32)) for i in range(8)]

    sLd = ctx.sem("ld")
    sLg = ctx.sem("ldg")
    sLvg = ctx.sem("ldvg")
    sLqg = ctx.sem("ldqg")
    sT = ctx.sem("t")
    sV = ctx.sem("v")
    sS = ctx.sem("s")
    sPk = ctx.sem("pk")
    sOF = ctx.sem("of")
    sStP = ctx.sem("stp")
    sStL = ctx.sem("stl")
    stage = Stage(ctx, stg, 2, 512, "sg")

    ctx.op("sync", lambda e: e.dma_start(
        out=xbuf[:].rearrange("p (c t) -> p c t", c=DC),
        in_=io["xn1"].rearrange("(c p) t -> p c t", p=128)),
        inc=(sLd, 16))
    for k_ in range(8):
        ctx.op("sync", lambda e, k_=k_: e.dma_start(
            out=kbuf[:].rearrange("p (g k t) -> p g k t", g=4, k=8)
            [:, :, k_, :],
            in_=io["kgf"].rearrange("(k g p) t -> k p g t", g=4, p=64)
            [k_]), inc=(sLd, 16))
    for nm in ("bo", "ln2w"):
        ctx.op("sync", lambda e, nm=nm: e.dma_start(
            out=par[nm][:], in_=io[nm]), inc=(sLd, 16))
    ctx.op("sync", lambda e: e.dma_start(out=br_sb[:], in_=io["br"]),
           inc=(sLd, 16))
    ctx.op("sync", lambda e: e.dma_start(out=bg_sb[:], in_=io["bg"]),
           inc=(sLd, 16))
    ctx.op("sync", lambda e: e.dma_start(
        out=wr_sb[:].rearrange("p (c m) -> p c m", c=DC),
        in_=io["wr"].rearrange("(c p) m -> p c m", p=128)),
        inc=(sLd, 16))
    ctx.op("sync", lambda e: e.dma_start(out=ones_row[:],
                                         in_=io["c_ones_row"]),
           inc=(sLd, 16))
    ctx.op("gpsimd", lambda e: e.dma_start(out=masks[:],
                                           in_=io["c_maskt"]),
           inc=(sLg, 16))
    ctx.op("gpsimd", lambda e: e.dma_start(out=ones_bf[:],
                                           in_=io["c_ones_bf"]),
           inc=(sLg, 16))
    ctx.op("gpsimd", lambda e: e.dma_start(out=wg_sb[:], in_=io["wg"]),
           inc=(sLg, 16))
    ctx.op("gpsimd", lambda e: e.dma_start(
        out=wsb[:].rearrange("p (c m) -> p c m", c=DC),
        in_=io["wo"].rearrange("(c p) m -> p c m", p=128)),
        inc=(sLg, 16))
    ctx.op("sync", lambda e: e.dma_start(out=bias_col[:],
                                         in_=io["c_cfix"]),
           inc=(sLd, 16))
    sMs = ctx.sem("ms")
    ctx.op("vector", lambda e: e.memset(vaug[:], 1.0), inc=(sMs, 1))

    ctx.wait("vector", sLg)
    ctx.wait("tensor", sLd)
    pk_endg = {}
    t_endq = {}
    lvg = {}
    lq = {}

    def emit_vload(g):
        vo = (g % 2) * 32 * 66
        ctx.wait("scalar", sMs, 1)
        if g >= 2:
            ctx.wait("scalar", sPk, pk_endg[g - 2])
        ctx.op("scalar", lambda e, g=g, vo=vo: e.dma_start(
            out=vaug[:, vo:vo + 32 * 66].rearrange(
                "p (kc w) -> p kc w", w=66)[:, :, 0:64],
            in_=io["vgf"][:, g * 64:g * 64 + 64].rearrange(
                "(kc p) c -> p kc c", p=128)), inc=(sLvg, 16))
        lvg[g] = sLvg.n

    def emit_qload(h):
        qko = (h % 2) * NT
        if h >= 2:
            ctx.wait("scalar", sT, t_endq[h - 2])
        ctx.op("scalar", lambda e, h=h, qko=qko: e.dma_start(
            out=qgh[:, qko:qko + NT],
            in_=io["qg"][h * 64:h * 64 + 64, :]), inc=(sLqg, 16))
        lq[h] = sLqg.n

    pending = [None]
    emit_vload(0)
    emit_qload(0)
    ctx.wait("scalar", sLd)
    for g in range(4):
        vo = (g % 2) * 32 * 66
        ctx.wait("tensor", sLg)
        ctx.wait("tensor", sLvg, lvg[g])
        if g + 1 < 4:
            emit_vload(g + 1)
        for hh in range(4):
            h = g * 4 + hh
            ctx.wait("tensor", sLqg, lq[h])
            if h + 1 < 16:
                emit_qload(h + 1)
            for slot in range(NTT):
                qo = slot * 512
                kcmax = KCMAX[slot]
                lo = MASKLO[slot]
                base_pk = sPk.n
                SB = (0, 1, 6, 7)
                ctx.wait("scalar", sV)
                ctx.wait("tensor", sS)
                tS = {}
                eS = {}
                mV = {}

                def emit_S(kc, qo=qo, g=g, vo=vo, h=h):
                    sb_ = SB[kc % 4]
                    if kc >= 4:
                        ctx.wait("tensor", sS, eS[kc - 4])
                    qko = (h % 2) * NT
                    ctx.op("tensor", lambda e, kc=kc, sb_=sb_, qo=qo,
                           g=g, qko=qko: e.matmul(
                               pb[sb_][:, 0:512],
                               kbuf[0:64, g * T + kc * 128:
                                    g * T + kc * 128 + 128],
                               qgh[0:64, qko + qo:qko + qo + 512],
                               start=True, stop=True), inc=(sT, 1))
                    tS[kc] = sT.n

                def emit_exp(kc, slot=slot, lo=lo, base_pk=base_pk):
                    if kc >= 4:
                        ctx.wait("scalar", sPk, base_pk + kc - 3)
                    elif base_pk > 0:
                        ctx.wait("scalar", sPk, base_pk)
                    ctx.wait("scalar", sT, tS[kc])
                    sb_ = SB[kc % 4]
                    ps = kc % 4
                    ctx.op("scalar", lambda e, sb_=sb_, ps=ps:
                           e.activation(
                               out=pbuf[:, ps * 512:ps * 512 + 512],
                               in_=pb[sb_][:, 0:512], func=AF.Exp,
                               bias=bias_col[:, 0:1], scale=1.0 / 8.0),
                           inc=(sS, 1))
                    eS[kc] = sS.n
                    if kc >= lo:
                        mi = slot * 8 + (kc - lo)
                        ctx.wait("vector", sS, eS[kc])
                        ctx.op("vector", lambda e, ps=ps, mi=mi:
                               e.tensor_mul(
                                   out=pbuf[:, ps * 512:ps * 512
                                            + 512],
                                   in0=pbuf[:, ps * 512:ps * 512
                                            + 512],
                                   in1=masks[:, mi * 512:mi * 512
                                             + 512]), inc=(sV, 1))
                        mV[kc] = sV.n
                    else:
                        mV[kc] = None

                for j in range(min(3, kcmax)):
                    emit_S(j)
                    emit_exp(j)
                if pending[0] is not None:
                    pending[0]()
                    pending[0] = None
                for kc in range(kcmax):
                    if kc + 3 < kcmax:
                        emit_S(kc + 3)
                        emit_exp(kc + 3)
                    if kc == 0:
                        ctx.wait("tensor", sOF)
                    if mV[kc] is not None:
                        ctx.wait("tensor", sV, mV[kc])
                    else:
                        ctx.wait("tensor", sS, eS[kc])
                    ps = kc % 4
                    ctx.op("tensor", lambda e, kc=kc, ps=ps, vo=vo,
                           kcmax=kcmax: e.matmul(
                               pb[2][0:65, 0:512],
                               vaug[:, vo + kc * 66:vo + kc * 66 + 65],
                               pbuf[:, ps * 512:ps * 512 + 512],
                               start=(kc == 0), stop=(kc == kcmax - 1)),
                           inc=(sPk, 1))

                def epilogue(h=h, qo=qo):
                    ctx.wait("vector", sPk)
                    ctx.wait("vector", sT)
                    ctx.op("vector", lambda e: e.tensor_copy(
                        out=osb[0:65, 0:512], in_=pb[2][0:65, 0:512]))
                    ctx.op("vector", lambda e: e.reciprocal(
                        out=rrow[:], in_=pb[2][64:65, 0:512]),
                        inc=(sV, 1))
                    ctx.wait("tensor", sV)
                    ctx.op("tensor", lambda e: e.matmul(
                        pb[3][0:64, 0:512], ones_row[0:1, 0:64],
                        rrow[:], start=True, stop=True))
                    ctx.op("tensor", lambda e: e.matmul(
                        pb[4][0:64, 0:512], wg_sb[:],
                        osb[0:64, 0:512], start=True, stop=True),
                        inc=(sT, 1))
                    ctx.wait("vector", sT)
                    ctx.op("vector", lambda e: e.tensor_copy(
                        out=dinv[:], in_=pb[3][0:64, 0:512]))
                    ctx.op("vector", lambda e: e.tensor_mul(
                        out=gsc[:], in0=pb[4][0:64, 0:512],
                        in1=dinv[:]), inc=(sV, 1))
                    ctx.wait("scalar", sV)
                    ctx.op("scalar", lambda e: e.activation(
                        out=gate[:], in_=gsc[:], func=AF.Sigmoid,
                        bias=bg_sb[:, 0:1], scale=1.0), inc=(sS, 1))
                    ctx.wait("vector", sS)
                    ctx.op("vector", lambda e: e.tensor_mul(
                        out=gate[:], in0=osb[0:64, 0:512],
                        in1=gate[:]))
                    ctx.op("vector", lambda e, h=h, qo=qo:
                           e.tensor_mul(
                               out=og[(h % 2) * 64:(h % 2) * 64 + 64,
                                      (h // 2) * NT + qo:
                                      (h // 2) * NT + qo + 512],
                               in0=gate[:], in1=dinv[:]),
                           inc=(sOF, 1))
                pending[0] = epilogue
            t_endq[h] = sT.n
        pk_endg[g] = sPk.n
    pending[0]()
    pending[0] = None

    sTo = ctx.sem("to")
    sEo = ctx.sem("eo")
    sVh = ctx.sem("vh")
    ctx.wait("tensor", sOF)
    if dbg:
        sDg = ctx.sem("dbgog")
        ctx.wait("sync", sOF)
        ctx.op("sync", lambda e: e.dma_start(
            out=io["dbg_og"], in_=og[:]), inc=(sDg, 16))
        ctx.wait("sync", sDg)
    for mc in range(DC):
        base = (mc % 2) * NTT
        if mc >= 2:
            ctx.wait("tensor", sVh, (mc - 1) * NTT)
        for dc in range(DC):
            for tt in range(NTT):
                ctx.op("tensor", lambda e, mc=mc, dc=dc, tt=tt,
                       base=base: e.matmul(
                           pb[base + tt][:, 0:512],
                           wsb[:, (dc * DC + mc) * 128:
                               (dc * DC + mc) * 128 + 128],
                           og[:, dc * NT + tt * 512:
                              dc * NT + tt * 512 + 512],
                           start=(dc == 0), stop=(dc == DC - 1)),
                       inc=(sTo, 1) if (dc == DC - 1 and tt == NTT - 1)
                       else None)
        ctx.wait("scalar", sTo, mc + 1)
        for tt in range(NTT):
            nv = mc * NTT + tt
            if nv >= 1:
                ctx.wait("scalar", sVh, nv)
            ctx.op("scalar", lambda e, mc=mc, tt=tt,
                   base=base: e.activation(
                       out=stgf[:, 0:512],
                       in_=pb[base + tt][:, 0:512], func=AF.Identity,
                       bias=par["bo"][:, mc:mc + 1], scale=1.0),
                   inc=(sEo, 1))
            ctx.wait("vector", sEo)
            ctx.op("vector", lambda e, mc=mc, tt=tt:
                   e.tensor_add(
                       out=xbuf[:, mc * NT + tt * 512:
                                mc * NT + tt * 512 + 512],
                       in0=stgf[:, 0:512],
                       in1=xbuf[:, mc * NT + tt * 512:
                                mc * NT + tt * 512 + 512]),
                   inc=(sVh, 1))
    ctx.wait("sync", sVh)
    ctx.op("sync", lambda e: e.dma_start(
        out=io["xp"].rearrange("(c p) t -> p c t", p=128),
        in_=xbuf[:].rearrange("p (c t) -> p c t", c=DC)),
        inc=(sStP, 16))

    sS2 = ctx.sem("s2")
    sT2 = ctx.sem("t2")
    sV2 = ctx.sem("v2")
    ctx.wait("scalar", sVh)
    ctx.op("scalar", lambda e: e.activation(
        out=og[:], in_=xbuf[:], func=AF.Square), inc=(sS2, 1))
    ctx.wait("tensor", sS2)
    for tt in range(NTT):
        o = tt * 512
        for c in range(DC):
            ctx.op("tensor", lambda e, c=c, o=o, tt=tt: e.matmul(
                pb[tt][0:1, 0:512], ones_bf[:, 0:1],
                og[:, c * NT + o:c * NT + o + 512],
                start=(c == 0), stop=(c == DC - 1)),
                inc=(sT2, 1) if c == DC - 1 else None)
    ctx.wait("vector", sT2)
    for tt in range(NTT):
        o = tt * 512
        ctx.op("vector", lambda e, o=o, tt=tt: e.tensor_scalar(
            out=rows[0:1, o:o + 512], in0=pb[tt][0:1, 0:512],
            scalar1=1.0 / D, scalar2=float(RMS_EPS),
            op0=ALU.mult, op1=ALU.add))
        ctx.op("vector", lambda e, o=o: e.reciprocal(
            out=rows[0:1, o:o + 512], in_=rows[0:1, o:o + 512]),
            inc=(sV2, 1))
    ctx.wait("scalar", sV2)
    ctx.op("scalar", lambda e: e.activation(
        out=rows[0:1, 0:NT], in_=rows[0:1, 0:NT], func=AF.Sqrt),
        inc=(sS2, 1))
    ctx.wait("tensor", sS2)
    for tt in range(NTT):
        o = tt * 512
        if tt >= 1:
            ctx.wait("tensor", sV2)
            ctx.wait("vector", sStL)
        ctx.op("tensor", lambda e, o=o: e.matmul(
            pb[4][:, 0:512], ones_row[0:1, :], rows[0:1, o:o + 512],
            start=True, stop=True), inc=(sT2, 1))
        ctx.wait("vector", sT2)
        for c in range(DC):
            slot = c % 2
            if tt * DC + c >= 2:
                ctx.wait("vector", sT2)
                ctx.wait("vector", sS2)
            ctx.op("vector", lambda e, c=c, o=o, slot=slot:
                   e.tensor_mul(
                       out=h2f[:, slot * 512:slot * 512 + 512],
                       in0=xbuf[:, c * NT + o:c * NT + o + 512],
                       in1=pb[4][:, 0:512]))
            ctx.op("vector", lambda e, c=c, slot=slot:
                   e.tensor_scalar_mul(
                       out=h2f[:, slot * 512:slot * 512 + 512],
                       in0=h2f[:, slot * 512:slot * 512 + 512],
                       scalar1=par["ln2w"][:, c:c + 1]), inc=(sV2, 1))
            ctx.wait("scalar", sV2)
            tile2, slot2 = stage.next("scalar")
            ctx.op("scalar", lambda e, tile2=tile2, slot=slot:
                   e.activation(
                       out=tile2,
                       in_=h2f[:, slot * 512:slot * 512 + 512],
                       func=AF.Identity), inc=(sS2, 1))
            ctx.wait("sync", sS2)
            stage.store("sync", lambda e, tile2=tile2, c=c, o=o:
                        e.dma_start(
                            out=io["h2b"].rearrange(
                                "(c p) t -> p c t", p=128)
                            [:, c:c + 1, o:o + 512], in_=tile2),
                        slot2)
            ctx.wait("tensor", sV2)
            ctx.op("tensor", lambda e, c=c, slot=slot: e.matmul(
                pb[5][0:8, 0:512], wr_sb[:, c * 8:c * 8 + 8],
                h2f[:, slot * 512:slot * 512 + 512],
                start=(c == 0), stop=(c == DC - 1)),
                inc=(sT2, 1))
        ctx.wait("vector", sT2)
        ctx.op("vector", lambda e, o=o, tt=tt: e.tensor_scalar_add(
            out=rrow[0:1, 0:512], in0=pb[5][0:8, 0:512],
            scalar1=br_sb[:]), inc=(sV2, 1))
        ctx.rec["vector"].pop()
        sV2.n -= 1
        ctx.op("vector", lambda e: e.tensor_scalar_add(
            out=h2f[0:8, 512:1024], in0=pb[5][0:8, 0:512],
            scalar1=br_sb[:]), inc=(sV2, 1))
        ctx.wait("sync", sV2)
        ctx.op("sync", lambda e, o=o: e.dma_start(
            out=io["logits"][:, o:o + 512], in_=h2f[0:8, 512:1024]),
            inc=(sStL, 16))
    ctx.wait("sync", sStP)
    ctx.wait("sync", sStL)
    stage.final_waits("sync")


def build_launch_G2(dbg=False):
    return _wrap(
        lambda di, dm, do: decl_G2(di, dm, do, dbg=dbg),
        lambda nc, ctx, io, en: emit_G2(nc, ctx, io, en, dbg=dbg))





def selg_mat():
    Smat = np.zeros((16, 16 * 128), np.float32)
    for g in range(16):
        Smat[g, g * 128:(g + 1) * 128] = 1.0
    return Smat


def decl_H1(din, dmid, dout):
    dmid("h2b", [D, NT], BF16)
    dmid("logits", [8, NT], F32)
    din("b1s", [128, 8 * 32])
    din("w1", [8, D, 4096])
    din("w2", [8, 2048, D])
    din("b2r", [1, 8192])
    din("c_selg", [16, 16 * 128])
    din("c_id8", [8, 8])
    din("c_idL", [128, 128])
    dout("y", [D, NT], BF16)


def emit_H1(nc, ctx, io, en):
    h2 = en(nc.sbuf_tensor("h2", [128, DC * NT], BF16))
    mb = en(nc.sbuf_tensor("mb", [128, 8 * NT], BF16))
    yacc = en(nc.sbuf_tensor("yacc", [128, DC * NT], BF16))
    actb = en(nc.sbuf_tensor("actb", [128, 16 * 512], BF16))
    w1a = en(nc.sbuf_tensor("w1a", [128, 2 * DC * 512], BF16))
    w1b = en(nc.sbuf_tensor("w1b", [128, 2 * DC * 512], BF16))
    w2s = en(nc.sbuf_tensor("w2s", [128, 2 * 4 * 1024], BF16))
    sgb = en(nc.sbuf_tensor("sgb", [128, 2 * 512], BF16))
    sgm = en(nc.sbuf_tensor("sgm", [128, 2 * 512], BF16))
    lbuf = en(nc.sbuf_tensor("lbuf", [8, NT], F32))
    ltm = en(nc.sbuf_tensor("ltm", [128, 128], F32))
    mxb = en(nc.sbuf_tensor("mxb", [128, 16], F32))
    geb = en(nc.sbuf_tensor("geb", [128, 128], F32))
    runb = en(nc.sbuf_tensor("runb", [128, 16], F32))
    met = en(nc.sbuf_tensor("met", [128, 128], F32))
    tscr = en(nc.sbuf_tensor("tscr", [128, 16], F32))
    mrow = en(nc.sbuf_tensor("mrow", [16, 8 * 128], BF16))
    selg = en(nc.sbuf_tensor("selg", [16, 16 * 128], BF16))
    id8 = en(nc.sbuf_tensor("id8", [8, 8], F32))
    idL = en(nc.sbuf_tensor("idL", [128, 128], F32))
    b1s = en(nc.sbuf_tensor("b1s_sb", [128, 8 * 32], F32))
    b2r = en(nc.sbuf_tensor("b2r_sb", [1, 8192], BF16))
    pb = [en(nc.psum_tensor(f"pb{i}", [128, 512], F32)) for i in range(8)]

    sLd = ctx.sem("ld")
    sLg = ctx.sem("ldg")
    sT = ctx.sem("t")
    sV = ctx.sem("v")
    sS = ctx.sem("s")
    sAct = ctx.sem("act")
    sEY = ctx.sem("ey")
    sSt = ctx.sem("st")

    ctx.op("sync", lambda e: e.dma_start(
        out=h2[:].rearrange("p (c t) -> p c t", c=DC),
        in_=io["h2b"].rearrange("(c p) t -> p c t", p=128)),
        inc=(sLd, 16))
    ctx.op("sync", lambda e: e.dma_start(out=lbuf[:], in_=io["logits"]),
           inc=(sLd, 16))
    ctx.op("sync", lambda e: e.dma_start(out=b1s[:], in_=io["b1s"]),
           inc=(sLd, 16))
    ctx.op("sync", lambda e: e.dma_start(out=id8[:], in_=io["c_id8"]),
           inc=(sLd, 16))
    ctx.op("sync", lambda e: e.dma_start(out=idL[:], in_=io["c_idL"]),
           inc=(sLd, 16))
    ctx.op("gpsimd", lambda e: e.dma_start(out=b2r[:], in_=io["b2r"]),
           inc=(sLg, 16))
    ctx.op("gpsimd", lambda e: e.dma_start(out=selg[:],
                                           in_=io["c_selg"]),
           inc=(sLg, 16))

    ctx.wait("tensor", sLd)
    for tg in range(16):
        ctx.op("tensor", lambda e, tg=tg: e.transpose(
            pb[0][:, tg * 8:tg * 8 + 8], lbuf[0:8, tg * 128:tg * 128
                                              + 128], id8[:]),
            inc=(sT, 1) if tg == 15 else None)
    ctx.wait("vector", sT)
    ctx.op("vector", lambda e: e.tensor_copy(out=ltm[:],
                                             in_=pb[0][:, 0:128]),
           inc=(sV, 1))
    ctx.wait("vector", sV)
    ctx.op("vector", lambda e: e.reduce_max(
        out=mxb[:], in_=ltm[:].rearrange("p (g e) -> p g e", e=8),
        axis=AX.X), inc=(sV, 1))
    ctx.wait("vector", sV)
    mxv = mxb[:].rearrange("p (g a) -> p g a", a=1).broadcast_to(
        [128, 16, 8])
    ctx.op("vector", lambda e, mxv=mxv: e.tensor_tensor(
        out=geb[:].rearrange("p (g e) -> p g e", e=8),
        in0=ltm[:].rearrange("p (g e) -> p g e", e=8), in1=mxv,
        op=ALU.is_ge), inc=(sV, 1))
    ctx.wait("vector", sV)
    ctx.op("vector", lambda e: e.memset(runb[:], 0.0), inc=(sV, 1))
    for e_ in range(8):
        gev = geb[:].rearrange("p (g e) -> p g e", e=8)[:, :, e_]
        mev = met[:].rearrange("p (g e) -> p g e", e=8)[:, :, e_]
        ctx.wait("vector", sV)
        ctx.op("vector", lambda e, gev=gev: e.tensor_mul(
            out=tscr[:], in0=gev, in1=runb[:]), inc=(sV, 1))
        ctx.wait("vector", sV)
        ctx.op("vector", lambda e, gev=gev, mev=mev: e.tensor_sub(
            out=mev, in0=gev, in1=tscr[:]), inc=(sV, 1))
        ctx.wait("vector", sV)
        ctx.op("vector", lambda e, mev=mev: e.tensor_add(
            out=runb[:], in0=runb[:], in1=mev), inc=(sV, 1))
    ctx.wait("tensor", sV)
    base_v = sV.n
    for e_ in range(8):
        bank = 1 + e_ % 2
        if e_ >= 2:
            ctx.wait("tensor", sV, base_v + e_ - 1)
        ctx.op("tensor", lambda e, e_=e_, bank=bank: e.transpose(
            pb[bank][0:16, 0:128],
            met[:].rearrange("p (g e) -> p g e", e=8)[:, :, e_],
            idL[:]), inc=(sT, 1))
        ctx.wait("vector", sT)
        ctx.op("vector", lambda e, e_=e_, bank=bank: e.tensor_copy(
            out=mrow[0:16, e_ * 128:e_ * 128 + 128],
            in_=pb[bank][0:16, 0:128]), inc=(sV, 1))
    ctx.wait("tensor", sV)
    ctx.wait("tensor", sLg)
    sMb = ctx.sem("mbv")
    for e_ in range(8):
        for gq in range(4):
            bank = 3 + gq % 2
            if e_ * 4 + gq >= 2:
                ctx.wait("tensor", sMb, e_ * 4 + gq - 1)
            for gi in range(4):
                g = gq * 4 + gi
                ctx.op("tensor", lambda e, e_=e_, g=g, gi=gi, bank=bank:
                       e.matmul(
                           pb[bank][:, gi * 128:gi * 128 + 128],
                           selg[:, g * 128:g * 128 + 128],
                           mrow[0:16, e_ * 128:e_ * 128 + 128],
                           start=True, stop=True),
                       inc=(sT, 1) if gi == 3 else None)
            ctx.wait("vector", sT)
            ctx.op("vector", lambda e, e_=e_, gq=gq, bank=bank:
                   e.tensor_copy(
                       out=mb[:, e_ * NT + gq * 512:e_ * NT + gq * 512
                              + 512],
                       in_=pb[bank][:, 0:512]), inc=(sMb, 1))

    ctx.wait("scalar", sLd)
    ctx.wait("vector", sLd)
    ctx.op("vector", lambda e: e.memset(yacc[:], 0.0), inc=(sV, 1))
    wld = {"a": DBuf(ctx, 2, "wla"), "b": DBuf(ctx, 2, "wlb"),
           "2": DBuf(ctx, 2, "wl2")}
    for e_ in range(8):
        for cp in range(4):
            it = e_ * 4 + cp
            half = (it % 2)
            if it >= 2:
                ctx.wait("gpsimd", sT)
            ctx.op("gpsimd", lambda e, e_=e_, cp=cp, half=half:
                   e.dma_start(
                       out=w1a[:, half * DC * 512:(half + 1) * DC * 512]
                       .rearrange("p (c m) -> p c m", c=DC),
                       in_=io["w1"][e_, :, cp * 512:cp * 512 + 512]
                       .rearrange("(c p) m -> p c m", p=128)),
                   inc=(wld["a"].sems[half], 16))
            wld["a"].idx += 1
            ctx.op("gpsimd", lambda e, e_=e_, cp=cp, half=half:
                   e.dma_start(
                       out=w1b[:, half * DC * 512:(half + 1) * DC * 512]
                       .rearrange("p (c m) -> p c m", c=DC),
                       in_=io["w1"][e_, :, 2048 + cp * 512:2048 + cp
                                    * 512 + 512]
                       .rearrange("(c p) m -> p c m", p=128)),
                   inc=(wld["b"].sems[half], 16))
            wld["b"].idx += 1
            ctx.op("gpsimd", lambda e, e_=e_, cp=cp, half=half:
                   e.dma_start(
                       out=w2s[:, half * 4096:(half + 1) * 4096]
                       .rearrange("p (r m) -> p r m", r=4),
                       in_=io["w2"][e_, cp * 512:cp * 512 + 512, :]
                       .rearrange("(r p) m -> p r m", p=128)),
                   inc=(wld["2"].sems[half], 16))
            wld["2"].idx += 1
            ctx.wait("tensor", wld["a"].sems[half])
            ctx.wait("tensor", wld["b"].sems[half])
            act_base = sAct.n
            v_base = sV.n
            for uc in range(4):
                for tt in range(NTT):
                    ui = uc * 4 + tt
                    bk1 = ui % 2
                    bk2 = 2 + ui % 2
                    if ui >= 2:
                        ctx.wait("tensor", sAct, act_base + ui - 1)
                    for dc in range(DC):
                        ctx.op("tensor", lambda e, dc=dc, uc=uc, tt=tt,
                               half=half, bk1=bk1: e.matmul(
                                   pb[bk1][:, 0:512],
                                   w1a[:, half * DC * 512 + dc * 512
                                       + uc * 128:half * DC * 512
                                       + dc * 512 + uc * 128 + 128],
                                   h2[:, dc * NT + tt * 512:
                                      dc * NT + tt * 512 + 512],
                                   start=(dc == 0), stop=(dc == DC - 1)))
                    for dc in range(DC):
                        ctx.op("tensor", lambda e, dc=dc, uc=uc, tt=tt,
                               half=half, bk2=bk2: e.matmul(
                                   pb[bk2][:, 0:512],
                                   w1b[:, half * DC * 512 + dc * 512
                                       + uc * 128:half * DC * 512
                                       + dc * 512 + uc * 128 + 128],
                                   h2[:, dc * NT + tt * 512:
                                      dc * NT + tt * 512 + 512],
                                   start=(dc == 0),
                                   stop=(dc == DC - 1)),
                               inc=(sT, 1) if dc == DC - 1 else None)
                    ctx.wait("scalar", sT)
                    if ui >= 2:
                        ctx.wait("scalar", sV, v_base + ui - 1)
                    ch1 = e_ * 32 + cp * 4 + uc
                    ch2 = e_ * 32 + 16 + cp * 4 + uc
                    ctx.op("scalar", lambda e, bk2=bk2, ui=ui, ch2=ch2:
                           e.activation(
                               out=sgb[:, (ui % 2) * 512:(ui % 2) * 512
                                       + 512],
                               in_=pb[bk2][:, 0:512], func=AF.Sigmoid,
                               bias=b1s[:, ch2:ch2 + 1], scale=1.0),
                           inc=(sS, 1))
                    ctx.wait("vector", sS)
                    ctx.op("vector", lambda e, ui=ui, e_=e_, tt=tt:
                           e.tensor_mul(
                               out=sgm[:, (ui % 2) * 512:(ui % 2) * 512
                                       + 512],
                               in0=sgb[:, (ui % 2) * 512:(ui % 2) * 512
                                       + 512],
                               in1=mb[:, e_ * NT + tt * 512:
                                      e_ * NT + tt * 512 + 512]),
                           inc=(sV, 1))
                    ctx.wait("vector", sV)
                    ctx.op("vector", lambda e, ui=ui, bk1=bk1, ch1=ch1:
                           e.scalar_tensor_tensor(
                               out=actb[:, ui * 512:ui * 512 + 512],
                               in0=pb[bk1][:, 0:512],
                               scalar=b1s[:, ch1:ch1 + 1],
                               in1=sgm[:, (ui % 2) * 512:(ui % 2) * 512
                                       + 512],
                               op0=ALU.add, op1=ALU.mult),
                           inc=(sAct, 1))
            ctx.wait("tensor", wld["2"].sems[half])
            ctx.wait("tensor", sAct)
            ey_base = sEY.n
            for dc in range(DC):
                for tt in range(NTT):
                    wi = dc * 4 + tt
                    bank = 4 + wi % 4
                    if wi >= 4:
                        ctx.wait("tensor", sEY, ey_base + wi - 3)
                    for rr in range(4):
                        last = (rr == 3 and cp != 0)
                        ctx.op("tensor", lambda e, dc=dc, tt=tt, rr=rr,
                               half=half, bank=bank, last=last:
                               e.matmul(
                                   pb[bank][:, 0:512],
                                   w2s[:, half * 4096 + rr * 1024
                                       + dc * 128:half * 4096
                                       + rr * 1024 + dc * 128 + 128],
                                   actb[:, (rr * 4 + tt) * 512:
                                        (rr * 4 + tt) * 512 + 512],
                                   start=(rr == 0), stop=last),
                               inc=(sT, 1) if last else None)
                    if cp == 0:
                        ctx.op("tensor", lambda e, e_=e_, dc=dc, tt=tt,
                               bank=bank: e.matmul(
                                   pb[bank][:, 0:512],
                                   b2r[0:1, e_ * 1024 + dc * 128:
                                       e_ * 1024 + dc * 128 + 128],
                                   mb[0:1, e_ * NT + tt * 512:
                                      e_ * NT + tt * 512 + 512],
                                   start=False, stop=True),
                               inc=(sT, 1))
                    ctx.wait("vector", sT)
                    ctx.op("vector", lambda e, dc=dc, tt=tt, bank=bank:
                           e.tensor_add(
                               out=yacc[:, dc * NT + tt * 512:
                                        dc * NT + tt * 512 + 512],
                               in0=pb[bank][:, 0:512],
                               in1=yacc[:, dc * NT + tt * 512:
                                        dc * NT + tt * 512 + 512]),
                           inc=(sEY, 1))

    ctx.wait("sync", sEY)
    ctx.op("sync", lambda e: e.dma_start(
        out=io["y"].rearrange("(c p) t -> p c t", p=128),
        in_=yacc[:].rearrange("p (c t) -> p c t", c=DC)),
        inc=(sSt, 16))
    ctx.wait("sync", sSt)


def build_launch_H1():
    return _wrap(decl_H1, emit_H1)





def decl_H2(din, dmid, dout):
    dmid("x1", [D, NT], F32)
    dmid("xp", [D, NT], F32)
    dmid("y", [D, NT], BF16)
    dmid("s1", [D, NT], BF16)
    dout("outp", [D, NT], F32)


def emit_H2(nc, ctx, io, en):
    x1b = en(nc.sbuf_tensor("x1b", [128, DC * NT], F32))
    xpb = en(nc.sbuf_tensor("xpb", [128, DC * NT], F32))
    yb = en(nc.sbuf_tensor("yb", [128, DC * NT], BF16))
    s1b = en(nc.sbuf_tensor("s1b", [128, DC * NT], BF16))
    sLd = ctx.sem("ld")
    sV = ctx.sem("v")
    sS = ctx.sem("s")
    sSt = ctx.sem("st")
    for nm, dst in (("x1", x1b), ("xp", xpb)):
        ctx.op("sync", lambda e, nm=nm, dst=dst: e.dma_start(
            out=dst[:].rearrange("p (c t) -> p c t", c=DC),
            in_=io[nm].rearrange("(c p) t -> p c t", p=128)),
            inc=(sLd, 16))
    for nm, dst in (("y", yb), ("s1", s1b)):
        ctx.op("sync", lambda e, nm=nm, dst=dst: e.dma_start(
            out=dst[:].rearrange("p (c t) -> p c t", c=DC),
            in_=io[nm].rearrange("(c p) t -> p c t", p=128)),
            inc=(sLd, 16))
    ctx.wait("vector", sLd)
    ctx.op("vector", lambda e: e.tensor_add(
        out=xpb[:], in0=xpb[:], in1=yb[:]), inc=(sV, 1))
    ctx.wait("scalar", sV)
    ctx.op("scalar", lambda e: e.activation(
        out=yb[:], in_=xpb[:], func=AF.Silu), inc=(sS, 1))
    ctx.wait("vector", sS)
    ctx.op("vector", lambda e: e.tensor_mul(
        out=xpb[:], in0=s1b[:], in1=yb[:]))
    ctx.op("vector", lambda e: e.tensor_add(
        out=x1b[:], in0=x1b[:], in1=xpb[:]), inc=(sV, 1))
    ctx.wait("sync", sV)
    ctx.op("sync", lambda e: e.dma_start(
        out=io["outp"].rearrange("(c p) t -> p c t", p=128),
        in_=x1b[:].rearrange("p (c t) -> p c t", c=DC)),
        inc=(sSt, 16))
    ctx.wait("sync", sSt)


def build_launch_H2():
    return _wrap(decl_H2, emit_H2)



def exchange_indices(core):
    r = core % 2
    KC = OWN[r]
    ix = np.zeros((128, 8), np.int32)
    for sidx in range(128):
        ix[sidx, 0] = KC[sidx // 32] * 32 + (sidx % 32)
        ix[sidx, 1] = (sidx // 16) * 32 + r * 16 + (sidx % 16)
    for b in range(256):
        ix[b % 128, 2 + b // 128] = KC[b // 64] * 64 + (b % 64)
        ix[b % 128, 4 + b // 128] = KC[b // 64] * 64 + (b % 64)
    for sidx in range(32):
        ix[sidx, 6] = KC[sidx // 8] * 8 + (sidx % 8)
    for sidx in range(64):
        ix[sidx, 7] = KC[sidx // 16] * 16 + (sidx % 16)
    ist = (np.arange(16) + r * 16).astype(np.int32).reshape(16, 1)
    return dict(x_ix=ix, x_idx_st=ist)


def _xc_stamp_poll(ctx, xs, sno, st_ap):
    sfl = ctx.sem("xfl")
    non = xs["non"]
    ctx.op("gpsimd", lambda e: e.indirect_dma_start(
        out=st_ap, out_offset=IndirectOffsetOnAxis(xs["ist"][:], 0),
        in_=non[:, sno * 16:(sno + 1) * 16], in_offset=None),
        inc=(sfl, 16))
    ctx.wait("gpsimd", sfl)
    flags = xs["flags"]
    poll_h = xs["polls"][sno]

    def poll(e):
        pf = f"x{sno}_"
        with e.register(pf + "cond") as cond, \
                e.register(pf + "cnt") as cnt, \
                e.register(pf + "rM") as rM, e.register(pf + "r0") as r0, \
                e.register(pf + "a") as a, e.register(pf + "bb") as bb, \
                e.register(pf + "acc") as acc:
            e.reg_load(rM, non[0:1, sno * 16:sno * 16 + 1])
            e.reg_mov(cond, 1)
            e.reg_mov(cnt, 0)
            with e.While(cond):
                e.reg_add(cnt, cnt, 16)
                e.dma_start(
                    out=flags[:],
                    in_=st_ap.rearrange("(a q) c -> a q c", q=16)[:, 0, :]
                ).then_inc(poll_h, 16)
                e.wait_ge(poll_h, cnt)
                e.reg_mov(acc, 1)
                for pp in range(2):
                    for w in range(2):
                        e.reg_load(r0, flags[pp:pp + 1, w:w + 1])
                        e.reg_alu(a, r0, rM, ALU.is_ge)
                        e.reg_alu(bb, rM, r0, ALU.is_ge)
                        e.reg_alu(a, a, bb, ALU.mult)
                        e.reg_alu(acc, acc, a, ALU.mult)
                e.reg_alu(cond, acc, 1, ALU.is_lt)
    ctx.op("gpsimd", poll)


def _xc_load_fm(ctx, buf, priv, ncp, sld):
    ng = ncp * 32 // 4
    for i in range(4):
        ctx.op("sync", lambda e, i=i: e.dma_start(
            out=buf[i * ng:(i + 1) * ng, 0:16384].rearrange(
                "s (q t) -> s q t", q=32),
            in_=priv.rearrange("(g q) (i t) -> g i q t", q=32, i=4)
            [:, i, :, :]), inc=(sld, 16))


def _xc_scatter(ctx, xs, buf, sh_view, col, n, ssc, half=0):
    ctx.op("gpsimd", lambda e: e.indirect_dma_start(
        out=sh_view,
        out_offset=IndirectOffsetOnAxis(xs["ixs"][0:n, col:col + 1], 0),
        in_=buf[0:n, half * 16384:half * 16384 + 16384],
        in_offset=None), inc=(ssc, 16))


def _v32(sh):
    return sh.rearrange("(k g q) t -> (k g) (q t)", g=32, q=32)


def emit_xchg1(nc, ctx, en, xs, P, SH):
    bv = en(nc.sbuf_tensor("bv", [128, 16384], BF16))
    bk = en(nc.sbuf_tensor("bk", [128, 16384], BF16))
    bfv = en(nc.sbuf_tensor("bfv", [128, 16384], BF16))
    bg = en(nc.sbuf_tensor("bg", [128, 16384], BF16))
    sld = ctx.sem("xld")
    ssc = ctx.sem("xsc")
    sga = ctx.sem("xga")
    sst = ctx.sem("xst")
    ctx.wait("sync", xs["sldi"])
    ctx.wait("gpsimd", xs["sldi"])
    _xc_load_fm(ctx, bv, P["v"], 4, sld)
    _xc_load_fm(ctx, bk, P["fk"], 4, sld)
    ctx.op("sync", lambda e: e.dma_start(
        out=bfv[:], in_=P["fv"].rearrange("(s q) d -> s (q d)", q=16)),
        inc=(sld, 16))
    ctx.wait("gpsimd", sld)
    _xc_scatter(ctx, xs, bv, _v32(SH["sh_v"]), 0, 128, ssc)
    _xc_scatter(ctx, xs, bk, _v32(SH["sh_fk"]), 0, 128, ssc)
    _xc_scatter(ctx, xs, bfv,
                SH["sh_fv"].rearrange("(s q) d -> s (q d)", q=16),
                0, 128, ssc)
    ctx.wait("gpsimd", ssc)
    _xc_stamp_poll(ctx, xs, 0, SH["st0"])
    ctx.op("gpsimd", lambda e: e.indirect_dma_start(
        out=bg[:], out_offset=None, in_=_v32(SH["sh_v"]),
        in_offset=IndirectOffsetOnAxis(xs["ixs"][:, 1:2], 0)),
        inc=(sga, 16))
    ctx.wait("sync", sga)
    for k in range(8):
        ctx.op("sync", lambda e, k=k: e.dma_start(
            out=P["v_ch"].rearrange("(g q) (k t) -> k g q t",
                                    q=32, k=8)[k],
            in_=bg[k * 16:(k + 1) * 16, :].rearrange(
                "s (q t) -> s q t", q=32)), inc=(sst, 16))
    ctx.wait("sync", sst)


def emit_xchg2(nc, ctx, en, xs, P, SH):
    bsy = en(nc.sbuf_tensor("bsy", [128, 16384], BF16))
    bg = en(nc.sbuf_tensor("bg", [128, 16384], BF16))
    sld = ctx.sem("xld")
    ssc = ctx.sem("xsc")
    sga = ctx.sem("xga")
    sst = ctx.sem("xst")
    for k in range(8):
        ctx.op("sync", lambda e, k=k: e.dma_start(
            out=bsy[k * 16:(k + 1) * 16, :].rearrange(
                "s (q t) -> s q t", q=32),
            in_=P["sy"].rearrange("(g q) (k t) -> g k q t", q=32, k=8)
            [:, k, :, :]), inc=(sld, 16))
    ctx.wait("gpsimd", sld)
    _xc_scatter(ctx, xs, bsy, _v32(SH["sh_sy"]), 1, 128, ssc)
    ctx.wait("gpsimd", ssc)
    _xc_stamp_poll(ctx, xs, 1, SH["st1"])
    ctx.op("gpsimd", lambda e: e.indirect_dma_start(
        out=bg[:], out_offset=None, in_=_v32(SH["sh_sy"]),
        in_offset=IndirectOffsetOnAxis(xs["ixs"][:, 0:1], 0)),
        inc=(sga, 16))
    ctx.wait("sync", sga)
    for i in range(4):
        ctx.op("sync", lambda e, i=i: e.dma_start(
            out=P["syr"].rearrange("(g q) (i t) -> g i q t",
                                   q=32, i=4)[:, i, :, :],
            in_=bg[i * 32:(i + 1) * 32, :].rearrange(
                "s (q t) -> s q t", q=32)), inc=(sst, 16))
    ctx.wait("sync", sst)


def emit_xchg3(nc, ctx, en, xs, P, SH):
    bK = en(nc.sbuf_tensor("bK", [128, 2 * 16384], BF16))
    bV = en(nc.sbuf_tensor("bV", [128, 2 * 16384], BF16))
    sld = ctx.sem("xld")
    ssc = ctx.sem("xsc")
    for i in range(4):
        ctx.op("sync", lambda e, i=i: e.dma_start(
            out=bK[(i % 2) * 64:(i % 2) * 64 + 64,
                   (i // 2) * 16384:(i // 2) * 16384 + 16384].rearrange(
                "s (q t) -> s q t", q=32),
            in_=P["Kr"].rearrange("(g q) (i t) -> g i q t", q=32, i=4)
            [:, i, :, :]), inc=(sld, 16))
    for hf in range(2):
        ctx.op("sync", lambda e, hf=hf: e.dma_start(
            out=bV[:, hf * 16384:hf * 16384 + 16384],
            in_=P["Vr"].rearrange("(s q) d -> s (q d)", q=8)
            [hf * 128:hf * 128 + 128, :]), inc=(sld, 16))
    ctx.wait("gpsimd", sld)
    shK = SH["sh_K"].rearrange("(k g q) t -> (k g) (q t)", g=64, q=32)
    shV = SH["sh_V"].rearrange("(s q) d -> s (q d)", q=8)
    _xc_scatter(ctx, xs, bK, shK, 4, 128, ssc, half=0)
    _xc_scatter(ctx, xs, bK, shK, 5, 128, ssc, half=1)
    _xc_scatter(ctx, xs, bV, shV, 2, 128, ssc, half=0)
    _xc_scatter(ctx, xs, bV, shV, 3, 128, ssc, half=1)
    ctx.wait("gpsimd", ssc)
    _xc_stamp_poll(ctx, xs, 2, SH["st2"])


def emit_xchg4(nc, ctx, en, xs, P, SH):
    bkg = en(nc.sbuf_tensor("bkg", [32, 16384], BF16))
    bvg = en(nc.sbuf_tensor("bvg", [64, 8192], BF16))
    sld = ctx.sem("xld")
    ssc = ctx.sem("xsc")
    for i in range(4):
        ctx.op("sync", lambda e, i=i: e.dma_start(
            out=bkg[i * 8:(i + 1) * 8, :].rearrange(
                "s (q t) -> s q t", q=32),
            in_=P["kg"].rearrange("(g q) (i t) -> g i q t", q=32, i=4)
            [:, i, :, :]), inc=(sld, 16))
    ctx.op("sync", lambda e: e.dma_start(
        out=bvg[:], in_=P["vg"].rearrange("(s q) d -> s (q d)", q=32)),
        inc=(sld, 16))
    ctx.wait("gpsimd", sld)
    shkg = SH["sh_kg"].rearrange("(k g q) t -> (k g) (q t)", g=8, q=32)
    shvg = SH["sh_vg"].rearrange("(s q) d -> s (q d)", q=32)
    ctx.op("gpsimd", lambda e: e.indirect_dma_start(
        out=shkg,
        out_offset=IndirectOffsetOnAxis(xs["ixs"][0:32, 6:7], 0),
        in_=bkg[:], in_offset=None), inc=(ssc, 16))
    ctx.op("gpsimd", lambda e: e.indirect_dma_start(
        out=shvg,
        out_offset=IndirectOffsetOnAxis(xs["ixs"][0:64, 7:8], 0),
        in_=bvg[:], in_offset=None), inc=(ssc, 16))
    ctx.wait("gpsimd", ssc)
    _xc_stamp_poll(ctx, xs, 3, SH["st3"])


def build_fused():
    nc = bass.Bass("TRN2", target_bir_lowering=False)
    ios = {}

    def din_for(ph):
        io = ios.setdefault(ph, {})

        def din(name, shape, dtype=F32):
            io[name] = nc.dram_tensor(ph + "_" + name, shape, dtype,
                                      kind="ExternalInput").ap()
        return din

    def route_for(ph, table):
        io = ios.setdefault(ph, {})

        def r(name, shape, dtype=F32):
            io[name] = table[name]
        return r

    def P_(name, shape, dtype):
        return nc.dram_tensor("p_" + name, shape, dtype,
                              kind="Internal").ap()

    def SH_(name, shape, dtype):
        return nc.dram_tensor(name, shape, dtype, kind="Internal",
                              addr_space="Shared").ap()

    P = {
        "xn0": P_("xn0", [D, NT], F32), "u": P_("u", [D, NT], BF16),
        "v": P_("v", [D, NT], BF16), "fq": P_("fq", [D, NT], BF16),
        "fk": P_("fk", [D, NT], BF16), "fv": P_("fv", [NT, D], BF16),
        "v_ch": P_("v_ch", [512, T], BF16), "sy": P_("sy", [512, T], BF16),
        "syr": P_("syr", [D, NT], BF16), "g": P_("g", [D, NT], BF16),
        "s0": P_("s0", [D, NT], F32), "x1": P_("x1", [D, NT], F32),
        "xn1": P_("xn1", [D, NT], F32),
        "xn1_bf": P_("xn1_bf", [D, NT], BF16),
        "Q": P_("Q", [2 * D, NT], BF16), "Kr": P_("Kr", [2 * D, NT], BF16),
        "Vr": P_("Vr", [NT, 2 * D], BF16), "s1": P_("s1", [D, NT], BF16),
        "qg": P_("qg", [D, NT], BF16), "kg": P_("kg", [256, NT], BF16),
        "vg": P_("vg", [NT, 256], BF16), "xp": P_("xp", [D, NT], F32),
        "h2b": P_("h2b", [D, NT], BF16),
        "logits": P_("logits", [8, NT], F32), "y": P_("y", [D, NT], BF16),
    }
    SH = {
        "sh_v": SH_("sh_v", [8 * D, 512], BF16),
        "sh_fk": SH_("sh_fk", [8 * D, 512], BF16),
        "sh_fv": SH_("sh_fv", [T, D], BF16),
        "sh_sy": SH_("sh_sy", [8 * D, 512], BF16),
        "sh_K": SH_("sh_K", [16 * D, 512], BF16),
        "sh_V": SH_("sh_V", [T, 2 * D], BF16),
        "sh_kg": SH_("sh_kg", [8 * 256, 512], BF16),
        "sh_vg": SH_("sh_vg", [T, 256], BF16),
    }
    for i in range(4):
        SH[f"st{i}"] = SH_(f"st{i}", [32, 16], I32)
    outp = nc.dram_tensor("outp", [D, NT], F32, kind="ExternalOutput").ap()

    xin = {}
    for nm, shp in (("x_ix", [128, 8]),
                    ("x_idx_st", [16, 1]), ("x_nonce", [16, 64])):
        xin[nm] = nc.dram_tensor(nm, shp, I32, kind="ExternalInput").ap()

    decl_A(din_for("A"), None, route_for("A", P))
    decl_B(din_for("B"), route_for("B", P), route_for("B", P))
    decl_C(din_for("C"),
           route_for("C", {"fq": P["fq"], "kf": SH["sh_fk"],
                           "vf": SH["sh_fv"]}),
           route_for("C", P))
    decl_D1(din_for("D1"), route_for("D1", P), route_for("D1", P))
    decl_D2(din_for("D2"), route_for("D2", P), route_for("D2", P))
    decl_E(din_for("E"), route_for("E", P), route_for("E", P))
    decl_F(din_for("F"),
           route_for("F", {"Q": P["Q"], "Kf": SH["sh_K"],
                           "Vf": SH["sh_V"]}),
           route_for("F", P))
    decl_G1(din_for("G1"), route_for("G1", P), route_for("G1", P))
    decl_G2(din_for("G2"),
            route_for("G2", {"qg": P["qg"], "kgf": SH["sh_kg"],
                             "vgf": SH["sh_vg"], "xn1": P["xn1"]}),
            route_for("G2", P))
    decl_H1(din_for("H1"), route_for("H1", P), route_for("H1", P))
    decl_H2(din_for("H2"), route_for("H2", P),
            route_for("H2", {"outp": outp}))

    with contextlib.ExitStack() as stack:
        ctx = Ctx(nc, stack)
        enp = stack.enter_context
        xs = {
            "ixs": enp(nc.sbuf_tensor("xixs", [128, 8], I32)),
            "ist": enp(nc.sbuf_tensor("xists", [16, 1], I32)),
            "non": enp(nc.sbuf_tensor("xnons", [16, 64], I32)),
            "flags": enp(nc.sbuf_tensor("xflags", [2, 16], I32)),
            "polls": [enp(nc.semaphore(f"xpoll{i}")) for i in range(4)],
        }
        ctx.new_phase("X0_")
        sldi = ctx.sem("ldi")
        for nm, dst in (("x_ix", "ixs"),
                        ("x_idx_st", "ist"), ("x_nonce", "non")):
            ctx.op("sync", lambda e, nm=nm, dst=dst: e.dma_start(
                out=xs[dst][:], in_=xin[nm]), inc=(sldi, 16))
        ctx.wait("gpsimd", sldi)
        ctx.wait("sync", sldi)
        xs["sldi"] = sldi

        seq = [
            ("A_", lambda n, c, e: emit_A(n, c, ios["A"], e)),
            ("X1_", lambda n, c, e: emit_xchg1(n, c, e, xs, P, SH)),
            ("B_", lambda n, c, e: emit_B(n, c, ios["B"], e)),
            ("C_", lambda n, c, e: emit_C(n, c, ios["C"], e)),
            ("X2_", lambda n, c, e: emit_xchg2(n, c, e, xs, P, SH)),
            ("D1_", lambda n, c, e: emit_D1(n, c, ios["D1"], e)),
            ("D2_", lambda n, c, e: emit_D2(n, c, ios["D2"], e)),
            ("E_", lambda n, c, e: emit_E(n, c, ios["E"], e)),
            ("X3_", lambda n, c, e: emit_xchg3(n, c, e, xs, P, SH)),
            ("F_", lambda n, c, e: emit_F(n, c, ios["F"], e)),
            ("G1_", lambda n, c, e: emit_G1(n, c, ios["G1"], e)),
            ("X4_", lambda n, c, e: emit_xchg4(n, c, e, xs, P, SH)),
            ("G2_", lambda n, c, e: emit_G2(n, c, ios["G2"], e)),
            ("H1_", lambda n, c, e: emit_H1(n, c, ios["H1"], e)),
            ("H2_", lambda n, c, e: emit_H2(n, c, ios["H2"], e)),
        ]
        for name, fn in seq:
            ctx.new_phase(name)
            with contextlib.ExitStack() as pst:
                fn(NCProxy(nc, name), ctx, pst.enter_context)
            ctx.barrier()
        ctx.finalize()
    return nc


def stageW_B(core, W):
    r = core % 2
    sl = slice(r * 512, (r + 1) * 512)
    return {"cw0": stage_bias(W["L0_attn_conv_w"][sl, 0, 0]),
            "cw1": stage_bias(W["L0_attn_conv_w"][sl, 0, 1]),
            "cw2": stage_bias(W["L0_attn_conv_w"][sl, 0, 2]),
            "cb": stage_bias(W["L0_attn_conv_b"][sl]),
            "la": stage_bias(W["L0_attn_logit_a"][sl]),
            "sb_": stage_bias(W["L0_attn_ssm_b"][sl]),
            "sc_": stage_bias(W["L0_attn_ssm_c"][sl]),
            "sd_": stage_bias(W["L0_attn_ssm_d"][sl])}


def stage_fused(core, W, nonce_val):
    b, r = core // 2, core % 2
    rows = own_rows(r)
    cst = const_arrays()
    m = {}

    def add(ph, d):
        for k, v in d.items():
            m[ph + "_" + k] = np.ascontiguousarray(v)

    add("A", stage_inputs_A(core, W))
    add("B", stageW_B(core, W))
    add("C", dict(c_ones_bf=cst["c_ones_bf"], c_ones_row=cst["c_ones_row"],
                  c_cfix=np.full((128, 1), -8.0, np.float32)))
    add("D1", dict(wout=W["L0_attn_out_proj_w"],
                   b_out=stage_bias(W["L0_attn_out_proj_b"]),
                   lnw=stage_bias(W["L0_attn_ln_w"]),
                   lnb=stage_bias(W["L0_attn_ln_b"]),
                   c_ones_bf=cst["c_ones_bf"],
                   c_ones_row=cst["c_ones_row"]))
    add("D2", dict(x_fm=fmT(W["L0_attn_x"][b][rows]),
                   c_ones_bf=cst["c_ones_bf"],
                   c_ones_row=cst["c_ones_row"]))
    add("E", dict(wq=W["L1_attn_Wq"], wk=W["L1_attn_Wk"],
                  wv=W["L1_attn_Wv"]))
    lam4 = np.concatenate([W["L1_attn_lambda_q1"], W["L1_attn_lambda_k1"],
                           W["L1_attn_lambda_q2"], W["L1_attn_lambda_k2"]],
                          axis=1)
    add("F", dict(lam4=lam4,
                  rsc=W["L1_attn_rms_scale"].reshape(128, 1),
                  wo=W["L1_attn_Wo"], c_maskt=mask_table(r),
                  c_id16=np.eye(16, dtype=np.float32),
                  c_e64c=e64c_mat(),
                  c_cfix=np.full((128, 2), -4.0, np.float32),
                  c_ones_bf=cst["c_ones_bf"],
                  c_ones_row=cst["c_ones_row"]))
    dG1 = stage_inputs_G1(core, W)
    del dG1["xn1"]
    add("G1", dG1)
    add("G2", dict(wg=W["L1_ffn_Wg"], bg=W["L1_ffn_bg"].reshape(64, 1),
                   wo=W["L1_ffn_Wo"], bo=stage_bias(W["L1_ffn_bo"]),
                   ln2w=stage_bias(W["L1_ffn_ln2_w"]),
                   wr=W["L1_ffn_Wr"], br=W["L1_ffn_br"].reshape(8, 1),
                   c_maskt=mask_table(r),
                   c_cfix=np.full((128, 1), -4.0, np.float32),
                   c_ones_bf=cst["c_ones_bf"],
                   c_ones_row=cst["c_ones_row"]))
    b1stage = np.zeros((128, 8 * 32), np.float32)
    for e in range(8):
        b1stage[:, e * 32:(e + 1) * 32] = stage_bias(W["L1_ffn_b1"][e])
    add("H1", dict(b1s=b1stage, w1=W["L1_ffn_W1"], w2=W["L1_ffn_W2"],
                   b2r=W["L1_ffn_b2"].reshape(1, 8192),
                   c_selg=selg_mat(), c_id8=np.eye(8, dtype=np.float32),
                   c_idL=np.eye(128, dtype=np.float32)))
    m.update(exchange_indices(core))
    non = np.zeros((16, 64), np.int32)
    for s in range(4):
        non[:, s * 16:(s + 1) * 16] = nonce_val + s
    m["x_nonce"] = non
    return m



_BUILD_CACHE = {}


def _get(name, fn):
    if name not in _BUILD_CACHE:
        _BUILD_CACHE[name] = fn()
    return _BUILD_CACHE[name]


def kernel(**W):
    cores = list(range(8))
    nc = _get("FUSED", build_fused)
    nonce_val = int(np.random.randint(1, 2 ** 31 - 100))
    in_maps = [stage_fused(c, W, nonce_val) for c in cores]
    res = run_bass_kernel_spmd(nc, in_maps, core_ids=cores).results
    rows_of = {r: own_rows(r) for r in (0, 1)}
    out = np.zeros((B, T, D), np.float32)
    for c in cores:
        b, r = c // 2, c % 2
        out[b][rows_of[r]] = np.asarray(res[c]["outp"]).T
    return out


def kernel_multi(**W):
    BFc = BF
    cores = list(range(8))
    cst = const_arrays()

    def run(name, build_fn, in_maps):
        nc = _get(name, build_fn)
        return run_bass_kernel_spmd(nc, in_maps, core_ids=cores).results

    ra = run("A", build_launch_A, [stage_inputs_A(c, W) for c in cores])

    rows_of = {r: own_rows(r) for r in (0, 1)}

    def asm(key, tm=False, width=1024):
        full = {}
        for b in range(4):
            out = np.zeros((T, width), dtype=BFc)
            for r in (0, 1):
                cr = ra[2 * b + r][key]
                out[rows_of[r]] = np.asarray(cr).T if not tm else cr
            full[b] = out
        return full

    v_full = asm("v")
    k_full = asm("fk")
    vq_full = asm("fv", tm=True)

    inB = []
    for c in cores:
        b, r = c // 2, c % 2
        v_fm = np.ascontiguousarray(v_full[b].T)
        inB.append(stage_inputs_B(c, W, v_fm))
    rb = run("B", build_launch_B, inB)

    inC = []
    for c in cores:
        b = c // 2
        inC.append(stage_inputs_C(
            c, W, np.ascontiguousarray(ra[c]["fq"]),
            np.ascontiguousarray(v_full[b].T * 0 + k_full[b].T)
            if False else np.ascontiguousarray(k_full[b].T),
            np.ascontiguousarray(vq_full[b])))
    rc = run("C", build_launch_C, inC)

    inD1 = []
    for c in cores:
        b, r = c // 2, c % 2
        rws = rows_of[r]
        syr = np.zeros((1024, NT), dtype=BFc)
        for rr in (0, 1):
            ch0 = rr * 512
            syc = np.asarray(rb[2 * b + rr]["sy"])
            syr[ch0:ch0 + 512, :] = syc[:, rws]
        inD1.append(dict(
            xn0=np.asarray(ra[c]["xn0"]),
            u=np.asarray(ra[c]["u"]), syr=np.ascontiguousarray(syr),
            wout=np.ascontiguousarray(W["L0_attn_out_proj_w"]),
            b_out=stage_bias(W["L0_attn_out_proj_b"]),
            lnw=stage_bias(W["L0_attn_ln_w"]),
            lnb=stage_bias(W["L0_attn_ln_b"]),
            c_ones_bf=cst["c_ones_bf"], c_ones_row=cst["c_ones_row"]))
    rd1 = run("D1", build_launch_D1, inD1)

    inD2 = []
    for c in cores:
        b, r = c // 2, c % 2
        inD2.append(dict(
            x_fm=fmT(W["L0_attn_x"][b][rows_of[r]]),
            s0=np.asarray(rd1[c]["s0"]), g=np.asarray(rc[c]["g"]),
            c_ones_bf=cst["c_ones_bf"], c_ones_row=cst["c_ones_row"]))
    rd2 = run("D2", build_launch_D2, inD2)

    inE = [dict(xn1_bf=np.asarray(rd2[c]["xn1_bf"]),
                wq=np.ascontiguousarray(W["L1_attn_Wq"]),
                wk=np.ascontiguousarray(W["L1_attn_Wk"]),
                wv=np.ascontiguousarray(W["L1_attn_Wv"])) for c in cores]
    re_ = run("E", build_launch_E, inE)

    Kfull, Vfull = {}, {}
    for b in range(4):
        kf = np.zeros((T, 2048), dtype=BFc)
        vf = np.zeros((T, 2048), dtype=BFc)
        for r in (0, 1):
            kf[rows_of[r]] = np.asarray(re_[2 * b + r]["Kr"]).T
            vf[rows_of[r]] = np.asarray(re_[2 * b + r]["Vr"])
        Kfull[b], Vfull[b] = kf, vf

    inF = [stage_inputs_F(c, W, np.asarray(re_[c]["Q"]),
                          np.ascontiguousarray(Kfull[c // 2].T),
                          np.ascontiguousarray(Vfull[c // 2]))
           for c in cores]
    rf = run("F", build_launch_F, inF)

    inG1 = []
    for c in cores:
        d = stage_inputs_G1(c, W)
        d["xn1"] = np.asarray(rd2[c]["xn1"])
        inG1.append(d)
    rg1 = run("G1", build_launch_G1, inG1)

    kgf, vgf = {}, {}
    for b in range(4):
        kg = np.zeros((T, 256), dtype=BFc)
        vg = np.zeros((T, 256), dtype=BFc)
        for r in (0, 1):
            kg[rows_of[r]] = np.asarray(rg1[2 * b + r]["kg"]).T
            vg[rows_of[r]] = np.asarray(rg1[2 * b + r]["vg"])
        kgf[b], vgf[b] = kg, vg

    inG2 = []
    for c in cores:
        b, r = c // 2, c % 2
        inG2.append(dict(
            qg=np.asarray(rg1[c]["qg"]),
            kgf=np.ascontiguousarray(kgf[b].T),
            vgf=np.ascontiguousarray(vgf[b]),
            wg=np.ascontiguousarray(W["L1_ffn_Wg"]),
            bg=np.ascontiguousarray(W["L1_ffn_bg"].reshape(64, 1)),
            wo=np.ascontiguousarray(W["L1_ffn_Wo"]),
            bo=stage_bias(W["L1_ffn_bo"]),
            xn1=np.asarray(rd2[c]["xn1"]),
            ln2w=stage_bias(W["L1_ffn_ln2_w"]),
            wr=np.ascontiguousarray(W["L1_ffn_Wr"]),
            br=np.ascontiguousarray(W["L1_ffn_br"].reshape(8, 1)),
            c_maskt=mask_table(r),
            c_cfix=np.full((128, 1), -4.0, np.float32),
            c_ones_bf=cst["c_ones_bf"], c_ones_row=cst["c_ones_row"]))
    rg2 = run("G2", build_launch_G2, inG2)

    b1stage = np.zeros((128, 8 * 32), np.float32)
    for e in range(8):
        b1stage[:, e * 32:(e + 1) * 32] = stage_bias(W["L1_ffn_b1"][e])
    inH1 = []
    for c in cores:
        inH1.append(dict(
            h2b=np.asarray(rg2[c]["h2b"]),
            logits=np.asarray(rg2[c]["logits"]),
            b1s=b1stage,
            w1=np.ascontiguousarray(W["L1_ffn_W1"]),
            w2=np.ascontiguousarray(W["L1_ffn_W2"]),
            b2r=np.ascontiguousarray(W["L1_ffn_b2"].reshape(1, 8192)),
            c_selg=selg_mat(),
            c_id8=np.eye(8, dtype=np.float32),
            c_idL=np.eye(128, dtype=np.float32)))
    rh1 = run("H1", build_launch_H1, inH1)

    inH2 = []
    for c in cores:
        b, r = c // 2, c % 2
        inH2.append(dict(
            x1=np.asarray(rd2[c]["x1"]), xp=np.asarray(rg2[c]["xp"]),
            y=np.asarray(rh1[c]["y"]), s1=np.asarray(rf[c]["s1"])))
    rh2 = run("H2", build_launch_H2, inH2)

    out = np.zeros((B, T, D), np.float32)
    for c in cores:
        b, r = c // 2, c % 2
        out[b][rows_of[r]] = np.asarray(rh2[c]["outp"]).T
    return out

